# version 08
# baseline (speedup 1.0000x reference)
import contextlib

import numpy as np

import concourse.bass as bass
import concourse.mybir as mybir
from concourse.bass_utils import run_bass_kernel_spmd

F32 = mybir.dt.float32
BF16 = mybir.dt.bfloat16
OP = mybir.AluOpType
AF = mybir.ActivationFunctionType
AX = mybir.AxisListType.X

B, S, D = 2, 8192, 1024
E, H = 15, 1024
DC = D // 128
HCN = H // 128
TOK = 2048
TS = 256
NTILE = TOK // TS
MSUB = 512
NMT = TOK // MSUB
NSUB = 512
NJ = TOK // NSUB
EPS = 1e-5
NEXP = 16

DEBUG = False


class Prog:

    ENGINES = ("sync", "tensor", "vector", "scalar", "gpsimd")

    def __init__(self, nc):
        self.nc = nc
        self.ops = {e: [] for e in self.ENGINES}
        self.sems = {}
        self.pending_gate = None
        self.gated = set()

    def set_gate(self, gate):
        self.pending_gate = gate
        self.gated = set()

    def add_sem(self, ctx, name):
        h = ctx.enter_context(self.nc.semaphore(name))
        self.sems[name] = [h, 0]

    def emit(self, eng, fn, waits=(), inc=None, tag=""):
        wl = [(s, v) for (s, v) in waits]
        if self.pending_gate is not None and eng not in self.gated:
            wl.append(self.pending_gate)
            self.gated.add(eng)
            if len(self.gated) == len(self.ENGINES):
                self.pending_gate = None
        self.ops[eng].append((fn, wl, inc, tag))
        if inc is not None:
            self.sems[inc[0]][1] += inc[1]
            return self.sems[inc[0]][1]
        return None

    def val(self, name):
        return self.sems[name][1]

    @staticmethod
    def _rv(v):
        return v() if callable(v) else v

    def check_deadlock(self):
        counts = {name: 0 for name in self.sems}
        ptrs = {e: 0 for e in self.ENGINES}
        progress = True
        while progress:
            progress = False
            for e in self.ENGINES:
                while ptrs[e] < len(self.ops[e]):
                    fn, waits, inc, tag = self.ops[e][ptrs[e]]
                    if all(counts[s] >= self._rv(v) for (s, v) in waits):
                        if inc is not None:
                            counts[inc[0]] += inc[1]
                        ptrs[e] += 1
                        progress = True
                    else:
                        break
        stuck = {e: ptrs[e] for e in self.ENGINES if ptrs[e] < len(self.ops[e])}
        if stuck:
            msgs = []
            for e, i in stuck.items():
                fn, waits, inc, tag = self.ops[e][i]
                blocked = [(s, self._rv(v), counts[s]) for (s, v) in waits
                           if counts[s] < self._rv(v)]
                msgs.append(f"{e}[{i}] tag={tag} blocked on {blocked}")
            raise RuntimeError("DEADLOCK in schedule:\n" + "\n".join(msgs))

    def build(self, block):
        def mk(e):
            def body(engobj):
                for fn, waits, inc, tag in self.ops[e]:
                    for (s, v) in waits:
                        engobj.wait_ge(self.sems[s][0], Prog._rv(v))
                    instr = fn(engobj)
                    if inc is not None:
                        assert instr is not None, f"inc without instruction: {e} {tag}"
                        instr.then_inc(self.sems[inc[0]][0], inc[1])
            return body

        block.sync(mk("sync"))
        block.tensor(mk("tensor"))
        block.vector(mk("vector"))
        block.scalar(mk("scalar"))
        block.gpsimd(mk("gpsimd"))


def build_nc():
    nc = bass.Bass("TRN2", target_bir_lowering=False, debug=False)

    xc = nc.dram_tensor("xc", [D, TOK], F32, kind="ExternalInput").ap()
    cmaskd = nc.dram_tensor("cmask", [128, 8], F32, kind="ExternalInput").ap()
    cm64d = nc.dram_tensor("cm64", [128, 64], F32, kind="ExternalInput").ap()
    cmask1d = nc.dram_tensor("cmask1m", [128, 8], F32, kind="ExternalInput").ap()
    wifd = nc.dram_tensor("wif", [128, 24], F32, kind="ExternalInput").ap()
    sbvd = nc.dram_tensor("sbv", [128, 16], F32, kind="ExternalInput").ap()
    eb1d = nc.dram_tensor("eb1p", [128, 120], F32, kind="ExternalInput").ap()
    eb2d = nc.dram_tensor("eb2p", [15, 1024], F32, kind="ExternalInput").ap()
    rwd = nc.dram_tensor("rwp", [128, 120], F32, kind="ExternalInput").ap()
    rbd = nc.dram_tensor("rbp", [15, 1], F32, kind="ExternalInput").ap()
    sW1d = nc.dram_tensor("sW1", [1024, 1024], F32, kind="ExternalInput").ap()
    sW2d = nc.dram_tensor("sW2", [1024, 1024], F32, kind="ExternalInput").ap()
    eW1d = nc.dram_tensor("eW1", [15, 1024, 1024], F32, kind="ExternalInput").ap()
    eW2d = nc.dram_tensor("eW2", [15, 1024, 1024], F32, kind="ExternalInput").ap()

    out_fm = nc.dram_tensor("out_fm", [D, TOK], F32, kind="ExternalOutput").ap()

    x1sp = nc.dram_tensor("x1sp", [D, TOK], F32, kind="Internal").ap()
    hsp = nc.dram_tensor("hsp", [D, TOK], F32, kind="Internal").ap()
    asp = nc.dram_tensor("asp", [D, TOK], BF16, kind="Internal").ap()
    agin = nc.dram_tensor("agin", [2, 1024], F32, kind="Internal").ap()
    agout = nc.dram_tensor("agout", [16, 1024], F32, kind="Internal").ap()

    identd = nc.inline_tensor(np.eye(128, dtype=np.float32), "identc").ap()
    onesk1d = nc.inline_tensor(np.ones((1, 128), dtype=np.float32), "onesk1c").ap()
    onesm1d = nc.inline_tensor(np.ones((128, 1), dtype=np.float32), "onesm1c").ap()
    oneh_np = np.zeros((15, 15 * 128), dtype=np.float32)
    for e in range(15):
        oneh_np[e, e * 128:(e + 1) * 128] = 1.0
    import ml_dtypes
    onehd = nc.inline_tensor(oneh_np.astype(ml_dtypes.bfloat16), "onehc").ap()

    dbg = {}
    if DEBUG:
        for name, shape in [
            ("dbg_ln1", [D, TOK]), ("dbg_f", [D, TOK]), ("dbg_v", [D, TOK]),
            ("dbg_h", [D, TOK]), ("dbg_ln2", [D, TOK]), ("dbg_cw", [15, TOK]),
            ("dbg_acc", [D, TOK]), ("dbg_csum", [128, 128]), ("dbg_cvec", [128, 8]),
            ("dbg_car", [128, 16]),
        ]:
            dbg[name] = nc.dram_tensor(name, shape, F32, kind="ExternalOutput").ap()

    def cpview(ap):
        return ap.rearrange("(c p) n -> c p n", p=128)

    xcv, hspv, aspv, x1v, outv = map(cpview, (xc, hsp, asp, x1sp, out_fm))

    p = Prog(nc)

    with contextlib.ExitStack() as ctx:
        for s in ("sx", "ssp", "sw", "scc", "smm", "sdv", "sac", "sgp"):
            p.add_sem(ctx, s)

        sb = nc.alloc_sbuf_tensor
        identt = sb("identt", [128, 128], F32)
        onesk1t = sb("onesk1t", [1, 128], F32)
        onesm1t = sb("onesm1t", [128, 1], F32)
        wift = sb("wift", [128, 24], F32)
        sbvt = sb("sbvt", [128, 16], F32)
        eb1t = sb("eb1t", [128, 120], F32)
        rwt = sb("rwt", [128, 120], F32)
        rwbt = sb("rwbt", [128, 120], BF16)
        rbt = sb("rbt", [15, 1], F32)
        cmt = sb("cmt", [128, 8], F32)
        cm1t = sb("cm1t", [128, 8], F32)
        cm64t = sb("cm64t", [128, 64], F32)
        carA = sb("carA", [128, 8], F32)
        carH = sb("carH", [128, 8], F32)
        cvec = sb("cvec", [128, 8], F32)
        csumA = sb("csumA", [128, 64], F32)
        csumB = sb("csumB", [128, 64], F32)
        lgb = sb("lgb", [128, 352], F32)
        cwfb = sb("cwfb", [15, TOK], BF16)
        lgsb = sb("lgsb", [15, 512], F32)
        ln2b = sb("ln2b", [128, DC * TOK], BF16)

        pb = [ctx.enter_context(nc.psum_tensor(f"pb{i}", [128, 512], F32))
              for i in range(8)]

        def d(fn):
            return fn

        consts = [
            (identt, identd), (onesk1t, onesk1d), (onesm1t, onesm1d),
            (wift, wifd), (sbvt, sbvd), (eb1t, eb1d), (rwt, rwd), (rbt, rbd),
            (cmt, cmaskd), (cm1t, cmask1d), (cm64t, cm64d),
        ]
        for tt, dd_ in consts:
            p.emit("sync", (lambda t=tt, dd=dd_: lambda eng: eng.dma_start(out=t[:], in_=dd))(),
                   inc=("sx", 16), tag="const")
        consts_loaded = p.val("sx")

        rwcast = p.emit("vector", lambda eng: eng.tensor_copy(rwbt[:], rwt[:]),
                        waits=[("sx", consts_loaded)], inc=("sdv", 1), tag="rwcast")
        p.emit("vector", lambda eng: eng.memset(carA[:], 1.0), tag="carA0")
        p.emit("vector", lambda eng: eng.memset(carH[:], 0.0), inc=("sdv", 1), tag="carH0")
        carinit = p.val("sdv")

        with contextlib.ExitStack() as sctx:
            srow = sctx.enter_context(nc.sbuf_tensor("srow", [1, 2048], F32))
            xb = sctx.enter_context(nc.sbuf_tensor("xb", [128, 3 * 2048], F32))
            ln1 = sctx.enter_context(nc.sbuf_tensor("ln1", [128, 2 * 2048], F32))
            t1 = sctx.enter_context(nc.sbuf_tensor("t1", [128, 2 * 2048], F32))
            t2 = sctx.enter_context(nc.sbuf_tensor("t2", [128, 2 * 2048], F32))
            t3 = sctx.enter_context(nc.sbuf_tensor("t3", [128, 2 * 2048], F32))
            t4 = sctx.enter_context(nc.sbuf_tensor("t4", [128, 2048], F32))
            hb = sctx.enter_context(nc.sbuf_tensor("hb", [128, 2048], F32))
            ab = sctx.enter_context(nc.sbuf_tensor("ab", [128, 2 * 2048], F32))
            xsq = sctx.enter_context(nc.sbuf_tensor("xsq", [128, 2048], F32))
            p.emit("vector", lambda eng: eng.memset(pb[4][:], 0.0),
                   waits=[("sx", consts_loaded)], inc=("sdv", 1), tag="zt0")
            wtl_done = p.val("sdv")

            ev_lnapply = [0] * NTILE
            ev_spill_h = [0] * NTILE
            ev_spill_a = [0] * NTILE
            ev_scan = [0] * NTILE
            ev_xload = [0] * NTILE
            ev_sq_used = [0] * NTILE
            ev_rows_dve = [0] * NTILE

            ev_bcast = [0] * NTILE
            def pre_sq_stats(k):
                xsl = xb[:, (k % 3) * 2048:((k % 3) + 1) * 2048]
                xqsl = xsq[:, 0:2048]
                waits = [("sx", lambda k=k: ev_xload[k])]
                if k >= 1:
                    waits.append(("smm", ev_sq_used[k - 1]))
                p.emit("scalar", (lambda x_=xsl, q_=xqsl: lambda eng: eng.activation(
                    q_, x_, AF.Square, bias=0.0, scale=1.0))(),
                    waits=waits, inc=("sac", 1), tag=f"sq{k}")
                sq_done = p.val("sac")

                waits = [("sac", sq_done)]
                if k >= 1:
                    waits.append(("sdv", ev_rows_dve[k - 1]))
                for dc in range(DC):
                    def mks(dc=dc, x_=xsl):
                        def fn(eng):
                            return eng.matmul(pb[0][0:1, 0:TS], onesm1t[:],
                                              x_[:, dc * TS:(dc + 1) * TS],
                                              start=(dc == 0), stop=(dc == DC - 1))
                        return fn
                    p.emit("tensor", mks(), waits=waits if dc == 0 else [], tag=f"summ{k}_{dc}")
                for dc in range(DC):
                    def mks2(dc=dc, q_=xqsl):
                        def fn(eng):
                            return eng.matmul(pb[1][0:1, 0:TS], onesm1t[:],
                                              q_[:, dc * TS:(dc + 1) * TS],
                                              start=(dc == 0), stop=(dc == DC - 1))
                        return fn
                    inc = ("smm", 1) if dc == DC - 1 else None
                    p.emit("tensor", mks2(), inc=inc, tag=f"sumsq{k}_{dc}")
                ev_sq_used[k] = p.val("smm")

            def pre_rows_bcast(k):
                par = k % 2
                mb = pb[2] if par == 0 else pb[5]
                rb = pb[3] if par == 0 else pb[6]
                base = par * 1024
                mrow = srow[0:1, base:base + TS]
                arow = srow[0:1, base + 256:base + 256 + TS]
                trow = srow[0:1, base + 512:base + 512 + TS]
                rrow = srow[0:1, base + 768:base + 768 + TS]

                waits = [("smm", ev_sq_used[k])]
                if k >= 2:
                    waits.append(("smm", ev_bcast[k - 2]))
                p.emit("vector", (lambda mr=mrow: lambda eng: eng.tensor_single_scalar(
                    mr, pb[0][0:1, 0:TS], 1.0 / D, OP.mult))(),
                    waits=waits, tag=f"rowm{k}")
                p.emit("vector", (lambda ar=arow: lambda eng: eng.tensor_scalar(
                    ar, pb[1][0:1, 0:TS], 1.0 / D, EPS, OP.mult, OP.add))(), tag=f"rowa{k}")
                p.emit("vector", (lambda mr=mrow, tr=trow: lambda eng: eng.tensor_tensor(
                    tr, mr, mr, OP.mult))(), tag=f"rowt{k}")
                ev_rows_dve[k] = p.emit("vector", (lambda ar=arow, tr=trow: lambda eng: eng.tensor_tensor(
                    ar, ar, tr, OP.subtract))(), inc=("sdv", 1), tag=f"rowb{k}")
                p.emit("scalar", (lambda ar=arow, tr=trow: lambda eng: eng.activation(
                    tr, ar, AF.Ln, bias=0.0, scale=1.0))(),
                    waits=[("sdv", ev_rows_dve[k])], tag=f"rowln{k}")
                rowsr = p.emit("scalar", (lambda rr=rrow, tr=trow: lambda eng: eng.activation(
                    rr, tr, AF.Exp, bias=0.0, scale=-0.5))(), inc=("sac", 1), tag=f"rowr{k}")

                waits = [("sac", rowsr)]
                if k >= 2:
                    waits.append(("sdv", ev_lnapply[k - 2]))
                p.emit("tensor", (lambda mr=mrow, mb=mb: lambda eng: eng.matmul(
                    mb[:, 0:TS], onesk1t[:], mr, start=True, stop=True))(),
                    waits=waits, tag=f"bcm{k}")
                ev_bcast[k] = p.emit("tensor", (lambda rr=rrow, rb=rb: lambda eng: eng.matmul(
                    rb[:, 0:TS], onesk1t[:], rr, start=True, stop=True))(),
                    inc=("smm", 1), tag=f"bcr{k}")

            def pre_load(k):
                xsl = xb[:, (k % 3) * 2048:((k % 3) + 1) * 2048]
                waits = []
                if k >= 3:
                    waits.append(("ssp", ev_spill_h[k - 3]))
                src = xcv[:, :, k * TS:(k + 1) * TS].transpose([1, 0, 2])
                dst = xsl.rearrange("p (c n) -> p c n", c=DC)
                ev_xload[k] = p.emit(
                    "sync", (lambda s_=src, d_=dst: lambda eng: eng.dma_start(out=d_, in_=s_))(),
                    waits=waits, inc=("sx", 16), tag=f"xload{k}")

            def mainpass(k):
                par = k % 2
                xsl = xb[:, (k % 3) * 2048:((k % 3) + 1) * 2048]
                l1sl = ln1[:, par * 2048:(par + 1) * 2048]
                t1sl = t1[:, par * 2048:(par + 1) * 2048]
                fsl = t2[:, par * 2048:(par + 1) * 2048]
                t3sl = t3[:, par * 2048:(par + 1) * 2048]
                vsl = t4[:, 0:2048]
                hsl = hb[:, 0:2048]
                asl = ab[:, par * 2048:(par + 1) * 2048]
                mb = pb[2] if par == 0 else pb[5]
                rb = pb[3] if par == 0 else pb[6]
                H = 1024

                def hsub(ap_, h):
                    return ap_[:, h * H:(h + 1) * H]

                for dc in range(DC):
                    def mka(dc=dc, x_=xsl, l_=l1sl, mb=mb):
                        def fn(eng):
                            return eng.tensor_tensor(l_[:, dc * TS:(dc + 1) * TS],
                                                     x_[:, dc * TS:(dc + 1) * TS],
                                                     mb[:, 0:TS], OP.subtract)
                        return fn
                    p.emit("vector", mka(), waits=[("smm", ev_bcast[k])] if dc == 0 else [],
                           tag=f"lna{k}_{dc}")
                lnh = [0, 0]
                for dc in range(DC):
                    def mkb(dc=dc, l_=l1sl, rb=rb):
                        def fn(eng):
                            sl = l_[:, dc * TS:(dc + 1) * TS]
                            return eng.tensor_tensor(sl, sl, rb[:, 0:TS], OP.mult)
                        return fn
                    inc = ("sdv", 1) if dc in (3, 7) else None
                    p.emit("vector", mkb(), inc=inc, tag=f"lnb{k}_{dc}")
                    if dc == 3:
                        lnh[0] = p.val("sdv")
                ev_lnapply[k] = lnh[1] = p.val("sdv")

                if DEBUG:
                    dmaf = (lambda k=k, l_=l1sl: lambda eng: eng.dma_start(
                        out=cpview(dbg["dbg_ln1"])[:, :, k * TS:(k + 1) * TS].transpose([1, 0, 2]),
                        in_=l_.rearrange("p (c n) -> p c n", c=DC)))()
                    gv = p.emit("sync", dmaf, waits=[("sdv", ev_lnapply[k])], inc=("ssp", 16),
                                tag=f"dbgln1_{k}")
                    p.set_gate(("ssp", gv))

                spd = [0, 0]
                for h in range(2):
                    for dc in range(4 * h, 4 * h + 4):
                        p.emit("scalar", (lambda o_=t1sl, l_=l1sl, dc=dc: lambda eng: eng.activation(
                            o_[:, dc * TS:(dc + 1) * TS], l_[:, dc * TS:(dc + 1) * TS],
                            AF.Exp, bias=0.0, scale=wift[:, dc:dc + 1]))(),
                            waits=[("sdv", max(lnh[h], wtl_done))] if dc == 4 * h else [],
                            tag=f"e1{k}_{dc}")
                    for dc in range(4 * h, 4 * h + 4):
                        p.emit("scalar", (lambda o_=t3sl, l_=l1sl, dc=dc: lambda eng: eng.activation(
                            o_[:, dc * TS:(dc + 1) * TS], l_[:, dc * TS:(dc + 1) * TS],
                            AF.Exp, bias=0.0, scale=wift[:, 8 + dc:9 + dc]))(), tag=f"e2{k}_{dc}")
                    p.emit("scalar", (lambda o_=t1sl, h=h: lambda eng: eng.activation(
                        hsub(o_, h), hsub(o_, h), AF.Ln, bias=1.0, scale=1.0))(), tag=f"l1{k}_{h}")
                    spd[h] = p.emit("scalar", (lambda o_=t3sl, h=h: lambda eng: eng.activation(
                        hsub(o_, h), hsub(o_, h), AF.Ln, bias=1.0, scale=1.0))(),
                        inc=("sac", 1), tag=f"l2{k}_{h}")

                gdiff = [0, 0]
                for h in range(2):
                    gdiff[h] = p.emit("gpsimd", (lambda a_=t1sl, b_=t3sl, h=h: lambda eng:
                                      eng.tensor_tensor(hsub(a_, h), hsub(a_, h), hsub(b_, h),
                                                        OP.subtract))(),
                                      waits=[("sac", spd[h])], inc=("sgp", 1), tag=f"diff{k}_{h}")

                sgd = [0, 0]
                for h in range(2):
                    p.emit("scalar", (lambda f_=fsl, a_=t1sl, h=h: lambda eng: eng.activation(
                        hsub(f_, h), hsub(a_, h), AF.Sigmoid, bias=0.0, scale=-1.0))(),
                        waits=[("sgp", gdiff[h])], tag=f"sf{k}_{h}")
                    for dc in range(4 * h, 4 * h + 4):
                        ins = p.emit("scalar", (lambda b_=t3sl, l_=l1sl, dc=dc: lambda eng:
                                     eng.activation(
                                         b_[:, dc * TS:(dc + 1) * TS], l_[:, dc * TS:(dc + 1) * TS],
                                         AF.Sigmoid, bias=0.0, scale=wift[:, 16 + dc:17 + dc]))(),
                                     inc=("sac", 1) if dc == 4 * h + 3 else None,
                                     tag=f"sh{k}_{dc}")
                    sgd[h] = p.val("sac")

                for h in range(2):
                    for dc in range(4 * h, 4 * h + 4):
                        p.emit("vector", (lambda l_=l1sl, dc=dc: lambda eng: eng.tensor_scalar(
                            hsl[:, dc * TS:(dc + 1) * TS], l_[:, dc * TS:(dc + 1) * TS],
                            wift[:, 16 + dc:17 + dc], 0.5, OP.mult, OP.add))(),
                            waits=[("sac", sgd[h])] if dc == 4 * h else [], tag=f"h05{k}_{dc}")
                    p.emit("vector", (lambda b_=t3sl, h=h: lambda eng: eng.tensor_tensor(
                        hsub(b_, h), hsub(hsl, h), hsub(b_, h), OP.max))(), tag=f"g{k}_{h}")
                    p.emit("vector", (lambda f_=fsl, a_=t1sl, h=h: lambda eng: eng.tensor_scalar(
                        hsub(a_, h), hsub(f_, h), -1.0, 1.0, OP.mult, OP.add))(), tag=f"i{k}_{h}")
                    p.emit("vector", (lambda v_=vsl, a_=t1sl, b_=t3sl, h=h: lambda eng:
                           eng.tensor_tensor(hsub(v_, h), hsub(a_, h), hsub(b_, h), OP.mult))(),
                           tag=f"v{k}_{h}")

                for h in range(2):
                    waits = [("sdv", carinit)]
                    if k >= 2 and h == 0:
                        waits += [("sw", ev_spill_a[k - 2])]
                    first = True
                    for dc in range(4 * h, 4 * h + 4):
                        def mksa(dc=dc, f_=fsl, a_=asl):
                            def fn(eng):
                                return eng.tensor_tensor_scan(
                                    a_[:, dc * TS:(dc + 1) * TS], f_[:, dc * TS:(dc + 1) * TS],
                                    pb[4][:, 0:TS], carA[:, dc:dc + 1], OP.mult, OP.add)
                            return fn
                        p.emit("vector", mksa(), waits=waits if first else [], tag=f"sa{k}_{dc}")
                        first = False
                    for dc in range(4 * h, 4 * h + 4):
                        def mksh(dc=dc, f_=fsl, v_=vsl, h_=hsl):
                            def fn(eng):
                                return eng.tensor_tensor_scan(
                                    h_[:, dc * TS:(dc + 1) * TS], f_[:, dc * TS:(dc + 1) * TS],
                                    v_[:, dc * TS:(dc + 1) * TS], carH[:, dc:dc + 1], OP.mult, OP.add)
                            return fn
                        p.emit("vector", mksh(), tag=f"shs{k}_{dc}")
                p.emit("vector", lambda eng, a_=asl: eng.tensor_copy(
                    carA[:], a_.rearrange("p (c n) -> p c n", c=DC)[:, :, TS - 1]), tag=f"ca{k}")
                p.emit("vector", lambda eng, h_=hsl: eng.tensor_copy(
                    carH[:], h_.rearrange("p (c n) -> p c n", c=DC)[:, :, TS - 1]), tag=f"ch{k}")
                ev_scan[k] = p.emit("vector", (lambda x_=xsl, h_=hsl: lambda eng: eng.tensor_tensor(
                    x_, x_, h_, OP.add))(), inc=("sdv", 1), tag=f"x1p{k}")

                hdst = hspv[:, :, k * TS:(k + 1) * TS].transpose([1, 0, 2])
                ev_spill_h[k] = p.emit(
                    "sync", (lambda d_=hdst, s_=xsl: lambda eng: eng.dma_start(
                        out=d_, in_=s_.rearrange("p (c n) -> p c n", c=DC)))(),
                    waits=[("sdv", ev_scan[k])], inc=("ssp", 16), tag=f"hspill{k}")
                adst = aspv[:, :, k * TS:(k + 1) * TS].transpose([1, 0, 2])
                ev_spill_a[k] = p.emit(
                    "gpsimd", (lambda d_=adst, s_=asl: lambda eng: eng.dma_start(
                        out=d_, in_=s_.rearrange("p (c n) -> p c n", c=DC)))(),
                    waits=[("sdv", ev_scan[k])], inc=("sw", 16), tag=f"aspill{k}")

            for k in range(NTILE + 2):
                if k < NTILE:
                    pre_load(k)
                    pre_sq_stats(k)
                if k >= 2:
                    mainpass(k - 2)
                if k < NTILE:
                    pre_rows_bcast(k)

            agin_vA = agin[0:1, :].rearrange("o (d q) -> (o q) d", q=128)
            agin_vB = agin[1:2, :].rearrange("o (d q) -> (o q) d", q=128)
            sA = p.emit("sync", lambda eng: eng.dma_start(out=agin_vA, in_=carA[:],
                                                          allow_slow_non_contiguous=True),
                        waits=[("sdv", ev_scan[NTILE - 1])], inc=("ssp", 16), tag="sumA")
            sB = p.emit("sync", lambda eng: eng.dma_start(out=agin_vB, in_=carH[:],
                                                          allow_slow_non_contiguous=True),
                        inc=("ssp", 16), tag="sumB")
            cc = p.emit("gpsimd", lambda eng: eng.collective_compute(
                "AllGather", OP.bypass, ins=[agin], outs=[agout],
                replica_groups=[[0, 1, 2, 3, 4, 5, 6, 7]]),
                waits=[("ssp", sB)], inc=("scc", 1), tag="allgather")
            agv = agout.rearrange("(j a) (d q) -> j a d q", a=2, q=128)
            gB = 0
            for j in range(8):
                for ab, dst in ((0, csumA), (1, csumB)):
                    gB = p.emit("scalar", (lambda j=j, ab=ab, dst=dst: lambda eng: eng.dma_start(
                        out=dst[:, j * 8:(j + 1) * 8],
                        in_=agv[j, ab, :, :].transpose([1, 0]),
                        allow_slow_non_contiguous=True))(),
                        waits=[("scc", cc)] if (j == 0 and ab == 0) else [],
                        inc=("sx", 16), tag=f"gath{ab}_{j}")

            if DEBUG:
                p.emit("sync", lambda eng: eng.dma_start(out=dbg["dbg_car"][:, 0:8], in_=carA[:]),
                       waits=[("sdv", ev_scan[NTILE - 1])], inc=("ssp", 16), tag="dbgcarA")
                p.emit("sync", lambda eng: eng.dma_start(out=dbg["dbg_car"][:, 8:16], in_=carH[:]),
                       inc=("ssp", 16), tag="dbgcarB")
                p.emit("sync", lambda eng: eng.dma_start(out=dbg["dbg_csum"][:, 0:64], in_=csumA[:]),
                       waits=[("sx", gB)], inc=("ssp", 16), tag="dbgcsA")
                gv = p.emit("sync", lambda eng: eng.dma_start(out=dbg["dbg_csum"][:, 64:128], in_=csumB[:]),
                            inc=("ssp", 16), tag="dbgcsB")
                p.set_gate(("ssp", gv))
            p.emit("vector", lambda eng: eng.memset(cvec[:], 0.0), waits=[("sx", gB)], tag="cv0")
            p.emit("vector", lambda eng: eng.tensor_single_scalar(
                lgb[:, 96:160], csumA[:], -1.0, OP.add), tag="cAm1")
            p.emit("vector", lambda eng: eng.tensor_tensor(
                lgb[:, 96:160], lgb[:, 96:160], cm64t[:], OP.mult), tag="cAm2")
            p.emit("vector", lambda eng: eng.tensor_tensor(
                lgb[:, 160:224], csumB[:], cm64t[:], OP.mult), tag="cBm")
            p.emit("vector", lambda eng: eng.drain(), tag="drn")
            for j in range(8):
                p.emit("vector", (lambda j=j: lambda eng: eng.tensor_tensor(
                    lgb[:, 88:96], cvec[:], lgb[:, 96 + j * 8:104 + j * 8], OP.mult))(),
                    tag=f"cm{j}")
                p.emit("vector", lambda eng: eng.drain(), tag="drn")
                p.emit("vector", lambda eng: eng.tensor_tensor(
                    cvec[:], cvec[:], lgb[:, 88:96], OP.add), tag=f"cu{j}")
                p.emit("vector", lambda eng: eng.drain(), tag="drn")
                p.emit("vector", (lambda j=j: lambda eng: eng.tensor_tensor(
                    cvec[:], cvec[:], lgb[:, 160 + j * 8:168 + j * 8], OP.add))(),
                    tag=f"cp{j}")
                p.emit("vector", lambda eng: eng.drain(), tag="drn")
            cdone = p.emit("vector", lambda eng: eng.tensor_copy(lgb[:, 0:8], cvec[:]),
                           inc=("sdv", 1), tag="cdone")
            if DEBUG:
                gv = p.emit("sync", lambda eng: eng.dma_start(out=dbg["dbg_cvec"], in_=cvec[:]),
                            waits=[("sdv", cdone)], inc=("ssp", 16), tag="dbgcvec")
                p.set_gate(("ssp", gv))
            spills_all = p.val("ssp")
            aspills_all = p.val("sw")

        with contextlib.ExitStack() as mctx:
            srow = mctx.enter_context(nc.sbuf_tensor("srow2", [1, 2048], F32))
            xm = mctx.enter_context(nc.sbuf_tensor("xm", [128, 2 * 4096], F32))
            am = mctx.enter_context(nc.sbuf_tensor("am", [128, 2 * 4096], BF16))
            xsqm = mctx.enter_context(nc.sbuf_tensor("xsqm", [128, 4096], BF16))
            x1bf = mctx.enter_context(nc.sbuf_tensor("x1bf", [128, 4096], BF16))
            onesmb = mctx.enter_context(nc.sbuf_tensor("onesmb", [128, 1], BF16))

            ev_ln2 = [0] * NMT
            ev_sqm_used = [0] * NMT
            ev_rows2 = [0] * NMT
            ev_x1sp = [0] * NMT
            cwstate = {"lgevac": 0, "cwout": 0}

            def emit_cw(j4):
                for dc in range(DC):
                    def mkl(dc=dc, j4=j4):
                        def fn(eng):
                            return eng.matmul(pb[6][0:15, :],
                                              rwbt[:, dc * 15:(dc + 1) * 15],
                                              ln2b[:, dc * TOK + j4 * NSUB:dc * TOK + (j4 + 1) * NSUB],
                                              start=(dc == 0), stop=(dc == DC - 1))
                        return fn
                    w = []
                    if dc == 0:
                        w = [("sdv", max(ev_ln2[j4], rwcast, cwstate["cwout"]))]
                        if j4 > 0:
                            w.append(("sac", cwstate["lgevac"]))
                    inc = ("smm", 1) if dc == DC - 1 else None
                    p.emit("tensor", mkl(), waits=w, inc=inc, tag=f"lg{j4}_{dc}")
                lgmm = p.val("smm")
                lgevac = p.emit("vector", lambda eng: eng.tensor_scalar_add(
                    lgsb[:], pb[6][0:15, :], rbt[:, 0:1]),
                    waits=[("smm", lgmm)], inc=("sac", 1), tag=f"lgev{j4}")
                cwstate["lgevac"] = lgevac

                w = [("sac", lgevac)]
                if j4 > 0:
                    w.append(("sdv", cwstate["cwout"]))
                tf = 0
                for g2 in range(4):
                    tf = p.emit("tensor", (lambda g2=g2: lambda eng: eng.transpose(
                        pb[7][:, g2 * 15:(g2 + 1) * 15], lgsb[:, g2 * 128:(g2 + 1) * 128],
                        identt[0:15, 0:15]))(),
                        waits=w if g2 == 0 else [], inc=("smm", 1) if g2 == 3 else None,
                        tag=f"trf{j4}_{g2}")
                def rounds1(g2):
                    L = g2 * 88
                    yield lambda eng: eng.tensor_copy(lgb[:, L:L + 15], pb[7][:, g2 * 15:(g2 + 1) * 15])
                    yield lambda eng: eng.reduce_max(lgb[:, L + 15:L + 16], lgb[:, L:L + 15],
                                                     axis=AX, negate=True)
                    yield lambda eng: eng.tensor_single_scalar(lgb[:, L + 16:L + 17],
                                                               lgb[:, L + 15:L + 16], -1.0, OP.mult)
                    yield lambda eng: eng.tensor_scalar(lgb[:, L + 21:L + 36], lgb[:, L:L + 15],
                                                        lgb[:, L + 16:L + 17], -1e30,
                                                        OP.is_equal, OP.mult)
                    yield lambda eng: eng.tensor_tensor(lgb[:, L + 36:L + 51], lgb[:, L:L + 15],
                                                        lgb[:, L + 21:L + 36], OP.add)
                    yield lambda eng: eng.reduce_max(lgb[:, L + 17:L + 18], lgb[:, L + 36:L + 51],
                                                     axis=AX, negate=True)
                    yield lambda eng: eng.tensor_single_scalar(lgb[:, L + 18:L + 19],
                                                               lgb[:, L + 17:L + 18], -1.0, OP.mult)
                    yield lambda eng: eng.tensor_scalar(lgb[:, L + 21:L + 36], lgb[:, L + 36:L + 51],
                                                        lgb[:, L + 18:L + 19], -1e30,
                                                        OP.is_equal, OP.mult)
                    yield lambda eng: eng.tensor_tensor(lgb[:, L + 36:L + 51], lgb[:, L + 36:L + 51],
                                                        lgb[:, L + 21:L + 36], OP.add)
                    yield lambda eng: eng.reduce_max(lgb[:, L + 17:L + 18], lgb[:, L + 36:L + 51],
                                                     axis=AX, negate=True)
                    yield lambda eng: eng.tensor_single_scalar(lgb[:, L + 18:L + 19],
                                                               lgb[:, L + 17:L + 18], -1.0, OP.mult)
                    yield lambda eng: eng.tensor_single_scalar(lgb[:, L + 21:L + 36], lgb[:, L:L + 15],
                                                               lgb[:, L + 18:L + 19], OP.is_ge)

                gens = [rounds1(g2) for g2 in range(4)]
                for r in range(12):
                    for g2 in range(4):
                        fn = next(gens[g2])
                        inc = ("sdv", 1) if (r == 11 and g2 == 3) else None
                        p.emit("vector", fn,
                               waits=[("smm", tf)] if (r == 0 and g2 == 0) else [],
                               inc=inc, tag=f"cwr1_{j4}_{r}_{g2}")
                r1d = p.val("sdv")
                exd = 0
                for g2 in range(4):
                    L = g2 * 88
                    exd = p.emit("scalar", (lambda L=L: lambda eng: eng.activation(
                        lgb[:, L + 51:L + 66], lgb[:, L:L + 15], AF.Exp,
                        bias=lgb[:, L + 15:L + 16], scale=1.0))(),
                        waits=[("sdv", r1d)] if g2 == 0 else [],
                        inc=("sac", 1) if g2 == 3 else None, tag=f"cwex{j4}_{g2}")

                def rounds2(g2):
                    L = g2 * 88
                    yield lambda eng: eng.reduce_sum(lgb[:, L + 19:L + 20], lgb[:, L + 51:L + 66], axis=AX)
                    yield lambda eng: eng.reciprocal(lgb[:, L + 20:L + 21], lgb[:, L + 19:L + 20])
                    yield lambda eng: eng.tensor_single_scalar(lgb[:, L + 51:L + 66],
                                                               lgb[:, L + 51:L + 66],
                                                               lgb[:, L + 20:L + 21], OP.mult)
                    yield lambda eng: eng.tensor_tensor(lgb[:, L + 66:L + 81], lgb[:, L + 51:L + 66],
                                                        lgb[:, L + 21:L + 36], OP.mult)

                gens = [rounds2(g2) for g2 in range(4)]
                for r in range(4):
                    for g2 in range(4):
                        fn = next(gens[g2])
                        inc = ("sdv", 1) if (r == 3 and g2 == 3) else None
                        p.emit("vector", fn, waits=[("sac", exd)] if (r == 0 and g2 == 0) else [],
                               inc=inc, tag=f"cwr2_{j4}_{r}_{g2}")
                r2d = p.val("sdv")
                tb = 0
                for g2 in range(4):
                    L = g2 * 88
                    tb = p.emit("tensor", (lambda L=L, g2=g2: lambda eng: eng.transpose(
                        pb[6][0:15, g2 * 128:(g2 + 1) * 128], lgb[:, L + 66:L + 81], identt[:]))(),
                        waits=[("sdv", r2d)] if g2 == 0 else [],
                        inc=("smm", 1) if g2 == 3 else None, tag=f"trb{j4}_{g2}")
                cwstate["cwout"] = p.emit("vector", (lambda j4=j4: lambda eng: eng.tensor_copy(
                    cwfb[0:15, j4 * 512:(j4 + 1) * 512], pb[6][0:15, 0:512]))(),
                    waits=[("smm", tb)], inc=("sdv", 1), tag=f"cwout{j4}")

            for m in range(NMT):
                par = m % 2
                xsl = xm[:, par * 4096:(par + 1) * 4096]
                asl = am[:, par * 4096:(par + 1) * 4096]
                MS = MSUB

                waits = [("ssp", spills_all), ("sw", aspills_all)]
                if m >= 2:
                    waits.append(("sdv", ev_ln2[m - 2]))
                    waits.append(("ssp", ev_x1sp[m - 2]))
                xld = p.emit("sync", (lambda m=m, d_=xsl: lambda eng: eng.dma_start(
                    out=d_.rearrange("p (c n) -> p c n", c=DC),
                    in_=hspv[:, :, m * MSUB:(m + 1) * MSUB].transpose([1, 0, 2])))(),
                    waits=waits, inc=("sx", 16), tag=f"mx{m}")
                ald = p.emit("sync", (lambda m=m, d_=asl: lambda eng: eng.dma_start(
                    out=d_.rearrange("p (c n) -> p c n", c=DC),
                    in_=aspv[:, :, m * MSUB:(m + 1) * MSUB].transpose([1, 0, 2])))(),
                    inc=("sx", 16), tag=f"ma{m}")

                for dc in range(DC):
                    def mkf(dc=dc, x_=xsl, a_=asl):
                        def fn(eng):
                            sl = slice(dc * MS, (dc + 1) * MS)
                            return eng.scalar_tensor_tensor(x_[:, sl], a_[:, sl], cvec[:, dc:dc + 1],
                                                            x_[:, sl], OP.mult, OP.add)
                        return fn
                    w = [("sx", ald), ("sdv", cdone)] if dc == 0 else []
                    inc = ("sdv", 1) if dc == DC - 1 else None
                    p.emit("vector", mkf(), waits=w, inc=inc, tag=f"fix{m}_{dc}")
                x1d = p.val("sdv")
                if DEBUG:
                    gv = p.emit("sync", (lambda m=m, x_=xsl: lambda eng: eng.dma_start(
                        out=cpview(dbg["dbg_h"])[:, :, m * MSUB:(m + 1) * MSUB].transpose([1, 0, 2]),
                        in_=x_.rearrange("p (c n) -> p c n", c=DC)))(),
                        waits=[("sdv", x1d)], inc=("ssp", 16), tag=f"dbgh{m}")
                    p.set_gate(("ssp", gv))

                ev_x1sp[m] = p.emit("sync", (lambda m=m, s_=xsl: lambda eng: eng.dma_start(
                    out=x1v[:, :, m * MSUB:(m + 1) * MSUB].transpose([1, 0, 2]),
                    in_=s_.rearrange("p (c n) -> p c n", c=DC)))(),
                    waits=[("sdv", x1d)], inc=("ssp", 16), tag=f"x1sp{m}")

                waits = [("sdv", x1d)]
                if m >= 1:
                    waits.append(("smm", ev_sqm_used[m - 1]))
                    waits.append(("sdv", ev_ln2[m - 1]))
                if m == 0:
                    p.emit("vector", lambda eng: eng.tensor_copy(onesmb[:], onesm1t[:]),
                           tag="onesmb")
                bw = [("sdv", x1d)]
                if m >= 1:
                    bw.append(("smm", ev_sqm_used[m - 1]))
                mxbf = p.emit("vector", (lambda x_=xsl: lambda eng: eng.tensor_copy(
                    x1bf[:], x_))(), waits=bw, inc=("sdv", 1), tag=f"mxbf{m}")
                sqm = p.emit("scalar", (lambda x_=xsl: lambda eng: eng.activation(
                    xsqm[:], x_, AF.Square, bias=0.0, scale=1.0))(),
                    waits=waits, inc=("sac", 1), tag=f"msq{m}")
                waits = [("sac", sqm), ("sdv", mxbf)]
                if m >= 1:
                    waits.append(("sdv", ev_rows2[m - 1]))
                for dc in range(DC):
                    def mks(dc=dc):
                        def fn(eng):
                            return eng.matmul(pb[0][0:1, 0:MS], onesmb[:],
                                              x1bf[:, dc * MS:(dc + 1) * MS],
                                              start=(dc == 0), stop=(dc == DC - 1))
                        return fn
                    p.emit("tensor", mks(), waits=waits if dc == 0 else [], tag=f"msum{m}_{dc}")
                for dc in range(DC):
                    def mks2(dc=dc):
                        def fn(eng):
                            return eng.matmul(pb[1][0:1, 0:MS], onesmb[:],
                                              xsqm[:, dc * MS:(dc + 1) * MS],
                                              start=(dc == 0), stop=(dc == DC - 1))
                        return fn
                    inc = ("smm", 1) if dc == DC - 1 else None
                    p.emit("tensor", mks2(), inc=inc, tag=f"msumsq{m}_{dc}")
                ev_sqm_used[m] = p.val("smm")

                mrow = srow[0:1, 0:MS]
                arow = srow[0:1, 512:512 + MS]
                trow = srow[0:1, 1024:1024 + MS]
                rrow = srow[0:1, 1536:1536 + MS]
                p.emit("vector", (lambda mr=mrow: lambda eng: eng.tensor_single_scalar(
                    mr, pb[0][0:1, 0:MS], 1.0 / D, OP.mult))(),
                    waits=[("smm", ev_sqm_used[m])], tag=f"mrowm{m}")
                p.emit("vector", (lambda ar=arow: lambda eng: eng.tensor_scalar(
                    ar, pb[1][0:1, 0:MS], 1.0 / D, EPS, OP.mult, OP.add))(), tag=f"mrowa{m}")
                p.emit("vector", (lambda mr=mrow, tr=trow: lambda eng: eng.tensor_tensor(
                    tr, mr, mr, OP.mult))(), tag=f"mrowt{m}")
                ev_rows2[m] = p.emit("vector", (lambda ar=arow, tr=trow: lambda eng: eng.tensor_tensor(
                    ar, ar, tr, OP.subtract))(), inc=("sdv", 1), tag=f"mrowb{m}")
                p.emit("scalar", (lambda ar=arow, tr=trow: lambda eng: eng.activation(
                    tr, ar, AF.Ln, bias=0.0, scale=1.0))(),
                    waits=[("sdv", ev_rows2[m])], tag=f"mrowln{m}")
                rr2 = p.emit("scalar", (lambda rr=rrow, tr=trow: lambda eng: eng.activation(
                    rr, tr, AF.Exp, bias=0.0, scale=-0.5))(), inc=("sac", 1), tag=f"mrowr{m}")

                waits = [("sac", rr2)]
                if m >= 1:
                    waits.append(("sdv", ev_ln2[m - 1]))
                p.emit("tensor", (lambda mr=mrow: lambda eng: eng.matmul(
                    pb[2][:, 0:MS], onesk1t[:], mr, start=True, stop=True))(),
                    waits=waits, tag=f"mbcm{m}")
                bcs = p.emit("tensor", (lambda rr=rrow: lambda eng: eng.matmul(
                    pb[3][:, 0:MS], onesk1t[:], rr, start=True, stop=True))(),
                    inc=("smm", 1), tag=f"mbcr{m}")

                for dc in range(DC):
                    def mka(dc=dc, x_=xsl):
                        def fn(eng):
                            sl = slice(dc * MS, (dc + 1) * MS)
                            return eng.tensor_tensor(xsqm[:, sl], x_[:, sl], pb[2][:, 0:MS], OP.subtract)
                        return fn
                    p.emit("vector", mka(), waits=[("smm", bcs)] if dc == 0 else [], tag=f"l2a{m}_{dc}")
                for dc in range(DC):
                    def mkb(dc=dc, m=m):
                        def fn(eng):
                            return eng.tensor_tensor(
                                ln2b[:, dc * TOK + m * MSUB:dc * TOK + (m + 1) * MSUB],
                                xsqm[:, dc * MS:(dc + 1) * MS], pb[3][:, 0:MS], OP.mult)
                        return fn
                    inc = ("sdv", 1) if dc == DC - 1 else None
                    p.emit("vector", mkb(), inc=inc, tag=f"l2b{m}_{dc}")
                ev_ln2[m] = p.val("sdv")
                if m < NMT - 1:
                    emit_cw(m)


            ln2_all = p.val("sdv")
            x1_spilled = p.val("ssp")

        with contextlib.ExitStack() as fctx:
            oneht = fctx.enter_context(nc.sbuf_tensor("oneht", [15, 1920], BF16))
            eb2t = fctx.enter_context(nc.sbuf_tensor("eb2t", [15, 1024], BF16))
            acc = fctx.enter_context(nc.sbuf_tensor("acc", [128, DC * TOK], F32))
            h_sb = fctx.enter_context(nc.sbuf_tensor("h_sb", [128, 2 * 4096], BF16))
            rtmp = fctx.enter_context(nc.sbuf_tensor("rtmp", [128, 2 * 512], F32))

            ohl = p.emit("sync", lambda eng: eng.dma_start(out=oneht[:], in_=onehd),
                         waits=[("sdv", ln2_all), ("ssp", x1_spilled)], inc=("sx", 16), tag="ohl")
            eb2l = None


            if DEBUG:
                gv = p.emit("gpsimd", lambda eng: eng.dma_start(out=dbg["dbg_cw"], in_=cwfb[:]),
                            waits=[("sdv", cwstate["cwout"])], inc=("sw", 16), tag="dbgcw")
                p.set_gate(("sw", gv))
                lv = (lambda eng: eng.dma_start(
                    out=cpview(dbg["dbg_ln2"]).transpose([1, 0, 2]),
                    in_=ln2b[:].rearrange("p (c n) -> p c n", c=DC)))
                gv = p.emit("gpsimd", lv, waits=[("sdv", ln2_all)], inc=("sw", 16), tag="dbgln2")
                p.set_gate(("sw", gv))

            with contextlib.ExitStack() as wctx:
                wbuf = [wctx.enter_context(nc.sbuf_tensor(f"wbuf{i}", [128, 2 * 8192], BF16))
                        for i in range(2)]

                def wload(eng_unused, ee, buf):
                    if ee == 0:
                        w1src, w2src = sW1d, sW2d
                    else:
                        w1src = eW1d[ee - 1]
                        w2src = eW2d[ee - 1]
                    v1 = w1src.rearrange("(c p) n -> c p n", p=128).transpose([1, 0, 2])
                    v2 = w2src.rearrange("(c p) n -> c p n", p=128).transpose([1, 0, 2])
                    d1 = wbuf[buf][:, 0:8192].rearrange("p (c n) -> p c n", c=DC)
                    d2 = wbuf[buf][:, 8192:16384].rearrange("p (c n) -> p c n", c=DC)
                    return v1, d1, v2, d2

                ev_wload = [0] * NEXP
                ev_ee_mm_end = [0] * NEXP

                for ee in range(2):
                    v1, d1, v2, d2 = wload(None, ee, ee % 2)
                    w = [("ssp", ev_spill_h[NTILE - 1]), ("sw", ev_spill_a[NTILE - 1])] if ee == 0 else []
                    p.emit("gpsimd", (lambda a=v1, b=d1: lambda eng: eng.dma_start(out=b, in_=a))(),
                           waits=w, inc=("sw", 16), tag=f"w1l{ee}")
                    ev_wload[ee] = p.emit(
                        "gpsimd", (lambda a=v2, b=d2: lambda eng: eng.dma_start(out=b, in_=a))(),
                        inc=("sw", 16), tag=f"w2l{ee}")
                eb2l = p.emit("gpsimd", lambda eng: eng.dma_start(out=eb2t[:], in_=eb2d),
                              waits=[("sdv", ln2_all), ("ssp", x1_spilled)], inc=("sw", 16), tag="eb2l")

                hfree = [0, 0]
                yfree = [0, 0]
                cwbfree = [0, 0]
                rtfree = [0, 0]

                for ee in range(NEXP):
                    wpar = ee % 2
                    hm_done = {}

                    def emit_cwb(j, ee=None):
                        cb = 4 + (j % 2)
                        w = [("sdv", max(cwstate["cwout"], cwbfree[j % 2])), ("sx", ohl)]
                        cwb = p.emit("tensor", (lambda ee=ee, j=j, cb=cb: lambda eng: eng.matmul(
                            pb[cb][:], oneht[:, (ee - 1) * 128:ee * 128],
                            cwfb[0:15, j * NSUB:(j + 1) * NSUB], start=True, stop=True))(),
                            waits=w, inc=("smm", 1), tag=f"cwb{ee}_{j}")
                        return cwb

                    def emit_w1(j, cwb, ee=None):
                        hpar = j % 2
                        cb = 4 + (j % 2)
                        hmul_events = []
                        for hc in range(HCN):
                            bk = hc % 2
                            w = []
                            if ee == 0 and j == 0 and hc == 0:
                                w.append(("sdv", ln2_all))
                                w.append(("ssp", x1_spilled))
                            w.append(("sw", ev_wload[ee]))
                            if hfree[bk]:
                                w.append(("sac", hfree[bk]))
                            for dc in range(DC):
                                def mkw1(ee=ee, j=j, hc=hc, dc=dc, bk=bk, wpar=wpar):
                                    def fn(eng):
                                        return eng.matmul(
                                            pb[bk][:],
                                            wbuf[wpar][:, dc * 1024 + hc * 128:dc * 1024 + (hc + 1) * 128],
                                            ln2b[:, dc * TOK + j * NSUB:dc * TOK + (j + 1) * NSUB],
                                            start=(dc == 0), stop=(dc == DC - 1))
                                    return fn
                                inc = ("smm", 1) if dc == DC - 1 else None
                                p.emit("tensor", mkw1(), waits=w if dc == 0 else [], inc=inc,
                                       tag=f"W1_{ee}_{j}_{hc}_{dc}")
                                w = []
                            w1g = p.val("smm")
                            if ee == 0:
                                hfree[bk] = p.emit("scalar", (lambda hc=hc, bk=bk, hpar=hpar: lambda eng:
                                                   eng.activation(
                                                       h_sb[:, hpar * 4096 + hc * 512:hpar * 4096 + (hc + 1) * 512],
                                                       pb[bk][:], AF.Relu, bias=sbvt[:, hc:hc + 1], scale=1.0))(),
                                                   waits=[("smm", w1g)], inc=("sac", 1), tag=f"relu{ee}_{j}_{hc}")
                                hmul_events.append(("sac", hfree[bk]))
                            else:
                                rslot = hc % 2
                                rw = [("smm", w1g)]
                                if rtfree[rslot]:
                                    rw.append(("sdv", rtfree[rslot]))
                                rel = p.emit("scalar", (lambda ee=ee, hc=hc, bk=bk, rslot=rslot: lambda eng:
                                             eng.activation(
                                                 rtmp[:, rslot * 512:(rslot + 1) * 512], pb[bk][:], AF.Relu,
                                                 bias=eb1t[:, (ee - 1) * 8 + hc:(ee - 1) * 8 + hc + 1],
                                                 scale=1.0))(),
                                             waits=rw, inc=("sac", 1), tag=f"relu{ee}_{j}_{hc}")
                                hfree[bk] = rel
                                hm_ = p.emit("vector", (lambda hc=hc, hpar=hpar, cb=cb, rslot=rslot: lambda eng:
                                             eng.tensor_tensor(
                                                 h_sb[:, hpar * 4096 + hc * 512:hpar * 4096 + (hc + 1) * 512],
                                                 rtmp[:, rslot * 512:(rslot + 1) * 512], pb[cb][:], OP.mult))(),
                                             waits=[("sac", rel), ("smm", cwb)], inc=("sdv", 1),
                                             tag=f"hmul{ee}_{j}_{hc}")
                                rtfree[rslot] = hm_
                                hmul_events.append(("sdv", hm_))
                        if ee >= 1:
                            cwbfree[j % 2] = hmul_events[-1][1]
                        hm_done[j] = hmul_events[-1]

                    def emit_w2(j, ee=None):
                        hpar = j % 2
                        for dc in range(DC):
                            bk = 2 + dc % 2
                            w = []
                            if dc == 0:
                                w.append(hm_done[j])
                            if yfree[dc % 2]:
                                w.append(("sdv", yfree[dc % 2]))
                            for hc in range(HCN):
                                def mkw2(ee=ee, j=j, hc=hc, dc=dc, bk=bk, wpar=wpar, hpar=hpar):
                                    def fn(eng):
                                        return eng.matmul(
                                            pb[bk][:],
                                            wbuf[wpar][:, 8192 + hc * 1024 + dc * 128:8192 + hc * 1024 + (dc + 1) * 128],
                                            h_sb[:, hpar * 4096 + hc * 512:hpar * 4096 + (hc + 1) * 512],
                                            start=(hc == 0), stop=(hc == HCN - 1 and ee != 1))
                                    return fn
                                inc = ("smm", 1) if (hc == HCN - 1 and ee != 1) else None
                                p.emit("tensor", mkw2(), waits=w if hc == 0 else [],
                                       inc=inc, tag=f"W2_{ee}_{j}_{dc}_{hc}")
                                w = []
                            if ee == 1:
                                def mkeb(j=j, dc=dc, bk=bk):
                                    def fn(eng):
                                        return eng.matmul(
                                            pb[bk][:], eb2t[:, dc * 128:(dc + 1) * 128],
                                            cwfb[0:15, j * NSUB:(j + 1) * NSUB],
                                            start=False, stop=True)
                                    return fn
                                p.emit("tensor", mkeb(), waits=[("sw", eb2l)],
                                       inc=("smm", 1), tag=f"eb2_{j}_{dc}")
                            w2g = p.val("smm")
                            if ee == 0:
                                yfree[dc % 2] = p.emit("vector", (lambda j=j, dc=dc, bk=bk: lambda eng:
                                                       eng.tensor_copy(
                                                           acc[:, dc * TOK + j * NSUB:dc * TOK + (j + 1) * NSUB],
                                                           pb[bk][:]))(),
                                                       waits=[("smm", w2g)], inc=("sdv", 1),
                                                       tag=f"acc{ee}_{j}_{dc}")
                            else:
                                yfree[dc % 2] = p.emit("vector", (lambda j=j, dc=dc, bk=bk: lambda eng:
                                                       eng.tensor_tensor(
                                                           acc[:, dc * TOK + j * NSUB:dc * TOK + (j + 1) * NSUB],
                                                           acc[:, dc * TOK + j * NSUB:dc * TOK + (j + 1) * NSUB],
                                                           pb[bk][:], OP.add))(),
                                                       waits=[("smm", w2g)], inc=("sdv", 1),
                                                       tag=f"acc{ee}_{j}_{dc}")

                    cwbs = {}
                    if ee >= 1:
                        cwbs[0] = emit_cwb(0, ee=ee)
                    emit_w1(0, cwbs.get(0), ee=ee)
                    if ee == 0:
                        emit_cw(NMT - 1)
                    if ee >= 1:
                        cwbs[1] = emit_cwb(1, ee=ee)
                    emit_w1(1, cwbs.get(1), ee=ee)
                    emit_w2(0, ee=ee)
                    if ee >= 1:
                        cwbs[2] = emit_cwb(2, ee=ee)
                    emit_w1(2, cwbs.get(2), ee=ee)
                    emit_w2(1, ee=ee)
                    if ee >= 1:
                        cwbs[3] = emit_cwb(3, ee=ee)
                    emit_w1(3, cwbs.get(3), ee=ee)
                    emit_w2(2, ee=ee)
                    emit_w2(3, ee=ee)

                    ev_ee_mm_end[ee] = p.val("smm")
                    if ee + 2 < NEXP:
                        v1, d1, v2, d2 = wload(None, ee + 2, wpar)
                        p.emit("gpsimd", (lambda a=v1, b=d1: lambda eng:
                               eng.dma_start(out=b, in_=a))(),
                               waits=[("smm", ev_ee_mm_end[ee])], inc=("sw", 16), tag=f"w1l{ee + 2}")
                        ev_wload[ee + 2] = p.emit(
                            "gpsimd", (lambda a=v2, b=d2: lambda eng: eng.dma_start(out=b, in_=a))(),
                            inc=("sw", 16), tag=f"w2l{ee + 2}")

                acc_done = p.val("sdv")
                if DEBUG:
                    p.emit("gpsimd", lambda eng: eng.dma_start(
                        out=cpview(dbg["dbg_acc"]).transpose([1, 0, 2]),
                        in_=acc[:].rearrange("p (c n) -> p c n", c=DC)),
                        waits=[("sdv", acc_done)], inc=("sw", 16), tag="dbgacc")

            with contextlib.ExitStack() as ectx:
                xr = ectx.enter_context(nc.sbuf_tensor("xr", [128, 2 * 4096], F32))
                xo = ectx.enter_context(nc.sbuf_tensor("xo", [128, 2 * 4096], F32))
                ev_out = [0] * NJ
                ev_fin = [0] * NJ
                for f in range(NJ):
                    par = f % 2
                    w = [("ssp", x1_spilled), ("smm", ev_ee_mm_end[NEXP - 2])]
                    if f >= 2:
                        w.append(("sdv", ev_fin[f - 2]))
                    xld = p.emit("sync", (lambda f=f, par=par: lambda eng: eng.dma_start(
                        out=xr[:, par * 4096:(par + 1) * 4096].rearrange("p (c n) -> p c n", c=DC),
                        in_=x1v[:, :, f * NSUB:(f + 1) * NSUB].transpose([1, 0, 2])))(),
                        waits=w, inc=("sx", 16), tag=f"xrl{f}")
                    w = [("sx", xld), ("sdv", acc_done)]
                    if f >= 2:
                        w.append(("ssp", ev_out[f - 2]))
                    for dc in range(DC):
                        def mkf(f=f, dc=dc, par=par):
                            def fn(eng):
                                return eng.scalar_tensor_tensor(
                                    xo[:, par * 4096 + dc * NSUB:par * 4096 + (dc + 1) * NSUB],
                                    acc[:, dc * TOK + f * NSUB:dc * TOK + (f + 1) * NSUB],
                                    sbvt[:, 8 + dc:9 + dc],
                                    xr[:, par * 4096 + dc * NSUB:par * 4096 + (dc + 1) * NSUB],
                                    OP.add, OP.add)
                            return fn
                        inc = ("sdv", 1) if dc == DC - 1 else None
                        p.emit("vector", mkf(), waits=w if dc == 0 else [], inc=inc, tag=f"fin{f}_{dc}")
                    ev_fin[f] = p.val("sdv")
                    ev_out[f] = p.emit("sync", (lambda f=f, par=par: lambda eng: eng.dma_start(
                        out=outv[:, :, f * NSUB:(f + 1) * NSUB].transpose([1, 0, 2]),
                        in_=xo[:, par * 4096:(par + 1) * 4096].rearrange("p (c n) -> p c n", c=DC)))(),
                        waits=[("sdv", ev_fin[f])], inc=("ssp", 16), tag=f"outs{f}")

                p.emit("sync", lambda eng: None, waits=[("ssp", p.val("ssp")), ("sw", p.val("sw"))],
                       tag="final_wait")

        p.check_deadlock()
        with nc.Block() as block:
            p.build(block)
    return nc


def kernel(L0_attn_x_t, L0_attn_w_i, L0_attn_w_f, L0_attn_w_h,
           L0_ffn_router_W, L0_ffn_router_b, L0_ffn_sW1, L0_ffn_sb1,
           L0_ffn_sW2, L0_ffn_sb2, L0_ffn_eW1, L0_ffn_eb1, L0_ffn_eW2, L0_ffn_eb2):
    x = np.asarray(L0_attn_x_t, dtype=np.float32)

    def fm(v):
        return np.ascontiguousarray(np.asarray(v, np.float32).reshape(8, 128).T)

    wif = np.concatenate([-fm(L0_attn_w_f), -fm(L0_attn_w_i), fm(L0_attn_w_h)], axis=1)
    sbv = np.concatenate([fm(L0_ffn_sb1), fm(L0_ffn_sb2)], axis=1)
    eb1p = np.ascontiguousarray(
        np.asarray(L0_ffn_eb1, np.float32).reshape(15, 8, 128).transpose(2, 0, 1).reshape(128, 120))
    eb2p = np.ascontiguousarray(np.asarray(L0_ffn_eb2, np.float32))
    rwp = np.ascontiguousarray(
        np.asarray(L0_ffn_router_W, np.float32).reshape(8, 128, 15).transpose(1, 0, 2).reshape(128, 120))
    rbp = np.ascontiguousarray(np.asarray(L0_ffn_router_b, np.float32).reshape(15, 1))
    sW1 = np.ascontiguousarray(np.asarray(L0_ffn_sW1, np.float32))
    sW2 = np.ascontiguousarray(np.asarray(L0_ffn_sW2, np.float32))
    eW1 = np.ascontiguousarray(np.asarray(L0_ffn_eW1, np.float32))
    eW2 = np.ascontiguousarray(np.asarray(L0_ffn_eW2, np.float32))

    nc = build_nc()
    in_maps = []
    for c in range(8):
        b, pp = c // 4, c % 4
        xcn = np.ascontiguousarray(x[b, pp * TOK:(pp + 1) * TOK, :].T)
        cm = np.zeros((128, 8), np.float32)
        for j in range(8):
            if j // 4 == b and j % 4 < pp:
                cm[:, j] = 1.0
        in_maps.append({
            "xc": xcn, "cmask": cm, "cmask1m": np.ascontiguousarray(1.0 - cm),
            "cm64": np.ascontiguousarray(np.repeat(cm, 8, axis=1)),
            "wif": wif, "sbv": sbv, "eb1p": eb1p, "eb2p": eb2p, "rwp": rwp, "rbp": rbp,
            "sW1": sW1, "sW2": sW2, "eW1": eW1, "eW2": eW2,
        })

    res = run_bass_kernel_spmd(nc, in_maps, core_ids=list(range(8)))
    out = np.empty((B, S, D), np.float32)
    for c in range(8):
        b, pp = c // 4, c % 4
        out[b, pp * TOK:(pp + 1) * TOK, :] = res.results[c]["out_fm"].T
    if DEBUG:
        kernel.last_results = res
    return out


# version 09
# speedup vs baseline: 1.0084x; 1.0084x over previous
import contextlib

import numpy as np

import concourse.bass as bass
import concourse.mybir as mybir
from concourse.bass_utils import run_bass_kernel_spmd

F32 = mybir.dt.float32
BF16 = mybir.dt.bfloat16
OP = mybir.AluOpType
AF = mybir.ActivationFunctionType
AX = mybir.AxisListType.X

B, S, D = 2, 8192, 1024
E, H = 15, 1024
DC = D // 128
HCN = H // 128
TOK = 2048
TS = 256
NTILE = TOK // TS
MSUB = 512
NMT = TOK // MSUB
NSUB = 512
NJ = TOK // NSUB
EPS = 1e-5
NEXP = 16

DEBUG = False


class Prog:

    ENGINES = ("sync", "tensor", "vector", "scalar", "gpsimd")

    def __init__(self, nc):
        self.nc = nc
        self.ops = {e: [] for e in self.ENGINES}
        self.sems = {}
        self.pending_gate = None
        self.gated = set()

    def set_gate(self, gate):
        self.pending_gate = gate
        self.gated = set()

    def add_sem(self, ctx, name):
        h = ctx.enter_context(self.nc.semaphore(name))
        self.sems[name] = [h, 0]

    def emit(self, eng, fn, waits=(), inc=None, tag=""):
        wl = [(s, v) for (s, v) in waits]
        if self.pending_gate is not None and eng not in self.gated:
            wl.append(self.pending_gate)
            self.gated.add(eng)
            if len(self.gated) == len(self.ENGINES):
                self.pending_gate = None
        self.ops[eng].append((fn, wl, inc, tag))
        if inc is not None:
            self.sems[inc[0]][1] += inc[1]
            return self.sems[inc[0]][1]
        return None

    def val(self, name):
        return self.sems[name][1]

    @staticmethod
    def _rv(v):
        return v() if callable(v) else v

    def check_deadlock(self):
        counts = {name: 0 for name in self.sems}
        ptrs = {e: 0 for e in self.ENGINES}
        progress = True
        while progress:
            progress = False
            for e in self.ENGINES:
                while ptrs[e] < len(self.ops[e]):
                    fn, waits, inc, tag = self.ops[e][ptrs[e]]
                    if all(counts[s] >= self._rv(v) for (s, v) in waits):
                        if inc is not None:
                            counts[inc[0]] += inc[1]
                        ptrs[e] += 1
                        progress = True
                    else:
                        break
        stuck = {e: ptrs[e] for e in self.ENGINES if ptrs[e] < len(self.ops[e])}
        if stuck:
            msgs = []
            for e, i in stuck.items():
                fn, waits, inc, tag = self.ops[e][i]
                blocked = [(s, self._rv(v), counts[s]) for (s, v) in waits
                           if counts[s] < self._rv(v)]
                msgs.append(f"{e}[{i}] tag={tag} blocked on {blocked}")
            raise RuntimeError("DEADLOCK in schedule:\n" + "\n".join(msgs))

    def build(self, block):
        def mk(e):
            def body(engobj):
                for fn, waits, inc, tag in self.ops[e]:
                    for (s, v) in waits:
                        engobj.wait_ge(self.sems[s][0], Prog._rv(v))
                    instr = fn(engobj)
                    if inc is not None:
                        assert instr is not None, f"inc without instruction: {e} {tag}"
                        instr.then_inc(self.sems[inc[0]][0], inc[1])
            return body

        block.sync(mk("sync"))
        block.tensor(mk("tensor"))
        block.vector(mk("vector"))
        block.scalar(mk("scalar"))
        block.gpsimd(mk("gpsimd"))


def build_nc():
    nc = bass.Bass("TRN2", target_bir_lowering=False, debug=False)

    xc = nc.dram_tensor("xc", [D, TOK], F32, kind="ExternalInput").ap()
    cmaskd = nc.dram_tensor("cmask", [128, 8], F32, kind="ExternalInput").ap()
    cm64d = nc.dram_tensor("cm64", [128, 64], F32, kind="ExternalInput").ap()
    cmask1d = nc.dram_tensor("cmask1m", [128, 8], F32, kind="ExternalInput").ap()
    wifd = nc.dram_tensor("wif", [128, 24], F32, kind="ExternalInput").ap()
    sbvd = nc.dram_tensor("sbv", [128, 16], F32, kind="ExternalInput").ap()
    eb1d = nc.dram_tensor("eb1p", [128, 120], F32, kind="ExternalInput").ap()
    eb2d = nc.dram_tensor("eb2p", [15, 1024], F32, kind="ExternalInput").ap()
    rwd = nc.dram_tensor("rwp", [128, 120], F32, kind="ExternalInput").ap()
    rbd = nc.dram_tensor("rbp", [15, 1], F32, kind="ExternalInput").ap()
    sW1d = nc.dram_tensor("sW1", [1024, 1024], F32, kind="ExternalInput").ap()
    sW2d = nc.dram_tensor("sW2", [1024, 1024], F32, kind="ExternalInput").ap()
    eW1d = nc.dram_tensor("eW1", [15, 1024, 1024], F32, kind="ExternalInput").ap()
    eW2d = nc.dram_tensor("eW2", [15, 1024, 1024], F32, kind="ExternalInput").ap()

    out_fm = nc.dram_tensor("out_fm", [D, TOK], F32, kind="ExternalOutput").ap()

    x1sp = nc.dram_tensor("x1sp", [D, TOK], F32, kind="Internal").ap()
    hsp = nc.dram_tensor("hsp", [D, TOK], F32, kind="Internal").ap()
    asp = nc.dram_tensor("asp", [D, TOK], BF16, kind="Internal").ap()
    agin = nc.dram_tensor("agin", [2, 1024], F32, kind="Internal").ap()
    agout = nc.dram_tensor("agout", [16, 1024], F32, kind="Internal").ap()

    identd = nc.inline_tensor(np.eye(128, dtype=np.float32), "identc").ap()
    onesk1d = nc.inline_tensor(np.ones((1, 128), dtype=np.float32), "onesk1c").ap()
    onesm1d = nc.inline_tensor(np.ones((128, 1), dtype=np.float32), "onesm1c").ap()
    oneh_np = np.zeros((15, 15 * 128), dtype=np.float32)
    for e in range(15):
        oneh_np[e, e * 128:(e + 1) * 128] = 1.0
    import ml_dtypes
    onehd = nc.inline_tensor(oneh_np.astype(ml_dtypes.bfloat16), "onehc").ap()

    dbg = {}
    if DEBUG:
        for name, shape in [
            ("dbg_ln1", [D, TOK]), ("dbg_f", [D, TOK]), ("dbg_v", [D, TOK]),
            ("dbg_h", [D, TOK]), ("dbg_ln2", [D, TOK]), ("dbg_cw", [15, TOK]),
            ("dbg_acc", [D, TOK]), ("dbg_csum", [128, 128]), ("dbg_cvec", [128, 8]),
            ("dbg_car", [128, 16]),
        ]:
            dbg[name] = nc.dram_tensor(name, shape, F32, kind="ExternalOutput").ap()

    def cpview(ap):
        return ap.rearrange("(c p) n -> c p n", p=128)

    xcv, hspv, aspv, x1v, outv = map(cpview, (xc, hsp, asp, x1sp, out_fm))

    p = Prog(nc)

    with contextlib.ExitStack() as ctx:
        for s in ("sx", "ssp", "sw", "scc", "smm", "sdv", "sac", "sgp"):
            p.add_sem(ctx, s)

        sb = nc.alloc_sbuf_tensor
        identt = sb("identt", [128, 128], F32)
        onesk1t = sb("onesk1t", [1, 128], F32)
        onesm1t = sb("onesm1t", [128, 1], F32)
        wift = sb("wift", [128, 24], F32)
        sbvt = sb("sbvt", [128, 16], F32)
        eb1t = sb("eb1t", [128, 120], F32)
        rwt = sb("rwt", [128, 120], F32)
        rwbt = sb("rwbt", [128, 120], BF16)
        rbt = sb("rbt", [15, 1], F32)
        cmt = sb("cmt", [128, 8], F32)
        cm1t = sb("cm1t", [128, 8], F32)
        cm64t = sb("cm64t", [128, 64], F32)
        carA = sb("carA", [128, 8], F32)
        carH = sb("carH", [128, 8], F32)
        cvec = sb("cvec", [128, 8], F32)
        csumA = sb("csumA", [128, 64], F32)
        csumB = sb("csumB", [128, 64], F32)
        lgb = sb("lgb", [128, 352], F32)
        cwfb = sb("cwfb", [15, TOK], BF16)
        lgsb = sb("lgsb", [15, 512], F32)
        ln2b = sb("ln2b", [128, DC * TOK], BF16)

        pb = [ctx.enter_context(nc.psum_tensor(f"pb{i}", [128, 512], F32))
              for i in range(8)]

        def d(fn):
            return fn

        consts = [
            (identt, identd), (onesk1t, onesk1d), (onesm1t, onesm1d),
            (wift, wifd), (sbvt, sbvd), (eb1t, eb1d), (rwt, rwd), (rbt, rbd),
            (cmt, cmaskd), (cm1t, cmask1d), (cm64t, cm64d),
        ]
        for tt, dd_ in consts:
            p.emit("sync", (lambda t=tt, dd=dd_: lambda eng: eng.dma_start(out=t[:], in_=dd))(),
                   inc=("sx", 16), tag="const")
        consts_loaded = p.val("sx")

        rwcast = p.emit("vector", lambda eng: eng.tensor_copy(rwbt[:], rwt[:]),
                        waits=[("sx", consts_loaded)], inc=("sdv", 1), tag="rwcast")
        p.emit("vector", lambda eng: eng.memset(carA[:], 1.0), tag="carA0")
        p.emit("vector", lambda eng: eng.memset(carH[:], 0.0), inc=("sdv", 1), tag="carH0")
        carinit = p.val("sdv")

        with contextlib.ExitStack() as sctx:
            srow = sctx.enter_context(nc.sbuf_tensor("srow", [1, 2048], F32))
            xb = sctx.enter_context(nc.sbuf_tensor("xb", [128, 3 * 2048], F32))
            ln1 = sctx.enter_context(nc.sbuf_tensor("ln1", [128, 2 * 2048], F32))
            t1 = sctx.enter_context(nc.sbuf_tensor("t1", [128, 2 * 2048], F32))
            t2 = sctx.enter_context(nc.sbuf_tensor("t2", [128, 2 * 2048], F32))
            t3 = sctx.enter_context(nc.sbuf_tensor("t3", [128, 2 * 2048], F32))
            t4 = sctx.enter_context(nc.sbuf_tensor("t4", [128, 2048], F32))
            hb = sctx.enter_context(nc.sbuf_tensor("hb", [128, 2048], F32))
            ab = sctx.enter_context(nc.sbuf_tensor("ab", [128, 2 * 2048], F32))
            xsq = sctx.enter_context(nc.sbuf_tensor("xsq", [128, 2048], F32))
            wtl = sctx.enter_context(nc.sbuf_tensor("wtl", [128, 3 * 2048], F32))

            p.emit("vector", lambda eng: eng.memset(pb[4][:], 0.0), tag="zt0")
            for wsel in range(3):
                for dc in range(DC):
                    def mkw(wsel=wsel, dc=dc):
                        def fn(eng):
                            return eng.tensor_scalar(
                                wtl[:, wsel * 2048 + dc * TS:(wsel * 2048 + (dc + 1) * TS)],
                                pb[4][:, 0:TS], 0.0, wift[:, wsel * 8 + dc:wsel * 8 + dc + 1],
                                OP.mult, OP.add)
                        return fn
                    lastw = p.emit("vector", mkw(),
                                   waits=[("sx", consts_loaded)], inc=("sdv", 1), tag=f"wtl{wsel}_{dc}")
            wtl_done = p.val("sdv")

            ev_lnapply = [0] * NTILE
            ev_spill_h = [0] * NTILE
            ev_spill_a = [0] * NTILE
            ev_scan = [0] * NTILE
            ev_xload = [0] * NTILE
            ev_sq_used = [0] * NTILE
            ev_rows_dve = [0] * NTILE

            ev_bcast = [0] * NTILE
            def pre_sq_stats(k):
                xsl = xb[:, (k % 3) * 2048:((k % 3) + 1) * 2048]
                xqsl = xsq[:, 0:2048]
                waits = [("sx", lambda k=k: ev_xload[k])]
                if k >= 1:
                    waits.append(("smm", ev_sq_used[k - 1]))
                p.emit("scalar", (lambda x_=xsl, q_=xqsl: lambda eng: eng.activation(
                    q_, x_, AF.Square, bias=0.0, scale=1.0))(),
                    waits=waits, inc=("sac", 1), tag=f"sq{k}")
                sq_done = p.val("sac")

                waits = [("sac", sq_done)]
                if k >= 1:
                    waits.append(("sdv", ev_rows_dve[k - 1]))
                for dc in range(DC):
                    def mks(dc=dc, x_=xsl):
                        def fn(eng):
                            return eng.matmul(pb[0][0:1, 0:TS], onesm1t[:],
                                              x_[:, dc * TS:(dc + 1) * TS],
                                              start=(dc == 0), stop=(dc == DC - 1))
                        return fn
                    p.emit("tensor", mks(), waits=waits if dc == 0 else [], tag=f"summ{k}_{dc}")
                for dc in range(DC):
                    def mks2(dc=dc, q_=xqsl):
                        def fn(eng):
                            return eng.matmul(pb[1][0:1, 0:TS], onesm1t[:],
                                              q_[:, dc * TS:(dc + 1) * TS],
                                              start=(dc == 0), stop=(dc == DC - 1))
                        return fn
                    inc = ("smm", 1) if dc == DC - 1 else None
                    p.emit("tensor", mks2(), inc=inc, tag=f"sumsq{k}_{dc}")
                ev_sq_used[k] = p.val("smm")

            def pre_rows_bcast(k):
                par = k % 2
                mb = pb[2] if par == 0 else pb[5]
                rb = pb[3] if par == 0 else pb[6]
                base = par * 1024
                mrow = srow[0:1, base:base + TS]
                arow = srow[0:1, base + 256:base + 256 + TS]
                trow = srow[0:1, base + 512:base + 512 + TS]
                rrow = srow[0:1, base + 768:base + 768 + TS]

                waits = [("smm", ev_sq_used[k])]
                if k >= 2:
                    waits.append(("smm", ev_bcast[k - 2]))
                p.emit("vector", (lambda mr=mrow: lambda eng: eng.tensor_single_scalar(
                    mr, pb[0][0:1, 0:TS], 1.0 / D, OP.mult))(),
                    waits=waits, tag=f"rowm{k}")
                p.emit("vector", (lambda ar=arow: lambda eng: eng.tensor_scalar(
                    ar, pb[1][0:1, 0:TS], 1.0 / D, EPS, OP.mult, OP.add))(), tag=f"rowa{k}")
                p.emit("vector", (lambda mr=mrow, tr=trow: lambda eng: eng.tensor_tensor(
                    tr, mr, mr, OP.mult))(), tag=f"rowt{k}")
                ev_rows_dve[k] = p.emit("vector", (lambda ar=arow, tr=trow: lambda eng: eng.tensor_tensor(
                    ar, ar, tr, OP.subtract))(), inc=("sdv", 1), tag=f"rowb{k}")
                p.emit("scalar", (lambda ar=arow, tr=trow: lambda eng: eng.activation(
                    tr, ar, AF.Ln, bias=0.0, scale=1.0))(),
                    waits=[("sdv", ev_rows_dve[k])], tag=f"rowln{k}")
                rowsr = p.emit("scalar", (lambda rr=rrow, tr=trow: lambda eng: eng.activation(
                    rr, tr, AF.Exp, bias=0.0, scale=-0.5))(), inc=("sac", 1), tag=f"rowr{k}")

                waits = [("sac", rowsr)]
                if k >= 2:
                    waits.append(("sdv", ev_lnapply[k - 2]))
                p.emit("tensor", (lambda mr=mrow, mb=mb: lambda eng: eng.matmul(
                    mb[:, 0:TS], onesk1t[:], mr, start=True, stop=True))(),
                    waits=waits, tag=f"bcm{k}")
                ev_bcast[k] = p.emit("tensor", (lambda rr=rrow, rb=rb: lambda eng: eng.matmul(
                    rb[:, 0:TS], onesk1t[:], rr, start=True, stop=True))(),
                    inc=("smm", 1), tag=f"bcr{k}")

            def pre_load(k):
                xsl = xb[:, (k % 3) * 2048:((k % 3) + 1) * 2048]
                waits = []
                if k >= 3:
                    waits.append(("ssp", ev_spill_h[k - 3]))
                src = xcv[:, :, k * TS:(k + 1) * TS].transpose([1, 0, 2])
                dst = xsl.rearrange("p (c n) -> p c n", c=DC)
                ev_xload[k] = p.emit(
                    "sync", (lambda s_=src, d_=dst: lambda eng: eng.dma_start(out=d_, in_=s_))(),
                    waits=waits, inc=("sx", 16), tag=f"xload{k}")

            def mainpass(k):
                par = k % 2
                xsl = xb[:, (k % 3) * 2048:((k % 3) + 1) * 2048]
                l1sl = ln1[:, par * 2048:(par + 1) * 2048]
                t1sl = t1[:, par * 2048:(par + 1) * 2048]
                fsl = t2[:, par * 2048:(par + 1) * 2048]
                t3sl = t3[:, par * 2048:(par + 1) * 2048]
                vsl = t4[:, 0:2048]
                hsl = hb[:, 0:2048]
                asl = ab[:, par * 2048:(par + 1) * 2048]
                mb = pb[2] if par == 0 else pb[5]
                rb = pb[3] if par == 0 else pb[6]
                H = 1024

                def hsub(ap_, h):
                    return ap_[:, h * H:(h + 1) * H]

                for dc in range(DC):
                    def mka(dc=dc, x_=xsl, l_=l1sl, mb=mb):
                        def fn(eng):
                            return eng.tensor_tensor(l_[:, dc * TS:(dc + 1) * TS],
                                                     x_[:, dc * TS:(dc + 1) * TS],
                                                     mb[:, 0:TS], OP.subtract)
                        return fn
                    p.emit("vector", mka(), waits=[("smm", ev_bcast[k])] if dc == 0 else [],
                           tag=f"lna{k}_{dc}")
                lnh = [0, 0]
                for dc in range(DC):
                    def mkb(dc=dc, l_=l1sl, rb=rb):
                        def fn(eng):
                            sl = l_[:, dc * TS:(dc + 1) * TS]
                            return eng.tensor_tensor(sl, sl, rb[:, 0:TS], OP.mult)
                        return fn
                    inc = ("sdv", 1) if dc in (3, 7) else None
                    p.emit("vector", mkb(), inc=inc, tag=f"lnb{k}_{dc}")
                    if dc == 3:
                        lnh[0] = p.val("sdv")
                ev_lnapply[k] = lnh[1] = p.val("sdv")

                if DEBUG:
                    dmaf = (lambda k=k, l_=l1sl: lambda eng: eng.dma_start(
                        out=cpview(dbg["dbg_ln1"])[:, :, k * TS:(k + 1) * TS].transpose([1, 0, 2]),
                        in_=l_.rearrange("p (c n) -> p c n", c=DC)))()
                    gv = p.emit("sync", dmaf, waits=[("sdv", ev_lnapply[k])], inc=("ssp", 16),
                                tag=f"dbgln1_{k}")
                    p.set_gate(("ssp", gv))

                gfg = [0, 0]
                for h in range(2):
                    gw = [("sdv", max(lnh[h], wtl_done))]
                    p.emit("gpsimd", (lambda l_=l1sl, o_=t1sl, h=h: lambda eng: eng.tensor_tensor(
                        hsub(o_, h), hsub(l_, h), wtl[:, h * H:(h + 1) * H], OP.mult))(),
                        waits=gw, tag=f"fg{k}_{h}")
                    gfg[h] = p.emit("gpsimd", (lambda l_=l1sl, o_=t3sl, h=h: lambda eng: eng.tensor_tensor(
                        hsub(o_, h), hsub(l_, h), wtl[:, 2048 + h * H:2048 + (h + 1) * H], OP.mult))(),
                        inc=("sgp", 1), tag=f"ig{k}_{h}")

                spd = [0, 0]
                for h in range(2):
                    p.emit("scalar", (lambda o_=t1sl, h=h: lambda eng: eng.activation(
                        hsub(o_, h), hsub(o_, h), AF.Exp, bias=0.0, scale=-1.0))(),
                        waits=[("sgp", gfg[h])], tag=f"e1{k}_{h}")
                    p.emit("scalar", (lambda o_=t3sl, h=h: lambda eng: eng.activation(
                        hsub(o_, h), hsub(o_, h), AF.Exp, bias=0.0, scale=-1.0))(), tag=f"e2{k}_{h}")
                    p.emit("scalar", (lambda o_=t1sl, h=h: lambda eng: eng.activation(
                        hsub(o_, h), hsub(o_, h), AF.Ln, bias=1.0, scale=1.0))(), tag=f"l1{k}_{h}")
                    spd[h] = p.emit("scalar", (lambda o_=t3sl, h=h: lambda eng: eng.activation(
                        hsub(o_, h), hsub(o_, h), AF.Ln, bias=1.0, scale=1.0))(),
                        inc=("sac", 1), tag=f"l2{k}_{h}")

                gdiff = [0, 0]
                hgd = [0, 0]
                for h in range(2):
                    gdiff[h] = p.emit("gpsimd", (lambda a_=t1sl, b_=t3sl, h=h: lambda eng:
                                      eng.tensor_tensor(hsub(a_, h), hsub(a_, h), hsub(b_, h),
                                                        OP.subtract))(),
                                      waits=[("sac", spd[h])], inc=("sgp", 1), tag=f"diff{k}_{h}")
                for h in range(2):
                    hgd[h] = p.emit("vector", (lambda l_=l1sl, h=h: lambda eng: eng.tensor_tensor(
                        hsub(l_, h), hsub(l_, h), wtl[:, 4096 + h * H:4096 + (h + 1) * H], OP.mult))(),
                        waits=[("sgp", gfg[h])], inc=("sdv", 1), tag=f"hg{k}_{h}")

                sgd = [0, 0]
                for h in range(2):
                    p.emit("scalar", (lambda f_=fsl, a_=t1sl, h=h: lambda eng: eng.activation(
                        hsub(f_, h), hsub(a_, h), AF.Sigmoid, bias=0.0, scale=-1.0))(),
                        waits=[("sdv", hgd[h]), ("sgp", gdiff[h])], tag=f"sf{k}_{h}")
                    sgd[h] = p.emit("scalar", (lambda l_=l1sl, b_=t3sl, h=h: lambda eng: eng.activation(
                        hsub(b_, h), hsub(l_, h), AF.Sigmoid, bias=0.0, scale=1.0))(),
                        inc=("sac", 1), tag=f"sh{k}_{h}")

                iv = [0, 0]
                vv = [0, 0]
                for h in range(2):
                    p.emit("vector", (lambda l_=l1sl, h=h: lambda eng: eng.tensor_single_scalar(
                        hsub(l_, h), hsub(l_, h), 0.5, OP.add))(),
                        waits=[("sac", sgd[h])], tag=f"h05{k}_{h}")
                    p.emit("vector", (lambda l_=l1sl, b_=t3sl, h=h: lambda eng: eng.tensor_tensor(
                        hsub(b_, h), hsub(l_, h), hsub(b_, h), OP.max))(), tag=f"g{k}_{h}")
                    iv[h] = p.emit("vector", (lambda f_=fsl, a_=t1sl, h=h: lambda eng: eng.tensor_scalar(
                        hsub(a_, h), hsub(f_, h), -1.0, 1.0, OP.mult, OP.add))(),
                        inc=("sdv", 1), tag=f"i{k}_{h}")
                for h in range(2):
                    vv[h] = p.emit("vector", (lambda v_=vsl, a_=t1sl, b_=t3sl, h=h: lambda eng:
                                   eng.tensor_tensor(hsub(v_, h), hsub(a_, h), hsub(b_, h),
                                                     OP.mult))(),
                                   inc=("sdv", 1), tag=f"v{k}_{h}")

                for h in range(2):
                    waits = [("sdv", carinit)]
                    if k >= 2 and h == 0:
                        waits += [("sw", ev_spill_a[k - 2])]
                    first = True
                    for dc in range(4 * h, 4 * h + 4):
                        def mksa(dc=dc, f_=fsl, a_=asl):
                            def fn(eng):
                                return eng.tensor_tensor_scan(
                                    a_[:, dc * TS:(dc + 1) * TS], f_[:, dc * TS:(dc + 1) * TS],
                                    pb[4][:, 0:TS], carA[:, dc:dc + 1], OP.mult, OP.add)
                            return fn
                        p.emit("vector", mksa(), waits=waits if first else [], tag=f"sa{k}_{dc}")
                        first = False
                    for dc in range(4 * h, 4 * h + 4):
                        def mksh(dc=dc, f_=fsl, v_=vsl, h_=hsl):
                            def fn(eng):
                                return eng.tensor_tensor_scan(
                                    h_[:, dc * TS:(dc + 1) * TS], f_[:, dc * TS:(dc + 1) * TS],
                                    v_[:, dc * TS:(dc + 1) * TS], carH[:, dc:dc + 1], OP.mult, OP.add)
                            return fn
                        p.emit("vector", mksh(), tag=f"shs{k}_{dc}")
                p.emit("vector", lambda eng, a_=asl: eng.tensor_copy(
                    carA[:], a_.rearrange("p (c n) -> p c n", c=DC)[:, :, TS - 1]), tag=f"ca{k}")
                p.emit("vector", lambda eng, h_=hsl: eng.tensor_copy(
                    carH[:], h_.rearrange("p (c n) -> p c n", c=DC)[:, :, TS - 1]), tag=f"ch{k}")
                ev_scan[k] = p.emit("vector", (lambda x_=xsl, h_=hsl: lambda eng: eng.tensor_tensor(
                    x_, x_, h_, OP.add))(), inc=("sdv", 1), tag=f"x1p{k}")

                hdst = hspv[:, :, k * TS:(k + 1) * TS].transpose([1, 0, 2])
                ev_spill_h[k] = p.emit(
                    "sync", (lambda d_=hdst, s_=xsl: lambda eng: eng.dma_start(
                        out=d_, in_=s_.rearrange("p (c n) -> p c n", c=DC)))(),
                    waits=[("sdv", ev_scan[k])], inc=("ssp", 16), tag=f"hspill{k}")
                adst = aspv[:, :, k * TS:(k + 1) * TS].transpose([1, 0, 2])
                ev_spill_a[k] = p.emit(
                    "gpsimd", (lambda d_=adst, s_=asl: lambda eng: eng.dma_start(
                        out=d_, in_=s_.rearrange("p (c n) -> p c n", c=DC)))(),
                    waits=[("sdv", ev_scan[k])], inc=("sw", 16), tag=f"aspill{k}")

            for k in range(NTILE + 2):
                if k < NTILE:
                    pre_load(k)
                    pre_sq_stats(k)
                if k >= 2:
                    mainpass(k - 2)
                if k < NTILE:
                    pre_rows_bcast(k)

            agin_vA = agin[0:1, :].rearrange("o (d q) -> (o q) d", q=128)
            agin_vB = agin[1:2, :].rearrange("o (d q) -> (o q) d", q=128)
            sA = p.emit("sync", lambda eng: eng.dma_start(out=agin_vA, in_=carA[:],
                                                          allow_slow_non_contiguous=True),
                        waits=[("sdv", ev_scan[NTILE - 1])], inc=("ssp", 16), tag="sumA")
            sB = p.emit("sync", lambda eng: eng.dma_start(out=agin_vB, in_=carH[:],
                                                          allow_slow_non_contiguous=True),
                        inc=("ssp", 16), tag="sumB")
            cc = p.emit("gpsimd", lambda eng: eng.collective_compute(
                "AllGather", OP.bypass, ins=[agin], outs=[agout],
                replica_groups=[[0, 1, 2, 3, 4, 5, 6, 7]]),
                waits=[("ssp", sB)], inc=("scc", 1), tag="allgather")
            agv = agout.rearrange("(j a) (d q) -> j a d q", a=2, q=128)
            gB = 0
            for j in range(8):
                for ab, dst in ((0, csumA), (1, csumB)):
                    gB = p.emit("scalar", (lambda j=j, ab=ab, dst=dst: lambda eng: eng.dma_start(
                        out=dst[:, j * 8:(j + 1) * 8],
                        in_=agv[j, ab, :, :].transpose([1, 0]),
                        allow_slow_non_contiguous=True))(),
                        waits=[("scc", cc)] if (j == 0 and ab == 0) else [],
                        inc=("sx", 16), tag=f"gath{ab}_{j}")

            if DEBUG:
                p.emit("sync", lambda eng: eng.dma_start(out=dbg["dbg_car"][:, 0:8], in_=carA[:]),
                       waits=[("sdv", ev_scan[NTILE - 1])], inc=("ssp", 16), tag="dbgcarA")
                p.emit("sync", lambda eng: eng.dma_start(out=dbg["dbg_car"][:, 8:16], in_=carH[:]),
                       inc=("ssp", 16), tag="dbgcarB")
                p.emit("sync", lambda eng: eng.dma_start(out=dbg["dbg_csum"][:, 0:64], in_=csumA[:]),
                       waits=[("sx", gB)], inc=("ssp", 16), tag="dbgcsA")
                gv = p.emit("sync", lambda eng: eng.dma_start(out=dbg["dbg_csum"][:, 64:128], in_=csumB[:]),
                            inc=("ssp", 16), tag="dbgcsB")
                p.set_gate(("ssp", gv))
            p.emit("vector", lambda eng: eng.memset(cvec[:], 0.0), waits=[("sx", gB)], tag="cv0")
            p.emit("vector", lambda eng: eng.tensor_single_scalar(
                lgb[:, 96:160], csumA[:], -1.0, OP.add), tag="cAm1")
            p.emit("vector", lambda eng: eng.tensor_tensor(
                lgb[:, 96:160], lgb[:, 96:160], cm64t[:], OP.mult), tag="cAm2")
            p.emit("vector", lambda eng: eng.tensor_tensor(
                lgb[:, 160:224], csumB[:], cm64t[:], OP.mult), tag="cBm")
            p.emit("vector", lambda eng: eng.drain(), tag="drn")
            for j in range(8):
                p.emit("vector", (lambda j=j: lambda eng: eng.tensor_tensor(
                    lgb[:, 88:96], cvec[:], lgb[:, 96 + j * 8:104 + j * 8], OP.mult))(),
                    tag=f"cm{j}")
                p.emit("vector", lambda eng: eng.drain(), tag="drn")
                p.emit("vector", lambda eng: eng.tensor_tensor(
                    cvec[:], cvec[:], lgb[:, 88:96], OP.add), tag=f"cu{j}")
                p.emit("vector", lambda eng: eng.drain(), tag="drn")
                p.emit("vector", (lambda j=j: lambda eng: eng.tensor_tensor(
                    cvec[:], cvec[:], lgb[:, 160 + j * 8:168 + j * 8], OP.add))(),
                    tag=f"cp{j}")
                p.emit("vector", lambda eng: eng.drain(), tag="drn")
            cdone = p.emit("vector", lambda eng: eng.tensor_copy(lgb[:, 0:8], cvec[:]),
                           inc=("sdv", 1), tag="cdone")
            if DEBUG:
                gv = p.emit("sync", lambda eng: eng.dma_start(out=dbg["dbg_cvec"], in_=cvec[:]),
                            waits=[("sdv", cdone)], inc=("ssp", 16), tag="dbgcvec")
                p.set_gate(("ssp", gv))
            spills_all = p.val("ssp")
            aspills_all = p.val("sw")

        with contextlib.ExitStack() as mctx:
            srow = mctx.enter_context(nc.sbuf_tensor("srow2", [1, 2048], F32))
            xm = mctx.enter_context(nc.sbuf_tensor("xm", [128, 2 * 4096], F32))
            am = mctx.enter_context(nc.sbuf_tensor("am", [128, 2 * 4096], BF16))
            xsqm = mctx.enter_context(nc.sbuf_tensor("xsqm", [128, 4096], BF16))
            x1bf = mctx.enter_context(nc.sbuf_tensor("x1bf", [128, 4096], BF16))
            onesmb = mctx.enter_context(nc.sbuf_tensor("onesmb", [128, 1], BF16))

            ev_ln2 = [0] * NMT
            ev_sqm_used = [0] * NMT
            ev_rows2 = [0] * NMT
            ev_x1sp = [0] * NMT
            cwstate = {"lgevac": 0, "cwout": 0}

            def emit_cw(j4):
                for dc in range(DC):
                    def mkl(dc=dc, j4=j4):
                        def fn(eng):
                            return eng.matmul(pb[6][0:15, :],
                                              rwbt[:, dc * 15:(dc + 1) * 15],
                                              ln2b[:, dc * TOK + j4 * NSUB:dc * TOK + (j4 + 1) * NSUB],
                                              start=(dc == 0), stop=(dc == DC - 1))
                        return fn
                    w = []
                    if dc == 0:
                        w = [("sdv", max(ev_ln2[j4], rwcast, cwstate["cwout"]))]
                        if j4 > 0:
                            w.append(("sac", cwstate["lgevac"]))
                    inc = ("smm", 1) if dc == DC - 1 else None
                    p.emit("tensor", mkl(), waits=w, inc=inc, tag=f"lg{j4}_{dc}")
                lgmm = p.val("smm")
                lgevac = p.emit("vector", lambda eng: eng.tensor_scalar_add(
                    lgsb[:], pb[6][0:15, :], rbt[:, 0:1]),
                    waits=[("smm", lgmm)], inc=("sac", 1), tag=f"lgev{j4}")
                cwstate["lgevac"] = lgevac

                w = [("sac", lgevac)]
                if j4 > 0:
                    w.append(("sdv", cwstate["cwout"]))
                tf = 0
                for g2 in range(4):
                    tf = p.emit("tensor", (lambda g2=g2: lambda eng: eng.transpose(
                        pb[7][:, g2 * 15:(g2 + 1) * 15], lgsb[:, g2 * 128:(g2 + 1) * 128],
                        identt[0:15, 0:15]))(),
                        waits=w if g2 == 0 else [], inc=("smm", 1) if g2 == 3 else None,
                        tag=f"trf{j4}_{g2}")
                def rounds1(g2):
                    L = g2 * 88
                    yield lambda eng: eng.tensor_copy(lgb[:, L:L + 15], pb[7][:, g2 * 15:(g2 + 1) * 15])
                    yield lambda eng: eng.reduce_max(lgb[:, L + 15:L + 16], lgb[:, L:L + 15],
                                                     axis=AX, negate=True)
                    yield lambda eng: eng.tensor_single_scalar(lgb[:, L + 16:L + 17],
                                                               lgb[:, L + 15:L + 16], -1.0, OP.mult)
                    yield lambda eng: eng.tensor_scalar(lgb[:, L + 21:L + 36], lgb[:, L:L + 15],
                                                        lgb[:, L + 16:L + 17], -1e30,
                                                        OP.is_equal, OP.mult)
                    yield lambda eng: eng.tensor_tensor(lgb[:, L + 36:L + 51], lgb[:, L:L + 15],
                                                        lgb[:, L + 21:L + 36], OP.add)
                    yield lambda eng: eng.reduce_max(lgb[:, L + 17:L + 18], lgb[:, L + 36:L + 51],
                                                     axis=AX, negate=True)
                    yield lambda eng: eng.tensor_single_scalar(lgb[:, L + 18:L + 19],
                                                               lgb[:, L + 17:L + 18], -1.0, OP.mult)
                    yield lambda eng: eng.tensor_scalar(lgb[:, L + 21:L + 36], lgb[:, L + 36:L + 51],
                                                        lgb[:, L + 18:L + 19], -1e30,
                                                        OP.is_equal, OP.mult)
                    yield lambda eng: eng.tensor_tensor(lgb[:, L + 36:L + 51], lgb[:, L + 36:L + 51],
                                                        lgb[:, L + 21:L + 36], OP.add)
                    yield lambda eng: eng.reduce_max(lgb[:, L + 17:L + 18], lgb[:, L + 36:L + 51],
                                                     axis=AX, negate=True)
                    yield lambda eng: eng.tensor_single_scalar(lgb[:, L + 18:L + 19],
                                                               lgb[:, L + 17:L + 18], -1.0, OP.mult)
                    yield lambda eng: eng.tensor_single_scalar(lgb[:, L + 21:L + 36], lgb[:, L:L + 15],
                                                               lgb[:, L + 18:L + 19], OP.is_ge)

                gens = [rounds1(g2) for g2 in range(4)]
                for r in range(12):
                    for g2 in range(4):
                        fn = next(gens[g2])
                        inc = ("sdv", 1) if (r == 11 and g2 == 3) else None
                        p.emit("vector", fn,
                               waits=[("smm", tf)] if (r == 0 and g2 == 0) else [],
                               inc=inc, tag=f"cwr1_{j4}_{r}_{g2}")
                r1d = p.val("sdv")
                exd = 0
                for g2 in range(4):
                    L = g2 * 88
                    exd = p.emit("scalar", (lambda L=L: lambda eng: eng.activation(
                        lgb[:, L + 51:L + 66], lgb[:, L:L + 15], AF.Exp,
                        bias=lgb[:, L + 15:L + 16], scale=1.0))(),
                        waits=[("sdv", r1d)] if g2 == 0 else [],
                        inc=("sac", 1) if g2 == 3 else None, tag=f"cwex{j4}_{g2}")

                def rounds2(g2):
                    L = g2 * 88
                    yield lambda eng: eng.reduce_sum(lgb[:, L + 19:L + 20], lgb[:, L + 51:L + 66], axis=AX)
                    yield lambda eng: eng.reciprocal(lgb[:, L + 20:L + 21], lgb[:, L + 19:L + 20])
                    yield lambda eng: eng.tensor_single_scalar(lgb[:, L + 51:L + 66],
                                                               lgb[:, L + 51:L + 66],
                                                               lgb[:, L + 20:L + 21], OP.mult)
                    yield lambda eng: eng.tensor_tensor(lgb[:, L + 66:L + 81], lgb[:, L + 51:L + 66],
                                                        lgb[:, L + 21:L + 36], OP.mult)

                gens = [rounds2(g2) for g2 in range(4)]
                for r in range(4):
                    for g2 in range(4):
                        fn = next(gens[g2])
                        inc = ("sdv", 1) if (r == 3 and g2 == 3) else None
                        p.emit("vector", fn, waits=[("sac", exd)] if (r == 0 and g2 == 0) else [],
                               inc=inc, tag=f"cwr2_{j4}_{r}_{g2}")
                r2d = p.val("sdv")
                tb = 0
                for g2 in range(4):
                    L = g2 * 88
                    tb = p.emit("tensor", (lambda L=L, g2=g2: lambda eng: eng.transpose(
                        pb[6][0:15, g2 * 128:(g2 + 1) * 128], lgb[:, L + 66:L + 81], identt[:]))(),
                        waits=[("sdv", r2d)] if g2 == 0 else [],
                        inc=("smm", 1) if g2 == 3 else None, tag=f"trb{j4}_{g2}")
                cwstate["cwout"] = p.emit("vector", (lambda j4=j4: lambda eng: eng.tensor_copy(
                    cwfb[0:15, j4 * 512:(j4 + 1) * 512], pb[6][0:15, 0:512]))(),
                    waits=[("smm", tb)], inc=("sdv", 1), tag=f"cwout{j4}")

            for m in range(NMT):
                par = m % 2
                xsl = xm[:, par * 4096:(par + 1) * 4096]
                asl = am[:, par * 4096:(par + 1) * 4096]
                MS = MSUB

                waits = [("ssp", spills_all), ("sw", aspills_all)]
                if m >= 2:
                    waits.append(("sdv", ev_ln2[m - 2]))
                    waits.append(("ssp", ev_x1sp[m - 2]))
                xld = p.emit("sync", (lambda m=m, d_=xsl: lambda eng: eng.dma_start(
                    out=d_.rearrange("p (c n) -> p c n", c=DC),
                    in_=hspv[:, :, m * MSUB:(m + 1) * MSUB].transpose([1, 0, 2])))(),
                    waits=waits, inc=("sx", 16), tag=f"mx{m}")
                ald = p.emit("sync", (lambda m=m, d_=asl: lambda eng: eng.dma_start(
                    out=d_.rearrange("p (c n) -> p c n", c=DC),
                    in_=aspv[:, :, m * MSUB:(m + 1) * MSUB].transpose([1, 0, 2])))(),
                    inc=("sx", 16), tag=f"ma{m}")

                for dc in range(DC):
                    def mkf(dc=dc, x_=xsl, a_=asl):
                        def fn(eng):
                            sl = slice(dc * MS, (dc + 1) * MS)
                            return eng.scalar_tensor_tensor(x_[:, sl], a_[:, sl], cvec[:, dc:dc + 1],
                                                            x_[:, sl], OP.mult, OP.add)
                        return fn
                    w = [("sx", ald), ("sdv", cdone)] if dc == 0 else []
                    inc = ("sdv", 1) if dc == DC - 1 else None
                    p.emit("vector", mkf(), waits=w, inc=inc, tag=f"fix{m}_{dc}")
                x1d = p.val("sdv")
                if DEBUG:
                    gv = p.emit("sync", (lambda m=m, x_=xsl: lambda eng: eng.dma_start(
                        out=cpview(dbg["dbg_h"])[:, :, m * MSUB:(m + 1) * MSUB].transpose([1, 0, 2]),
                        in_=x_.rearrange("p (c n) -> p c n", c=DC)))(),
                        waits=[("sdv", x1d)], inc=("ssp", 16), tag=f"dbgh{m}")
                    p.set_gate(("ssp", gv))

                ev_x1sp[m] = p.emit("sync", (lambda m=m, s_=xsl: lambda eng: eng.dma_start(
                    out=x1v[:, :, m * MSUB:(m + 1) * MSUB].transpose([1, 0, 2]),
                    in_=s_.rearrange("p (c n) -> p c n", c=DC)))(),
                    waits=[("sdv", x1d)], inc=("ssp", 16), tag=f"x1sp{m}")

                waits = [("sdv", x1d)]
                if m >= 1:
                    waits.append(("smm", ev_sqm_used[m - 1]))
                    waits.append(("sdv", ev_ln2[m - 1]))
                if m == 0:
                    p.emit("vector", lambda eng: eng.tensor_copy(onesmb[:], onesm1t[:]),
                           tag="onesmb")
                bw = [("sdv", x1d)]
                if m >= 1:
                    bw.append(("smm", ev_sqm_used[m - 1]))
                mxbf = p.emit("vector", (lambda x_=xsl: lambda eng: eng.tensor_copy(
                    x1bf[:], x_))(), waits=bw, inc=("sdv", 1), tag=f"mxbf{m}")
                sqm = p.emit("scalar", (lambda x_=xsl: lambda eng: eng.activation(
                    xsqm[:], x_, AF.Square, bias=0.0, scale=1.0))(),
                    waits=waits, inc=("sac", 1), tag=f"msq{m}")
                waits = [("sac", sqm), ("sdv", mxbf)]
                if m >= 1:
                    waits.append(("sdv", ev_rows2[m - 1]))
                for dc in range(DC):
                    def mks(dc=dc):
                        def fn(eng):
                            return eng.matmul(pb[0][0:1, 0:MS], onesmb[:],
                                              x1bf[:, dc * MS:(dc + 1) * MS],
                                              start=(dc == 0), stop=(dc == DC - 1))
                        return fn
                    p.emit("tensor", mks(), waits=waits if dc == 0 else [], tag=f"msum{m}_{dc}")
                for dc in range(DC):
                    def mks2(dc=dc):
                        def fn(eng):
                            return eng.matmul(pb[1][0:1, 0:MS], onesmb[:],
                                              xsqm[:, dc * MS:(dc + 1) * MS],
                                              start=(dc == 0), stop=(dc == DC - 1))
                        return fn
                    inc = ("smm", 1) if dc == DC - 1 else None
                    p.emit("tensor", mks2(), inc=inc, tag=f"msumsq{m}_{dc}")
                ev_sqm_used[m] = p.val("smm")

                mrow = srow[0:1, 0:MS]
                arow = srow[0:1, 512:512 + MS]
                trow = srow[0:1, 1024:1024 + MS]
                rrow = srow[0:1, 1536:1536 + MS]
                p.emit("vector", (lambda mr=mrow: lambda eng: eng.tensor_single_scalar(
                    mr, pb[0][0:1, 0:MS], 1.0 / D, OP.mult))(),
                    waits=[("smm", ev_sqm_used[m])], tag=f"mrowm{m}")
                p.emit("vector", (lambda ar=arow: lambda eng: eng.tensor_scalar(
                    ar, pb[1][0:1, 0:MS], 1.0 / D, EPS, OP.mult, OP.add))(), tag=f"mrowa{m}")
                p.emit("vector", (lambda mr=mrow, tr=trow: lambda eng: eng.tensor_tensor(
                    tr, mr, mr, OP.mult))(), tag=f"mrowt{m}")
                ev_rows2[m] = p.emit("vector", (lambda ar=arow, tr=trow: lambda eng: eng.tensor_tensor(
                    ar, ar, tr, OP.subtract))(), inc=("sdv", 1), tag=f"mrowb{m}")
                p.emit("scalar", (lambda ar=arow, tr=trow: lambda eng: eng.activation(
                    tr, ar, AF.Ln, bias=0.0, scale=1.0))(),
                    waits=[("sdv", ev_rows2[m])], tag=f"mrowln{m}")
                rr2 = p.emit("scalar", (lambda rr=rrow, tr=trow: lambda eng: eng.activation(
                    rr, tr, AF.Exp, bias=0.0, scale=-0.5))(), inc=("sac", 1), tag=f"mrowr{m}")

                waits = [("sac", rr2)]
                if m >= 1:
                    waits.append(("sdv", ev_ln2[m - 1]))
                p.emit("tensor", (lambda mr=mrow: lambda eng: eng.matmul(
                    pb[2][:, 0:MS], onesk1t[:], mr, start=True, stop=True))(),
                    waits=waits, tag=f"mbcm{m}")
                bcs = p.emit("tensor", (lambda rr=rrow: lambda eng: eng.matmul(
                    pb[3][:, 0:MS], onesk1t[:], rr, start=True, stop=True))(),
                    inc=("smm", 1), tag=f"mbcr{m}")

                for dc in range(DC):
                    def mka(dc=dc, x_=xsl):
                        def fn(eng):
                            sl = slice(dc * MS, (dc + 1) * MS)
                            return eng.tensor_tensor(xsqm[:, sl], x_[:, sl], pb[2][:, 0:MS], OP.subtract)
                        return fn
                    p.emit("vector", mka(), waits=[("smm", bcs)] if dc == 0 else [], tag=f"l2a{m}_{dc}")
                for dc in range(DC):
                    def mkb(dc=dc, m=m):
                        def fn(eng):
                            return eng.tensor_tensor(
                                ln2b[:, dc * TOK + m * MSUB:dc * TOK + (m + 1) * MSUB],
                                xsqm[:, dc * MS:(dc + 1) * MS], pb[3][:, 0:MS], OP.mult)
                        return fn
                    inc = ("sdv", 1) if dc == DC - 1 else None
                    p.emit("vector", mkb(), inc=inc, tag=f"l2b{m}_{dc}")
                ev_ln2[m] = p.val("sdv")
                if m < NMT - 1:
                    emit_cw(m)


            ln2_all = p.val("sdv")
            x1_spilled = p.val("ssp")

        with contextlib.ExitStack() as fctx:
            oneht = fctx.enter_context(nc.sbuf_tensor("oneht", [15, 1920], BF16))
            eb2t = fctx.enter_context(nc.sbuf_tensor("eb2t", [15, 1024], BF16))
            acc = fctx.enter_context(nc.sbuf_tensor("acc", [128, DC * TOK], F32))
            h_sb = fctx.enter_context(nc.sbuf_tensor("h_sb", [128, 2 * 4096], BF16))
            rtmp = fctx.enter_context(nc.sbuf_tensor("rtmp", [128, 2 * 512], F32))

            ohl = p.emit("sync", lambda eng: eng.dma_start(out=oneht[:], in_=onehd),
                         waits=[("sdv", ln2_all), ("ssp", x1_spilled)], inc=("sx", 16), tag="ohl")
            eb2l = None


            if DEBUG:
                gv = p.emit("gpsimd", lambda eng: eng.dma_start(out=dbg["dbg_cw"], in_=cwfb[:]),
                            waits=[("sdv", cwstate["cwout"])], inc=("sw", 16), tag="dbgcw")
                p.set_gate(("sw", gv))
                lv = (lambda eng: eng.dma_start(
                    out=cpview(dbg["dbg_ln2"]).transpose([1, 0, 2]),
                    in_=ln2b[:].rearrange("p (c n) -> p c n", c=DC)))
                gv = p.emit("gpsimd", lv, waits=[("sdv", ln2_all)], inc=("sw", 16), tag="dbgln2")
                p.set_gate(("sw", gv))

            with contextlib.ExitStack() as wctx:
                wbuf = [wctx.enter_context(nc.sbuf_tensor(f"wbuf{i}", [128, 2 * 8192], BF16))
                        for i in range(2)]

                def wload(eng_unused, ee, buf):
                    if ee == 0:
                        w1src, w2src = sW1d, sW2d
                    else:
                        w1src = eW1d[ee - 1]
                        w2src = eW2d[ee - 1]
                    v1 = w1src.rearrange("(c p) n -> c p n", p=128).transpose([1, 0, 2])
                    v2 = w2src.rearrange("(c p) n -> c p n", p=128).transpose([1, 0, 2])
                    d1 = wbuf[buf][:, 0:8192].rearrange("p (c n) -> p c n", c=DC)
                    d2 = wbuf[buf][:, 8192:16384].rearrange("p (c n) -> p c n", c=DC)
                    return v1, d1, v2, d2

                ev_wload = [0] * NEXP
                ev_ee_mm_end = [0] * NEXP

                for ee in range(2):
                    v1, d1, v2, d2 = wload(None, ee, ee % 2)
                    w = [("ssp", ev_spill_h[NTILE - 1]), ("sw", ev_spill_a[NTILE - 1])] if ee == 0 else []
                    p.emit("gpsimd", (lambda a=v1, b=d1: lambda eng: eng.dma_start(out=b, in_=a))(),
                           waits=w, inc=("sw", 16), tag=f"w1l{ee}")
                    ev_wload[ee] = p.emit(
                        "gpsimd", (lambda a=v2, b=d2: lambda eng: eng.dma_start(out=b, in_=a))(),
                        inc=("sw", 16), tag=f"w2l{ee}")
                eb2l = p.emit("gpsimd", lambda eng: eng.dma_start(out=eb2t[:], in_=eb2d),
                              waits=[("sdv", ln2_all), ("ssp", x1_spilled)], inc=("sw", 16), tag="eb2l")

                hfree = [0, 0]
                yfree = [0, 0]
                cwbfree = [0, 0]
                rtfree = [0, 0]

                for ee in range(NEXP):
                    wpar = ee % 2
                    hm_done = {}

                    def emit_cwb(j, ee=None):
                        cb = 4 + (j % 2)
                        w = [("sdv", max(cwstate["cwout"], cwbfree[j % 2])), ("sx", ohl)]
                        cwb = p.emit("tensor", (lambda ee=ee, j=j, cb=cb: lambda eng: eng.matmul(
                            pb[cb][:], oneht[:, (ee - 1) * 128:ee * 128],
                            cwfb[0:15, j * NSUB:(j + 1) * NSUB], start=True, stop=True))(),
                            waits=w, inc=("smm", 1), tag=f"cwb{ee}_{j}")
                        return cwb

                    def emit_w1(j, cwb, ee=None):
                        hpar = j % 2
                        cb = 4 + (j % 2)
                        hmul_events = []
                        for hc in range(HCN):
                            bk = hc % 2
                            w = []
                            if ee == 0 and j == 0 and hc == 0:
                                w.append(("sdv", ln2_all))
                                w.append(("ssp", x1_spilled))
                            w.append(("sw", ev_wload[ee]))
                            if hfree[bk]:
                                w.append(("sac", hfree[bk]))
                            for dc in range(DC):
                                def mkw1(ee=ee, j=j, hc=hc, dc=dc, bk=bk, wpar=wpar):
                                    def fn(eng):
                                        return eng.matmul(
                                            pb[bk][:],
                                            wbuf[wpar][:, dc * 1024 + hc * 128:dc * 1024 + (hc + 1) * 128],
                                            ln2b[:, dc * TOK + j * NSUB:dc * TOK + (j + 1) * NSUB],
                                            start=(dc == 0), stop=(dc == DC - 1))
                                    return fn
                                inc = ("smm", 1) if dc == DC - 1 else None
                                p.emit("tensor", mkw1(), waits=w if dc == 0 else [], inc=inc,
                                       tag=f"W1_{ee}_{j}_{hc}_{dc}")
                                w = []
                            w1g = p.val("smm")
                            if ee == 0:
                                hfree[bk] = p.emit("scalar", (lambda hc=hc, bk=bk, hpar=hpar: lambda eng:
                                                   eng.activation(
                                                       h_sb[:, hpar * 4096 + hc * 512:hpar * 4096 + (hc + 1) * 512],
                                                       pb[bk][:], AF.Relu, bias=sbvt[:, hc:hc + 1], scale=1.0))(),
                                                   waits=[("smm", w1g)], inc=("sac", 1), tag=f"relu{ee}_{j}_{hc}")
                                hmul_events.append(("sac", hfree[bk]))
                            else:
                                rslot = hc % 2
                                rw = [("smm", w1g)]
                                if rtfree[rslot]:
                                    rw.append(("sdv", rtfree[rslot]))
                                rel = p.emit("scalar", (lambda ee=ee, hc=hc, bk=bk, rslot=rslot: lambda eng:
                                             eng.activation(
                                                 rtmp[:, rslot * 512:(rslot + 1) * 512], pb[bk][:], AF.Relu,
                                                 bias=eb1t[:, (ee - 1) * 8 + hc:(ee - 1) * 8 + hc + 1],
                                                 scale=1.0))(),
                                             waits=rw, inc=("sac", 1), tag=f"relu{ee}_{j}_{hc}")
                                hfree[bk] = rel
                                hm_ = p.emit("vector", (lambda hc=hc, hpar=hpar, cb=cb, rslot=rslot: lambda eng:
                                             eng.tensor_tensor(
                                                 h_sb[:, hpar * 4096 + hc * 512:hpar * 4096 + (hc + 1) * 512],
                                                 rtmp[:, rslot * 512:(rslot + 1) * 512], pb[cb][:], OP.mult))(),
                                             waits=[("sac", rel), ("smm", cwb)], inc=("sdv", 1),
                                             tag=f"hmul{ee}_{j}_{hc}")
                                rtfree[rslot] = hm_
                                hmul_events.append(("sdv", hm_))
                        if ee >= 1:
                            cwbfree[j % 2] = hmul_events[-1][1]
                        hm_done[j] = hmul_events[-1]

                    def emit_w2(j, ee=None):
                        hpar = j % 2
                        for dc in range(DC):
                            bk = 2 + dc % 2
                            w = []
                            if dc == 0:
                                w.append(hm_done[j])
                            if yfree[dc % 2]:
                                w.append(("sdv", yfree[dc % 2]))
                            for hc in range(HCN):
                                def mkw2(ee=ee, j=j, hc=hc, dc=dc, bk=bk, wpar=wpar, hpar=hpar):
                                    def fn(eng):
                                        return eng.matmul(
                                            pb[bk][:],
                                            wbuf[wpar][:, 8192 + hc * 1024 + dc * 128:8192 + hc * 1024 + (dc + 1) * 128],
                                            h_sb[:, hpar * 4096 + hc * 512:hpar * 4096 + (hc + 1) * 512],
                                            start=(hc == 0), stop=(hc == HCN - 1 and ee != 1))
                                    return fn
                                inc = ("smm", 1) if (hc == HCN - 1 and ee != 1) else None
                                p.emit("tensor", mkw2(), waits=w if hc == 0 else [],
                                       inc=inc, tag=f"W2_{ee}_{j}_{dc}_{hc}")
                                w = []
                            if ee == 1:
                                def mkeb(j=j, dc=dc, bk=bk):
                                    def fn(eng):
                                        return eng.matmul(
                                            pb[bk][:], eb2t[:, dc * 128:(dc + 1) * 128],
                                            cwfb[0:15, j * NSUB:(j + 1) * NSUB],
                                            start=False, stop=True)
                                    return fn
                                p.emit("tensor", mkeb(), waits=[("sw", eb2l)],
                                       inc=("smm", 1), tag=f"eb2_{j}_{dc}")
                            w2g = p.val("smm")
                            if ee == 0:
                                yfree[dc % 2] = p.emit("vector", (lambda j=j, dc=dc, bk=bk: lambda eng:
                                                       eng.tensor_copy(
                                                           acc[:, dc * TOK + j * NSUB:dc * TOK + (j + 1) * NSUB],
                                                           pb[bk][:]))(),
                                                       waits=[("smm", w2g)], inc=("sdv", 1),
                                                       tag=f"acc{ee}_{j}_{dc}")
                            else:
                                yfree[dc % 2] = p.emit("vector", (lambda j=j, dc=dc, bk=bk: lambda eng:
                                                       eng.tensor_tensor(
                                                           acc[:, dc * TOK + j * NSUB:dc * TOK + (j + 1) * NSUB],
                                                           acc[:, dc * TOK + j * NSUB:dc * TOK + (j + 1) * NSUB],
                                                           pb[bk][:], OP.add))(),
                                                       waits=[("smm", w2g)], inc=("sdv", 1),
                                                       tag=f"acc{ee}_{j}_{dc}")

                    cwbs = {}
                    if ee >= 1:
                        cwbs[0] = emit_cwb(0, ee=ee)
                    emit_w1(0, cwbs.get(0), ee=ee)
                    if ee == 0:
                        emit_cw(NMT - 1)
                    if ee >= 1:
                        cwbs[1] = emit_cwb(1, ee=ee)
                    emit_w1(1, cwbs.get(1), ee=ee)
                    emit_w2(0, ee=ee)
                    if ee >= 1:
                        cwbs[2] = emit_cwb(2, ee=ee)
                    emit_w1(2, cwbs.get(2), ee=ee)
                    emit_w2(1, ee=ee)
                    if ee >= 1:
                        cwbs[3] = emit_cwb(3, ee=ee)
                    emit_w1(3, cwbs.get(3), ee=ee)
                    emit_w2(2, ee=ee)
                    emit_w2(3, ee=ee)

                    ev_ee_mm_end[ee] = p.val("smm")
                    if ee + 2 < NEXP:
                        v1, d1, v2, d2 = wload(None, ee + 2, wpar)
                        p.emit("gpsimd", (lambda a=v1, b=d1: lambda eng:
                               eng.dma_start(out=b, in_=a))(),
                               waits=[("smm", ev_ee_mm_end[ee])], inc=("sw", 16), tag=f"w1l{ee + 2}")
                        ev_wload[ee + 2] = p.emit(
                            "gpsimd", (lambda a=v2, b=d2: lambda eng: eng.dma_start(out=b, in_=a))(),
                            inc=("sw", 16), tag=f"w2l{ee + 2}")

                acc_done = p.val("sdv")
                if DEBUG:
                    p.emit("gpsimd", lambda eng: eng.dma_start(
                        out=cpview(dbg["dbg_acc"]).transpose([1, 0, 2]),
                        in_=acc[:].rearrange("p (c n) -> p c n", c=DC)),
                        waits=[("sdv", acc_done)], inc=("sw", 16), tag="dbgacc")

            with contextlib.ExitStack() as ectx:
                xr = ectx.enter_context(nc.sbuf_tensor("xr", [128, 2 * 4096], F32))
                xo = ectx.enter_context(nc.sbuf_tensor("xo", [128, 2 * 4096], F32))
                ev_out = [0] * NJ
                ev_fin = [0] * NJ
                for f in range(NJ):
                    par = f % 2
                    w = [("ssp", x1_spilled), ("smm", ev_ee_mm_end[NEXP - 2])]
                    if f >= 2:
                        w.append(("sdv", ev_fin[f - 2]))
                    xld = p.emit("sync", (lambda f=f, par=par: lambda eng: eng.dma_start(
                        out=xr[:, par * 4096:(par + 1) * 4096].rearrange("p (c n) -> p c n", c=DC),
                        in_=x1v[:, :, f * NSUB:(f + 1) * NSUB].transpose([1, 0, 2])))(),
                        waits=w, inc=("sx", 16), tag=f"xrl{f}")
                    w = [("sx", xld), ("sdv", acc_done)]
                    if f >= 2:
                        w.append(("ssp", ev_out[f - 2]))
                    for dc in range(DC):
                        def mkf(f=f, dc=dc, par=par):
                            def fn(eng):
                                return eng.scalar_tensor_tensor(
                                    xo[:, par * 4096 + dc * NSUB:par * 4096 + (dc + 1) * NSUB],
                                    acc[:, dc * TOK + f * NSUB:dc * TOK + (f + 1) * NSUB],
                                    sbvt[:, 8 + dc:9 + dc],
                                    xr[:, par * 4096 + dc * NSUB:par * 4096 + (dc + 1) * NSUB],
                                    OP.add, OP.add)
                            return fn
                        inc = ("sdv", 1) if dc == DC - 1 else None
                        p.emit("vector", mkf(), waits=w if dc == 0 else [], inc=inc, tag=f"fin{f}_{dc}")
                    ev_fin[f] = p.val("sdv")
                    ev_out[f] = p.emit("sync", (lambda f=f, par=par: lambda eng: eng.dma_start(
                        out=outv[:, :, f * NSUB:(f + 1) * NSUB].transpose([1, 0, 2]),
                        in_=xo[:, par * 4096:(par + 1) * 4096].rearrange("p (c n) -> p c n", c=DC)))(),
                        waits=[("sdv", ev_fin[f])], inc=("ssp", 16), tag=f"outs{f}")

                p.emit("sync", lambda eng: None, waits=[("ssp", p.val("ssp")), ("sw", p.val("sw"))],
                       tag="final_wait")

        p.check_deadlock()
        with nc.Block() as block:
            p.build(block)
    return nc


def kernel(L0_attn_x_t, L0_attn_w_i, L0_attn_w_f, L0_attn_w_h,
           L0_ffn_router_W, L0_ffn_router_b, L0_ffn_sW1, L0_ffn_sb1,
           L0_ffn_sW2, L0_ffn_sb2, L0_ffn_eW1, L0_ffn_eb1, L0_ffn_eW2, L0_ffn_eb2):
    x = np.asarray(L0_attn_x_t, dtype=np.float32)

    def fm(v):
        return np.ascontiguousarray(np.asarray(v, np.float32).reshape(8, 128).T)

    wif = np.concatenate([fm(L0_attn_w_f), fm(L0_attn_w_i), fm(L0_attn_w_h)], axis=1)
    sbv = np.concatenate([fm(L0_ffn_sb1), fm(L0_ffn_sb2)], axis=1)
    eb1p = np.ascontiguousarray(
        np.asarray(L0_ffn_eb1, np.float32).reshape(15, 8, 128).transpose(2, 0, 1).reshape(128, 120))
    eb2p = np.ascontiguousarray(np.asarray(L0_ffn_eb2, np.float32))
    rwp = np.ascontiguousarray(
        np.asarray(L0_ffn_router_W, np.float32).reshape(8, 128, 15).transpose(1, 0, 2).reshape(128, 120))
    rbp = np.ascontiguousarray(np.asarray(L0_ffn_router_b, np.float32).reshape(15, 1))
    sW1 = np.ascontiguousarray(np.asarray(L0_ffn_sW1, np.float32))
    sW2 = np.ascontiguousarray(np.asarray(L0_ffn_sW2, np.float32))
    eW1 = np.ascontiguousarray(np.asarray(L0_ffn_eW1, np.float32))
    eW2 = np.ascontiguousarray(np.asarray(L0_ffn_eW2, np.float32))

    nc = build_nc()
    in_maps = []
    for c in range(8):
        b, pp = c // 4, c % 4
        xcn = np.ascontiguousarray(x[b, pp * TOK:(pp + 1) * TOK, :].T)
        cm = np.zeros((128, 8), np.float32)
        for j in range(8):
            if j // 4 == b and j % 4 < pp:
                cm[:, j] = 1.0
        in_maps.append({
            "xc": xcn, "cmask": cm, "cmask1m": np.ascontiguousarray(1.0 - cm),
            "cm64": np.ascontiguousarray(np.repeat(cm, 8, axis=1)),
            "wif": wif, "sbv": sbv, "eb1p": eb1p, "eb2p": eb2p, "rwp": rwp, "rbp": rbp,
            "sW1": sW1, "sW2": sW2, "eW1": eW1, "eW2": eW2,
        })

    res = run_bass_kernel_spmd(nc, in_maps, core_ids=list(range(8)))
    out = np.empty((B, S, D), np.float32)
    for c in range(8):
        b, pp = c // 4, c % 4
        out[b, pp * TOK:(pp + 1) * TOK, :] = res.results[c]["out_fm"].T
    if DEBUG:
        kernel.last_results = res
    return out


# version 10
# speedup vs baseline: 1.0116x; 1.0032x over previous
import contextlib

import numpy as np

import concourse.bass as bass
import concourse.mybir as mybir
from concourse.bass_utils import run_bass_kernel_spmd

F32 = mybir.dt.float32
BF16 = mybir.dt.bfloat16
OP = mybir.AluOpType
AF = mybir.ActivationFunctionType
AX = mybir.AxisListType.X

B, S, D = 2, 8192, 1024
E, H = 15, 1024
DC = D // 128
HCN = H // 128
TOK = 2048
TS = 256
NTILE = TOK // TS
MSUB = 512
NMT = TOK // MSUB
NSUB = 512
NJ = TOK // NSUB
EPS = 1e-5
NEXP = 16

DEBUG = False


class Prog:

    ENGINES = ("sync", "tensor", "vector", "scalar", "gpsimd")

    def __init__(self, nc):
        self.nc = nc
        self.ops = {e: [] for e in self.ENGINES}
        self.sems = {}
        self.pending_gate = None
        self.gated = set()

    def set_gate(self, gate):
        self.pending_gate = gate
        self.gated = set()

    def add_sem(self, ctx, name):
        h = ctx.enter_context(self.nc.semaphore(name))
        self.sems[name] = [h, 0]

    def emit(self, eng, fn, waits=(), inc=None, tag=""):
        wl = [(s, v) for (s, v) in waits]
        if self.pending_gate is not None and eng not in self.gated:
            wl.append(self.pending_gate)
            self.gated.add(eng)
            if len(self.gated) == len(self.ENGINES):
                self.pending_gate = None
        self.ops[eng].append((fn, wl, inc, tag))
        if inc is not None:
            self.sems[inc[0]][1] += inc[1]
            return self.sems[inc[0]][1]
        return None

    def val(self, name):
        return self.sems[name][1]

    @staticmethod
    def _rv(v):
        return v() if callable(v) else v

    def check_deadlock(self):
        counts = {name: 0 for name in self.sems}
        ptrs = {e: 0 for e in self.ENGINES}
        progress = True
        while progress:
            progress = False
            for e in self.ENGINES:
                while ptrs[e] < len(self.ops[e]):
                    fn, waits, inc, tag = self.ops[e][ptrs[e]]
                    if all(counts[s] >= self._rv(v) for (s, v) in waits):
                        if inc is not None:
                            counts[inc[0]] += inc[1]
                        ptrs[e] += 1
                        progress = True
                    else:
                        break
        stuck = {e: ptrs[e] for e in self.ENGINES if ptrs[e] < len(self.ops[e])}
        if stuck:
            msgs = []
            for e, i in stuck.items():
                fn, waits, inc, tag = self.ops[e][i]
                blocked = [(s, self._rv(v), counts[s]) for (s, v) in waits
                           if counts[s] < self._rv(v)]
                msgs.append(f"{e}[{i}] tag={tag} blocked on {blocked}")
            raise RuntimeError("DEADLOCK in schedule:\n" + "\n".join(msgs))

    def build(self, block):
        def mk(e):
            def body(engobj):
                for fn, waits, inc, tag in self.ops[e]:
                    for (s, v) in waits:
                        engobj.wait_ge(self.sems[s][0], Prog._rv(v))
                    instr = fn(engobj)
                    if inc is not None:
                        assert instr is not None, f"inc without instruction: {e} {tag}"
                        instr.then_inc(self.sems[inc[0]][0], inc[1])
            return body

        block.sync(mk("sync"))
        block.tensor(mk("tensor"))
        block.vector(mk("vector"))
        block.scalar(mk("scalar"))
        block.gpsimd(mk("gpsimd"))


def build_nc():
    nc = bass.Bass("TRN2", target_bir_lowering=False, debug=False)

    xc = nc.dram_tensor("xc", [D, TOK], F32, kind="ExternalInput").ap()
    cmaskd = nc.dram_tensor("cmask", [128, 8], F32, kind="ExternalInput").ap()
    cm64d = nc.dram_tensor("cm64", [128, 64], F32, kind="ExternalInput").ap()
    cmask1d = nc.dram_tensor("cmask1m", [128, 8], F32, kind="ExternalInput").ap()
    wifd = nc.dram_tensor("wif", [128, 24], F32, kind="ExternalInput").ap()
    sbvd = nc.dram_tensor("sbv", [128, 16], F32, kind="ExternalInput").ap()
    eb1d = nc.dram_tensor("eb1p", [128, 120], F32, kind="ExternalInput").ap()
    eb2d = nc.dram_tensor("eb2p", [15, 1024], F32, kind="ExternalInput").ap()
    rwd = nc.dram_tensor("rwp", [128, 120], F32, kind="ExternalInput").ap()
    rbd = nc.dram_tensor("rbp", [15, 1], F32, kind="ExternalInput").ap()
    sW1d = nc.dram_tensor("sW1", [1024, 1024], F32, kind="ExternalInput").ap()
    sW2d = nc.dram_tensor("sW2", [1024, 1024], F32, kind="ExternalInput").ap()
    eW1d = nc.dram_tensor("eW1", [15, 1024, 1024], F32, kind="ExternalInput").ap()
    eW2d = nc.dram_tensor("eW2", [15, 1024, 1024], F32, kind="ExternalInput").ap()

    out_fm = nc.dram_tensor("out_fm", [D, TOK], F32, kind="ExternalOutput").ap()

    x1sp = nc.dram_tensor("x1sp", [D, TOK], F32, kind="Internal").ap()
    hsp = nc.dram_tensor("hsp", [D, TOK], F32, kind="Internal").ap()
    asp = nc.dram_tensor("asp", [D, TOK], BF16, kind="Internal").ap()
    agin = nc.dram_tensor("agin", [2, 1024], F32, kind="Internal").ap()
    agout = nc.dram_tensor("agout", [16, 1024], F32, kind="Internal").ap()

    identd = nc.inline_tensor(np.eye(128, dtype=np.float32), "identc").ap()
    onesk1d = nc.inline_tensor(np.ones((1, 128), dtype=np.float32), "onesk1c").ap()
    onesm1d = nc.inline_tensor(np.ones((128, 1), dtype=np.float32), "onesm1c").ap()
    oneh_np = np.zeros((15, 15 * 128), dtype=np.float32)
    for e in range(15):
        oneh_np[e, e * 128:(e + 1) * 128] = 1.0
    import ml_dtypes
    onehd = nc.inline_tensor(oneh_np.astype(ml_dtypes.bfloat16), "onehc").ap()

    dbg = {}
    if DEBUG:
        for name, shape in [
            ("dbg_ln1", [D, TOK]), ("dbg_f", [D, TOK]), ("dbg_v", [D, TOK]),
            ("dbg_h", [D, TOK]), ("dbg_ln2", [D, TOK]), ("dbg_cw", [15, TOK]),
            ("dbg_acc", [D, TOK]), ("dbg_csum", [128, 128]), ("dbg_cvec", [128, 8]),
            ("dbg_car", [128, 16]),
        ]:
            dbg[name] = nc.dram_tensor(name, shape, F32, kind="ExternalOutput").ap()

    def cpview(ap):
        return ap.rearrange("(c p) n -> c p n", p=128)

    xcv, hspv, aspv, x1v, outv = map(cpview, (xc, hsp, asp, x1sp, out_fm))

    p = Prog(nc)

    with contextlib.ExitStack() as ctx:
        for s in ("sx", "ssp", "sw", "scc", "smm", "sdv", "sac", "sgp"):
            p.add_sem(ctx, s)

        sb = nc.alloc_sbuf_tensor
        identt = sb("identt", [128, 128], F32)
        onesk1t = sb("onesk1t", [1, 128], F32)
        onesm1t = sb("onesm1t", [128, 1], F32)
        wift = sb("wift", [128, 24], F32)
        sbvt = sb("sbvt", [128, 16], F32)
        eb1t = sb("eb1t", [128, 120], F32)
        rwt = sb("rwt", [128, 120], F32)
        rwbt = sb("rwbt", [128, 120], BF16)
        rbt = sb("rbt", [15, 1], F32)
        cmt = sb("cmt", [128, 8], F32)
        cm1t = sb("cm1t", [128, 8], F32)
        cm64t = sb("cm64t", [128, 64], F32)
        carA = sb("carA", [128, 8], F32)
        carH = sb("carH", [128, 8], F32)
        cvec = sb("cvec", [128, 8], F32)
        csumA = sb("csumA", [128, 64], F32)
        csumB = sb("csumB", [128, 64], F32)
        lgb = sb("lgb", [128, 352], F32)
        cwfb = sb("cwfb", [15, TOK], BF16)
        lgsb = sb("lgsb", [15, 512], F32)
        ln2b = sb("ln2b", [128, DC * TOK], BF16)

        pb = [ctx.enter_context(nc.psum_tensor(f"pb{i}", [128, 512], F32))
              for i in range(8)]

        def d(fn):
            return fn

        consts = [
            (identt, identd), (onesk1t, onesk1d), (onesm1t, onesm1d),
            (wift, wifd), (sbvt, sbvd), (eb1t, eb1d), (rwt, rwd), (rbt, rbd),
            (cmt, cmaskd), (cm1t, cmask1d), (cm64t, cm64d),
        ]
        for tt, dd_ in consts:
            p.emit("sync", (lambda t=tt, dd=dd_: lambda eng: eng.dma_start(out=t[:], in_=dd))(),
                   inc=("sx", 16), tag="const")
        consts_loaded = p.val("sx")

        rwcast = p.emit("vector", lambda eng: eng.tensor_copy(rwbt[:], rwt[:]),
                        waits=[("sx", consts_loaded)], inc=("sdv", 1), tag="rwcast")
        p.emit("vector", lambda eng: eng.memset(carA[:], 1.0), tag="carA0")
        p.emit("vector", lambda eng: eng.memset(carH[:], 0.0), inc=("sdv", 1), tag="carH0")
        carinit = p.val("sdv")

        with contextlib.ExitStack() as sctx:
            srow = sctx.enter_context(nc.sbuf_tensor("srow", [1, 2048], F32))
            xb = sctx.enter_context(nc.sbuf_tensor("xb", [128, 3 * 2048], F32))
            ln1 = sctx.enter_context(nc.sbuf_tensor("ln1", [128, 2 * 2048], F32))
            t1 = sctx.enter_context(nc.sbuf_tensor("t1", [128, 2 * 2048], F32))
            t2 = sctx.enter_context(nc.sbuf_tensor("t2", [128, 2 * 2048], F32))
            t3 = sctx.enter_context(nc.sbuf_tensor("t3", [128, 2 * 2048], F32))
            t4 = sctx.enter_context(nc.sbuf_tensor("t4", [128, 2048], F32))
            hb = sctx.enter_context(nc.sbuf_tensor("hb", [128, 2048], F32))
            ab = sctx.enter_context(nc.sbuf_tensor("ab", [128, 2 * 2048], F32))
            xsq = sctx.enter_context(nc.sbuf_tensor("xsq", [128, 2048], F32))
            wtl = sctx.enter_context(nc.sbuf_tensor("wtl", [128, 3 * 2048], F32))

            p.emit("vector", lambda eng: eng.memset(pb[4][:], 0.0), tag="zt0")
            for wsel in range(3):
                for dc in range(DC):
                    def mkw(wsel=wsel, dc=dc):
                        def fn(eng):
                            return eng.tensor_scalar(
                                wtl[:, wsel * 2048 + dc * TS:(wsel * 2048 + (dc + 1) * TS)],
                                pb[4][:, 0:TS], 0.0, wift[:, wsel * 8 + dc:wsel * 8 + dc + 1],
                                OP.mult, OP.add)
                        return fn
                    lastw = p.emit("vector", mkw(),
                                   waits=[("sx", consts_loaded)], inc=("sdv", 1), tag=f"wtl{wsel}_{dc}")
            wtl_done = p.val("sdv")

            ev_lnapply = [0] * NTILE
            ev_spill_h = [0] * NTILE
            ev_spill_a = [0] * NTILE
            ev_scan = [0] * NTILE
            ev_xload = [0] * NTILE
            ev_sq_used = [0] * NTILE
            ev_rows_dve = [0] * NTILE

            ev_bcast = [0] * NTILE
            def pre_sq_stats(k):
                xsl = xb[:, (k % 3) * 2048:((k % 3) + 1) * 2048]
                xqsl = xsq[:, 0:2048]
                waits = [("sx", lambda k=k: ev_xload[k])]
                if k >= 1:
                    waits.append(("smm", ev_sq_used[k - 1]))
                p.emit("scalar", (lambda x_=xsl, q_=xqsl: lambda eng: eng.activation(
                    q_, x_, AF.Square, bias=0.0, scale=1.0))(),
                    waits=waits, inc=("sac", 1), tag=f"sq{k}")
                sq_done = p.val("sac")

                waits = [("sac", sq_done)]
                if k >= 1:
                    waits.append(("sdv", ev_rows_dve[k - 1]))
                for dc in range(DC):
                    def mks(dc=dc, x_=xsl):
                        def fn(eng):
                            return eng.matmul(pb[0][0:1, 0:TS], onesm1t[:],
                                              x_[:, dc * TS:(dc + 1) * TS],
                                              start=(dc == 0), stop=(dc == DC - 1))
                        return fn
                    p.emit("tensor", mks(), waits=waits if dc == 0 else [], tag=f"summ{k}_{dc}")
                for dc in range(DC):
                    def mks2(dc=dc, q_=xqsl):
                        def fn(eng):
                            return eng.matmul(pb[1][0:1, 0:TS], onesm1t[:],
                                              q_[:, dc * TS:(dc + 1) * TS],
                                              start=(dc == 0), stop=(dc == DC - 1))
                        return fn
                    inc = ("smm", 1) if dc == DC - 1 else None
                    p.emit("tensor", mks2(), inc=inc, tag=f"sumsq{k}_{dc}")
                ev_sq_used[k] = p.val("smm")

            def pre_rows_bcast(k):
                par = k % 2
                mb = pb[2] if par == 0 else pb[5]
                rb = pb[3] if par == 0 else pb[6]
                base = par * 1024
                mrow = srow[0:1, base:base + TS]
                arow = srow[0:1, base + 256:base + 256 + TS]
                trow = srow[0:1, base + 512:base + 512 + TS]
                rrow = srow[0:1, base + 768:base + 768 + TS]

                waits = [("smm", ev_sq_used[k])]
                if k >= 2:
                    waits.append(("smm", ev_bcast[k - 2]))
                p.emit("vector", (lambda mr=mrow: lambda eng: eng.tensor_single_scalar(
                    mr, pb[0][0:1, 0:TS], 1.0 / D, OP.mult))(),
                    waits=waits, tag=f"rowm{k}")
                p.emit("vector", (lambda ar=arow: lambda eng: eng.tensor_scalar(
                    ar, pb[1][0:1, 0:TS], 1.0 / D, EPS, OP.mult, OP.add))(), tag=f"rowa{k}")
                p.emit("vector", (lambda mr=mrow, tr=trow: lambda eng: eng.tensor_tensor(
                    tr, mr, mr, OP.mult))(), tag=f"rowt{k}")
                ev_rows_dve[k] = p.emit("vector", (lambda ar=arow, tr=trow: lambda eng: eng.tensor_tensor(
                    ar, ar, tr, OP.subtract))(), inc=("sdv", 1), tag=f"rowb{k}")
                p.emit("scalar", (lambda ar=arow, tr=trow: lambda eng: eng.activation(
                    tr, ar, AF.Ln, bias=0.0, scale=1.0))(),
                    waits=[("sdv", ev_rows_dve[k])], tag=f"rowln{k}")
                rowsr = p.emit("scalar", (lambda rr=rrow, tr=trow: lambda eng: eng.activation(
                    rr, tr, AF.Exp, bias=0.0, scale=-0.5))(), inc=("sac", 1), tag=f"rowr{k}")

                waits = [("sac", rowsr)]
                if k >= 2:
                    waits.append(("sdv", ev_lnapply[k - 2]))
                p.emit("tensor", (lambda mr=mrow, mb=mb: lambda eng: eng.matmul(
                    mb[:, 0:TS], onesk1t[:], mr, start=True, stop=True))(),
                    waits=waits, tag=f"bcm{k}")
                ev_bcast[k] = p.emit("tensor", (lambda rr=rrow, rb=rb: lambda eng: eng.matmul(
                    rb[:, 0:TS], onesk1t[:], rr, start=True, stop=True))(),
                    inc=("smm", 1), tag=f"bcr{k}")

            def pre_load(k):
                xsl = xb[:, (k % 3) * 2048:((k % 3) + 1) * 2048]
                waits = []
                if k >= 3:
                    waits.append(("ssp", ev_spill_h[k - 3]))
                src = xcv[:, :, k * TS:(k + 1) * TS].transpose([1, 0, 2])
                dst = xsl.rearrange("p (c n) -> p c n", c=DC)
                ev_xload[k] = p.emit(
                    "sync", (lambda s_=src, d_=dst: lambda eng: eng.dma_start(out=d_, in_=s_))(),
                    waits=waits, inc=("sx", 16), tag=f"xload{k}")

            def mainpass(k):
                par = k % 2
                xsl = xb[:, (k % 3) * 2048:((k % 3) + 1) * 2048]
                l1sl = ln1[:, par * 2048:(par + 1) * 2048]
                t1sl = t1[:, par * 2048:(par + 1) * 2048]
                fsl = t2[:, par * 2048:(par + 1) * 2048]
                t3sl = t3[:, par * 2048:(par + 1) * 2048]
                vsl = t4[:, 0:2048]
                hsl = hb[:, 0:2048]
                asl = ab[:, par * 2048:(par + 1) * 2048]
                mb = pb[2] if par == 0 else pb[5]
                rb = pb[3] if par == 0 else pb[6]
                H = 1024

                def hsub(ap_, h):
                    return ap_[:, h * H:(h + 1) * H]

                for dc in range(DC):
                    def mka(dc=dc, x_=xsl, l_=l1sl, mb=mb):
                        def fn(eng):
                            return eng.tensor_tensor(l_[:, dc * TS:(dc + 1) * TS],
                                                     x_[:, dc * TS:(dc + 1) * TS],
                                                     mb[:, 0:TS], OP.subtract)
                        return fn
                    p.emit("vector", mka(), waits=[("smm", ev_bcast[k])] if dc == 0 else [],
                           tag=f"lna{k}_{dc}")
                lnh = [0, 0]
                for dc in range(DC):
                    def mkb(dc=dc, l_=l1sl, rb=rb):
                        def fn(eng):
                            sl = l_[:, dc * TS:(dc + 1) * TS]
                            return eng.tensor_tensor(sl, sl, rb[:, 0:TS], OP.mult)
                        return fn
                    inc = ("sdv", 1) if dc in (3, 7) else None
                    p.emit("vector", mkb(), inc=inc, tag=f"lnb{k}_{dc}")
                    if dc == 3:
                        lnh[0] = p.val("sdv")
                ev_lnapply[k] = lnh[1] = p.val("sdv")

                if DEBUG:
                    dmaf = (lambda k=k, l_=l1sl: lambda eng: eng.dma_start(
                        out=cpview(dbg["dbg_ln1"])[:, :, k * TS:(k + 1) * TS].transpose([1, 0, 2]),
                        in_=l_.rearrange("p (c n) -> p c n", c=DC)))()
                    gv = p.emit("sync", dmaf, waits=[("sdv", ev_lnapply[k])], inc=("ssp", 16),
                                tag=f"dbgln1_{k}")
                    p.set_gate(("ssp", gv))

                gfg = [0, 0]
                for h in range(2):
                    gw = [("sdv", max(lnh[h], wtl_done))]
                    p.emit("gpsimd", (lambda l_=l1sl, o_=t1sl, h=h: lambda eng: eng.tensor_tensor(
                        hsub(o_, h), hsub(l_, h), wtl[:, h * H:(h + 1) * H], OP.mult))(),
                        waits=gw, tag=f"fg{k}_{h}")
                    gfg[h] = p.emit("gpsimd", (lambda l_=l1sl, o_=t3sl, h=h: lambda eng: eng.tensor_tensor(
                        hsub(o_, h), hsub(l_, h), wtl[:, 2048 + h * H:2048 + (h + 1) * H], OP.mult))(),
                        inc=("sgp", 1), tag=f"ig{k}_{h}")

                spd = [0, 0]
                for h in range(2):
                    p.emit("scalar", (lambda o_=t1sl, h=h: lambda eng: eng.activation(
                        hsub(o_, h), hsub(o_, h), AF.Exp, bias=0.0, scale=-1.0))(),
                        waits=[("sgp", gfg[h])], tag=f"e1{k}_{h}")
                    p.emit("scalar", (lambda o_=t3sl, h=h: lambda eng: eng.activation(
                        hsub(o_, h), hsub(o_, h), AF.Exp, bias=0.0, scale=-1.0))(), tag=f"e2{k}_{h}")
                    p.emit("scalar", (lambda o_=t1sl, h=h: lambda eng: eng.activation(
                        hsub(o_, h), hsub(o_, h), AF.Ln, bias=1.0, scale=1.0))(), tag=f"l1{k}_{h}")
                    spd[h] = p.emit("scalar", (lambda o_=t3sl, h=h: lambda eng: eng.activation(
                        hsub(o_, h), hsub(o_, h), AF.Ln, bias=1.0, scale=1.0))(),
                        inc=("sac", 1), tag=f"l2{k}_{h}")

                gdiff = [0, 0]
                hgd = [0, 0]
                for h in range(2):
                    gdiff[h] = p.emit("gpsimd", (lambda a_=t1sl, b_=t3sl, h=h: lambda eng:
                                      eng.tensor_tensor(hsub(a_, h), hsub(a_, h), hsub(b_, h),
                                                        OP.subtract))(),
                                      waits=[("sac", spd[h])], inc=("sgp", 1), tag=f"diff{k}_{h}")
                for h in range(2):
                    hgd[h] = p.emit("vector", (lambda l_=l1sl, h=h: lambda eng: eng.tensor_tensor(
                        hsub(l_, h), hsub(l_, h), wtl[:, 4096 + h * H:4096 + (h + 1) * H], OP.mult))(),
                        waits=[("sgp", gfg[h])], inc=("sdv", 1), tag=f"hg{k}_{h}")

                sgd = [0, 0]
                for h in range(2):
                    p.emit("scalar", (lambda f_=fsl, a_=t1sl, h=h: lambda eng: eng.activation(
                        hsub(f_, h), hsub(a_, h), AF.Sigmoid, bias=0.0, scale=-1.0))(),
                        waits=[("sdv", hgd[h]), ("sgp", gdiff[h])], tag=f"sf{k}_{h}")
                    sgd[h] = p.emit("scalar", (lambda l_=l1sl, b_=t3sl, h=h: lambda eng: eng.activation(
                        hsub(b_, h), hsub(l_, h), AF.Sigmoid, bias=0.0, scale=1.0))(),
                        inc=("sac", 1), tag=f"sh{k}_{h}")

                iv = [0, 0]
                vv = [0, 0]
                for h in range(2):
                    p.emit("vector", (lambda l_=l1sl, h=h: lambda eng: eng.tensor_single_scalar(
                        hsub(l_, h), hsub(l_, h), 0.5, OP.add))(),
                        waits=[("sac", sgd[h])], tag=f"h05{k}_{h}")
                    p.emit("vector", (lambda l_=l1sl, b_=t3sl, h=h: lambda eng: eng.tensor_tensor(
                        hsub(b_, h), hsub(l_, h), hsub(b_, h), OP.max))(), tag=f"g{k}_{h}")
                    iv[h] = p.emit("vector", (lambda f_=fsl, a_=t1sl, h=h: lambda eng: eng.tensor_scalar(
                        hsub(a_, h), hsub(f_, h), -1.0, 1.0, OP.mult, OP.add))(),
                        inc=("sdv", 1), tag=f"i{k}_{h}")
                for h in range(2):
                    vw = [("sdv", iv[h])]
                    if k >= 1:
                        vw.append(("sdv", ev_scan[k - 1]))
                    vv[h] = p.emit("gpsimd", (lambda v_=vsl, a_=t1sl, b_=t3sl, h=h: lambda eng:
                                   eng.tensor_tensor(hsub(v_, h), hsub(a_, h), hsub(b_, h),
                                                     OP.mult))(),
                                   waits=vw, inc=("sgp", 1), tag=f"v{k}_{h}")

                for h in range(2):
                    waits = [("sdv", carinit), ("sgp", vv[h])]
                    if k >= 2 and h == 0:
                        waits += [("sw", ev_spill_a[k - 2])]
                    first = True
                    for dc in range(4 * h, 4 * h + 4):
                        def mksa(dc=dc, f_=fsl, a_=asl):
                            def fn(eng):
                                return eng.tensor_tensor_scan(
                                    a_[:, dc * TS:(dc + 1) * TS], f_[:, dc * TS:(dc + 1) * TS],
                                    pb[4][:, 0:TS], carA[:, dc:dc + 1], OP.mult, OP.add)
                            return fn
                        p.emit("vector", mksa(), waits=waits if first else [], tag=f"sa{k}_{dc}")
                        first = False
                    for dc in range(4 * h, 4 * h + 4):
                        def mksh(dc=dc, f_=fsl, v_=vsl, h_=hsl):
                            def fn(eng):
                                return eng.tensor_tensor_scan(
                                    h_[:, dc * TS:(dc + 1) * TS], f_[:, dc * TS:(dc + 1) * TS],
                                    v_[:, dc * TS:(dc + 1) * TS], carH[:, dc:dc + 1], OP.mult, OP.add)
                            return fn
                        p.emit("vector", mksh(), tag=f"shs{k}_{dc}")
                p.emit("vector", lambda eng, a_=asl: eng.tensor_copy(
                    carA[:], a_.rearrange("p (c n) -> p c n", c=DC)[:, :, TS - 1]), tag=f"ca{k}")
                p.emit("vector", lambda eng, h_=hsl: eng.tensor_copy(
                    carH[:], h_.rearrange("p (c n) -> p c n", c=DC)[:, :, TS - 1]), tag=f"ch{k}")
                ev_scan[k] = p.emit("vector", (lambda x_=xsl, h_=hsl: lambda eng: eng.tensor_tensor(
                    x_, x_, h_, OP.add))(), inc=("sdv", 1), tag=f"x1p{k}")

                hdst = hspv[:, :, k * TS:(k + 1) * TS].transpose([1, 0, 2])
                ev_spill_h[k] = p.emit(
                    "sync", (lambda d_=hdst, s_=xsl: lambda eng: eng.dma_start(
                        out=d_, in_=s_.rearrange("p (c n) -> p c n", c=DC)))(),
                    waits=[("sdv", ev_scan[k])], inc=("ssp", 16), tag=f"hspill{k}")
                adst = aspv[:, :, k * TS:(k + 1) * TS].transpose([1, 0, 2])
                ev_spill_a[k] = p.emit(
                    "gpsimd", (lambda d_=adst, s_=asl: lambda eng: eng.dma_start(
                        out=d_, in_=s_.rearrange("p (c n) -> p c n", c=DC)))(),
                    waits=[("sdv", ev_scan[k])], inc=("sw", 16), tag=f"aspill{k}")

            for k in range(NTILE + 2):
                if k < NTILE:
                    pre_load(k)
                    pre_sq_stats(k)
                if k >= 2:
                    mainpass(k - 2)
                if k < NTILE:
                    pre_rows_bcast(k)

            agin_vA = agin[0:1, :].rearrange("o (d q) -> (o q) d", q=128)
            agin_vB = agin[1:2, :].rearrange("o (d q) -> (o q) d", q=128)
            sA = p.emit("sync", lambda eng: eng.dma_start(out=agin_vA, in_=carA[:],
                                                          allow_slow_non_contiguous=True),
                        waits=[("sdv", ev_scan[NTILE - 1])], inc=("ssp", 16), tag="sumA")
            sB = p.emit("sync", lambda eng: eng.dma_start(out=agin_vB, in_=carH[:],
                                                          allow_slow_non_contiguous=True),
                        inc=("ssp", 16), tag="sumB")
            cc = p.emit("gpsimd", lambda eng: eng.collective_compute(
                "AllGather", OP.bypass, ins=[agin], outs=[agout],
                replica_groups=[[0, 1, 2, 3, 4, 5, 6, 7]]),
                waits=[("ssp", sB)], inc=("scc", 1), tag="allgather")
            agv = agout.rearrange("(j a) (d q) -> j a d q", a=2, q=128)
            gB = 0
            for j in range(8):
                for ab, dst in ((0, csumA), (1, csumB)):
                    gB = p.emit("scalar", (lambda j=j, ab=ab, dst=dst: lambda eng: eng.dma_start(
                        out=dst[:, j * 8:(j + 1) * 8],
                        in_=agv[j, ab, :, :].transpose([1, 0]),
                        allow_slow_non_contiguous=True))(),
                        waits=[("scc", cc)] if (j == 0 and ab == 0) else [],
                        inc=("sx", 16), tag=f"gath{ab}_{j}")

            if DEBUG:
                p.emit("sync", lambda eng: eng.dma_start(out=dbg["dbg_car"][:, 0:8], in_=carA[:]),
                       waits=[("sdv", ev_scan[NTILE - 1])], inc=("ssp", 16), tag="dbgcarA")
                p.emit("sync", lambda eng: eng.dma_start(out=dbg["dbg_car"][:, 8:16], in_=carH[:]),
                       inc=("ssp", 16), tag="dbgcarB")
                p.emit("sync", lambda eng: eng.dma_start(out=dbg["dbg_csum"][:, 0:64], in_=csumA[:]),
                       waits=[("sx", gB)], inc=("ssp", 16), tag="dbgcsA")
                gv = p.emit("sync", lambda eng: eng.dma_start(out=dbg["dbg_csum"][:, 64:128], in_=csumB[:]),
                            inc=("ssp", 16), tag="dbgcsB")
                p.set_gate(("ssp", gv))
            p.emit("vector", lambda eng: eng.memset(cvec[:], 0.0), waits=[("sx", gB)], tag="cv0")
            p.emit("vector", lambda eng: eng.tensor_single_scalar(
                lgb[:, 96:160], csumA[:], -1.0, OP.add), tag="cAm1")
            p.emit("vector", lambda eng: eng.tensor_tensor(
                lgb[:, 96:160], lgb[:, 96:160], cm64t[:], OP.mult), tag="cAm2")
            p.emit("vector", lambda eng: eng.tensor_tensor(
                lgb[:, 160:224], csumB[:], cm64t[:], OP.mult), tag="cBm")
            p.emit("vector", lambda eng: eng.drain(), tag="drn")
            for j in range(8):
                p.emit("vector", (lambda j=j: lambda eng: eng.tensor_tensor(
                    lgb[:, 88:96], cvec[:], lgb[:, 96 + j * 8:104 + j * 8], OP.mult))(),
                    tag=f"cm{j}")
                p.emit("vector", lambda eng: eng.drain(), tag="drn")
                p.emit("vector", lambda eng: eng.tensor_tensor(
                    cvec[:], cvec[:], lgb[:, 88:96], OP.add), tag=f"cu{j}")
                p.emit("vector", lambda eng: eng.drain(), tag="drn")
                p.emit("vector", (lambda j=j: lambda eng: eng.tensor_tensor(
                    cvec[:], cvec[:], lgb[:, 160 + j * 8:168 + j * 8], OP.add))(),
                    tag=f"cp{j}")
                p.emit("vector", lambda eng: eng.drain(), tag="drn")
            cdone = p.emit("vector", lambda eng: eng.tensor_copy(lgb[:, 0:8], cvec[:]),
                           inc=("sdv", 1), tag="cdone")
            if DEBUG:
                gv = p.emit("sync", lambda eng: eng.dma_start(out=dbg["dbg_cvec"], in_=cvec[:]),
                            waits=[("sdv", cdone)], inc=("ssp", 16), tag="dbgcvec")
                p.set_gate(("ssp", gv))
            spills_all = p.val("ssp")
            aspills_all = p.val("sw")

        with contextlib.ExitStack() as mctx:
            srow = mctx.enter_context(nc.sbuf_tensor("srow2", [1, 2048], F32))
            xm = mctx.enter_context(nc.sbuf_tensor("xm", [128, 2 * 4096], F32))
            am = mctx.enter_context(nc.sbuf_tensor("am", [128, 2 * 4096], BF16))
            xsqm = mctx.enter_context(nc.sbuf_tensor("xsqm", [128, 4096], BF16))
            x1bf = mctx.enter_context(nc.sbuf_tensor("x1bf", [128, 4096], BF16))
            onesmb = mctx.enter_context(nc.sbuf_tensor("onesmb", [128, 1], BF16))

            ev_ln2 = [0] * NMT
            ev_sqm_used = [0] * NMT
            ev_rows2 = [0] * NMT
            ev_x1sp = [0] * NMT
            cwstate = {"lgevac": 0, "cwout": 0}

            def emit_cw(j4):
                for dc in range(DC):
                    def mkl(dc=dc, j4=j4):
                        def fn(eng):
                            return eng.matmul(pb[6][0:15, :],
                                              rwbt[:, dc * 15:(dc + 1) * 15],
                                              ln2b[:, dc * TOK + j4 * NSUB:dc * TOK + (j4 + 1) * NSUB],
                                              start=(dc == 0), stop=(dc == DC - 1))
                        return fn
                    w = []
                    if dc == 0:
                        w = [("sdv", max(ev_ln2[j4], rwcast, cwstate["cwout"]))]
                        if j4 > 0:
                            w.append(("sac", cwstate["lgevac"]))
                    inc = ("smm", 1) if dc == DC - 1 else None
                    p.emit("tensor", mkl(), waits=w, inc=inc, tag=f"lg{j4}_{dc}")
                lgmm = p.val("smm")
                lgevac = p.emit("vector", lambda eng: eng.tensor_scalar_add(
                    lgsb[:], pb[6][0:15, :], rbt[:, 0:1]),
                    waits=[("smm", lgmm)], inc=("sac", 1), tag=f"lgev{j4}")
                cwstate["lgevac"] = lgevac

                w = [("sac", lgevac)]
                if j4 > 0:
                    w.append(("sdv", cwstate["cwout"]))
                tf = 0
                for g2 in range(4):
                    tf = p.emit("tensor", (lambda g2=g2: lambda eng: eng.transpose(
                        pb[7][:, g2 * 15:(g2 + 1) * 15], lgsb[:, g2 * 128:(g2 + 1) * 128],
                        identt[0:15, 0:15]))(),
                        waits=w if g2 == 0 else [], inc=("smm", 1) if g2 == 3 else None,
                        tag=f"trf{j4}_{g2}")
                def rounds1(g2):
                    L = g2 * 88
                    yield lambda eng: eng.tensor_copy(lgb[:, L:L + 15], pb[7][:, g2 * 15:(g2 + 1) * 15])
                    yield lambda eng: eng.reduce_max(lgb[:, L + 15:L + 16], lgb[:, L:L + 15],
                                                     axis=AX, negate=True)
                    yield lambda eng: eng.tensor_single_scalar(lgb[:, L + 16:L + 17],
                                                               lgb[:, L + 15:L + 16], -1.0, OP.mult)
                    yield lambda eng: eng.tensor_scalar(lgb[:, L + 21:L + 36], lgb[:, L:L + 15],
                                                        lgb[:, L + 16:L + 17], -1e30,
                                                        OP.is_equal, OP.mult)
                    yield lambda eng: eng.tensor_tensor(lgb[:, L + 36:L + 51], lgb[:, L:L + 15],
                                                        lgb[:, L + 21:L + 36], OP.add)
                    yield lambda eng: eng.reduce_max(lgb[:, L + 17:L + 18], lgb[:, L + 36:L + 51],
                                                     axis=AX, negate=True)
                    yield lambda eng: eng.tensor_single_scalar(lgb[:, L + 18:L + 19],
                                                               lgb[:, L + 17:L + 18], -1.0, OP.mult)
                    yield lambda eng: eng.tensor_scalar(lgb[:, L + 21:L + 36], lgb[:, L + 36:L + 51],
                                                        lgb[:, L + 18:L + 19], -1e30,
                                                        OP.is_equal, OP.mult)
                    yield lambda eng: eng.tensor_tensor(lgb[:, L + 36:L + 51], lgb[:, L + 36:L + 51],
                                                        lgb[:, L + 21:L + 36], OP.add)
                    yield lambda eng: eng.reduce_max(lgb[:, L + 17:L + 18], lgb[:, L + 36:L + 51],
                                                     axis=AX, negate=True)
                    yield lambda eng: eng.tensor_single_scalar(lgb[:, L + 18:L + 19],
                                                               lgb[:, L + 17:L + 18], -1.0, OP.mult)
                    yield lambda eng: eng.tensor_single_scalar(lgb[:, L + 21:L + 36], lgb[:, L:L + 15],
                                                               lgb[:, L + 18:L + 19], OP.is_ge)

                gens = [rounds1(g2) for g2 in range(4)]
                for r in range(12):
                    for g2 in range(4):
                        fn = next(gens[g2])
                        inc = ("sdv", 1) if (r == 11 and g2 == 3) else None
                        p.emit("vector", fn,
                               waits=[("smm", tf)] if (r == 0 and g2 == 0) else [],
                               inc=inc, tag=f"cwr1_{j4}_{r}_{g2}")
                r1d = p.val("sdv")
                exd = 0
                for g2 in range(4):
                    L = g2 * 88
                    exd = p.emit("scalar", (lambda L=L: lambda eng: eng.activation(
                        lgb[:, L + 51:L + 66], lgb[:, L:L + 15], AF.Exp,
                        bias=lgb[:, L + 15:L + 16], scale=1.0))(),
                        waits=[("sdv", r1d)] if g2 == 0 else [],
                        inc=("sac", 1) if g2 == 3 else None, tag=f"cwex{j4}_{g2}")

                def rounds2(g2):
                    L = g2 * 88
                    yield lambda eng: eng.reduce_sum(lgb[:, L + 19:L + 20], lgb[:, L + 51:L + 66], axis=AX)
                    yield lambda eng: eng.reciprocal(lgb[:, L + 20:L + 21], lgb[:, L + 19:L + 20])
                    yield lambda eng: eng.tensor_single_scalar(lgb[:, L + 51:L + 66],
                                                               lgb[:, L + 51:L + 66],
                                                               lgb[:, L + 20:L + 21], OP.mult)
                    yield lambda eng: eng.tensor_tensor(lgb[:, L + 66:L + 81], lgb[:, L + 51:L + 66],
                                                        lgb[:, L + 21:L + 36], OP.mult)

                gens = [rounds2(g2) for g2 in range(4)]
                for r in range(4):
                    for g2 in range(4):
                        fn = next(gens[g2])
                        inc = ("sdv", 1) if (r == 3 and g2 == 3) else None
                        p.emit("vector", fn, waits=[("sac", exd)] if (r == 0 and g2 == 0) else [],
                               inc=inc, tag=f"cwr2_{j4}_{r}_{g2}")
                r2d = p.val("sdv")
                tb = 0
                for g2 in range(4):
                    L = g2 * 88
                    tb = p.emit("tensor", (lambda L=L, g2=g2: lambda eng: eng.transpose(
                        pb[6][0:15, g2 * 128:(g2 + 1) * 128], lgb[:, L + 66:L + 81], identt[:]))(),
                        waits=[("sdv", r2d)] if g2 == 0 else [],
                        inc=("smm", 1) if g2 == 3 else None, tag=f"trb{j4}_{g2}")
                cwstate["cwout"] = p.emit("vector", (lambda j4=j4: lambda eng: eng.tensor_copy(
                    cwfb[0:15, j4 * 512:(j4 + 1) * 512], pb[6][0:15, 0:512]))(),
                    waits=[("smm", tb)], inc=("sdv", 1), tag=f"cwout{j4}")

            for m in range(NMT):
                par = m % 2
                xsl = xm[:, par * 4096:(par + 1) * 4096]
                asl = am[:, par * 4096:(par + 1) * 4096]
                MS = MSUB

                waits = [("ssp", spills_all), ("sw", aspills_all)]
                if m >= 2:
                    waits.append(("sdv", ev_ln2[m - 2]))
                    waits.append(("ssp", ev_x1sp[m - 2]))
                xld = p.emit("sync", (lambda m=m, d_=xsl: lambda eng: eng.dma_start(
                    out=d_.rearrange("p (c n) -> p c n", c=DC),
                    in_=hspv[:, :, m * MSUB:(m + 1) * MSUB].transpose([1, 0, 2])))(),
                    waits=waits, inc=("sx", 16), tag=f"mx{m}")
                ald = p.emit("sync", (lambda m=m, d_=asl: lambda eng: eng.dma_start(
                    out=d_.rearrange("p (c n) -> p c n", c=DC),
                    in_=aspv[:, :, m * MSUB:(m + 1) * MSUB].transpose([1, 0, 2])))(),
                    inc=("sx", 16), tag=f"ma{m}")

                for dc in range(DC):
                    def mkf(dc=dc, x_=xsl, a_=asl):
                        def fn(eng):
                            sl = slice(dc * MS, (dc + 1) * MS)
                            return eng.scalar_tensor_tensor(x_[:, sl], a_[:, sl], cvec[:, dc:dc + 1],
                                                            x_[:, sl], OP.mult, OP.add)
                        return fn
                    w = [("sx", ald), ("sdv", cdone)] if dc == 0 else []
                    inc = ("sdv", 1) if dc == DC - 1 else None
                    p.emit("vector", mkf(), waits=w, inc=inc, tag=f"fix{m}_{dc}")
                x1d = p.val("sdv")
                if DEBUG:
                    gv = p.emit("sync", (lambda m=m, x_=xsl: lambda eng: eng.dma_start(
                        out=cpview(dbg["dbg_h"])[:, :, m * MSUB:(m + 1) * MSUB].transpose([1, 0, 2]),
                        in_=x_.rearrange("p (c n) -> p c n", c=DC)))(),
                        waits=[("sdv", x1d)], inc=("ssp", 16), tag=f"dbgh{m}")
                    p.set_gate(("ssp", gv))

                ev_x1sp[m] = p.emit("sync", (lambda m=m, s_=xsl: lambda eng: eng.dma_start(
                    out=x1v[:, :, m * MSUB:(m + 1) * MSUB].transpose([1, 0, 2]),
                    in_=s_.rearrange("p (c n) -> p c n", c=DC)))(),
                    waits=[("sdv", x1d)], inc=("ssp", 16), tag=f"x1sp{m}")

                waits = [("sdv", x1d)]
                if m >= 1:
                    waits.append(("smm", ev_sqm_used[m - 1]))
                    waits.append(("sdv", ev_ln2[m - 1]))
                if m == 0:
                    p.emit("vector", lambda eng: eng.tensor_copy(onesmb[:], onesm1t[:]),
                           tag="onesmb")
                bw = [("sdv", x1d)]
                if m >= 1:
                    bw.append(("smm", ev_sqm_used[m - 1]))
                mxbf = p.emit("vector", (lambda x_=xsl: lambda eng: eng.tensor_copy(
                    x1bf[:], x_))(), waits=bw, inc=("sdv", 1), tag=f"mxbf{m}")
                sqm = p.emit("scalar", (lambda x_=xsl: lambda eng: eng.activation(
                    xsqm[:], x_, AF.Square, bias=0.0, scale=1.0))(),
                    waits=waits, inc=("sac", 1), tag=f"msq{m}")
                waits = [("sac", sqm), ("sdv", mxbf)]
                if m >= 1:
                    waits.append(("sdv", ev_rows2[m - 1]))
                for dc in range(DC):
                    def mks(dc=dc):
                        def fn(eng):
                            return eng.matmul(pb[0][0:1, 0:MS], onesmb[:],
                                              x1bf[:, dc * MS:(dc + 1) * MS],
                                              start=(dc == 0), stop=(dc == DC - 1))
                        return fn
                    p.emit("tensor", mks(), waits=waits if dc == 0 else [], tag=f"msum{m}_{dc}")
                for dc in range(DC):
                    def mks2(dc=dc):
                        def fn(eng):
                            return eng.matmul(pb[1][0:1, 0:MS], onesmb[:],
                                              xsqm[:, dc * MS:(dc + 1) * MS],
                                              start=(dc == 0), stop=(dc == DC - 1))
                        return fn
                    inc = ("smm", 1) if dc == DC - 1 else None
                    p.emit("tensor", mks2(), inc=inc, tag=f"msumsq{m}_{dc}")
                ev_sqm_used[m] = p.val("smm")

                mrow = srow[0:1, 0:MS]
                arow = srow[0:1, 512:512 + MS]
                trow = srow[0:1, 1024:1024 + MS]
                rrow = srow[0:1, 1536:1536 + MS]
                p.emit("vector", (lambda mr=mrow: lambda eng: eng.tensor_single_scalar(
                    mr, pb[0][0:1, 0:MS], 1.0 / D, OP.mult))(),
                    waits=[("smm", ev_sqm_used[m])], tag=f"mrowm{m}")
                p.emit("vector", (lambda ar=arow: lambda eng: eng.tensor_scalar(
                    ar, pb[1][0:1, 0:MS], 1.0 / D, EPS, OP.mult, OP.add))(), tag=f"mrowa{m}")
                p.emit("vector", (lambda mr=mrow, tr=trow: lambda eng: eng.tensor_tensor(
                    tr, mr, mr, OP.mult))(), tag=f"mrowt{m}")
                ev_rows2[m] = p.emit("vector", (lambda ar=arow, tr=trow: lambda eng: eng.tensor_tensor(
                    ar, ar, tr, OP.subtract))(), inc=("sdv", 1), tag=f"mrowb{m}")
                p.emit("scalar", (lambda ar=arow, tr=trow: lambda eng: eng.activation(
                    tr, ar, AF.Ln, bias=0.0, scale=1.0))(),
                    waits=[("sdv", ev_rows2[m])], tag=f"mrowln{m}")
                rr2 = p.emit("scalar", (lambda rr=rrow, tr=trow: lambda eng: eng.activation(
                    rr, tr, AF.Exp, bias=0.0, scale=-0.5))(), inc=("sac", 1), tag=f"mrowr{m}")

                waits = [("sac", rr2)]
                if m >= 1:
                    waits.append(("sdv", ev_ln2[m - 1]))
                p.emit("tensor", (lambda mr=mrow: lambda eng: eng.matmul(
                    pb[2][:, 0:MS], onesk1t[:], mr, start=True, stop=True))(),
                    waits=waits, tag=f"mbcm{m}")
                bcs = p.emit("tensor", (lambda rr=rrow: lambda eng: eng.matmul(
                    pb[3][:, 0:MS], onesk1t[:], rr, start=True, stop=True))(),
                    inc=("smm", 1), tag=f"mbcr{m}")

                for dc in range(DC):
                    def mka(dc=dc, x_=xsl):
                        def fn(eng):
                            sl = slice(dc * MS, (dc + 1) * MS)
                            return eng.tensor_tensor(xsqm[:, sl], x_[:, sl], pb[2][:, 0:MS], OP.subtract)
                        return fn
                    p.emit("vector", mka(), waits=[("smm", bcs)] if dc == 0 else [], tag=f"l2a{m}_{dc}")
                for dc in range(DC):
                    def mkb(dc=dc, m=m):
                        def fn(eng):
                            return eng.tensor_tensor(
                                ln2b[:, dc * TOK + m * MSUB:dc * TOK + (m + 1) * MSUB],
                                xsqm[:, dc * MS:(dc + 1) * MS], pb[3][:, 0:MS], OP.mult)
                        return fn
                    inc = ("sdv", 1) if dc == DC - 1 else None
                    p.emit("vector", mkb(), inc=inc, tag=f"l2b{m}_{dc}")
                ev_ln2[m] = p.val("sdv")
                if m < NMT - 1:
                    emit_cw(m)


            ln2_all = p.val("sdv")
            x1_spilled = p.val("ssp")

        with contextlib.ExitStack() as fctx:
            oneht = fctx.enter_context(nc.sbuf_tensor("oneht", [15, 1920], BF16))
            eb2t = fctx.enter_context(nc.sbuf_tensor("eb2t", [15, 1024], BF16))
            acc = fctx.enter_context(nc.sbuf_tensor("acc", [128, DC * TOK], F32))
            h_sb = fctx.enter_context(nc.sbuf_tensor("h_sb", [128, 2 * 4096], BF16))
            rtmp = fctx.enter_context(nc.sbuf_tensor("rtmp", [128, 2 * 512], F32))

            ohl = p.emit("sync", lambda eng: eng.dma_start(out=oneht[:], in_=onehd),
                         waits=[("sdv", ln2_all), ("ssp", x1_spilled)], inc=("sx", 16), tag="ohl")
            eb2l = None


            if DEBUG:
                gv = p.emit("gpsimd", lambda eng: eng.dma_start(out=dbg["dbg_cw"], in_=cwfb[:]),
                            waits=[("sdv", cwstate["cwout"])], inc=("sw", 16), tag="dbgcw")
                p.set_gate(("sw", gv))
                lv = (lambda eng: eng.dma_start(
                    out=cpview(dbg["dbg_ln2"]).transpose([1, 0, 2]),
                    in_=ln2b[:].rearrange("p (c n) -> p c n", c=DC)))
                gv = p.emit("gpsimd", lv, waits=[("sdv", ln2_all)], inc=("sw", 16), tag="dbgln2")
                p.set_gate(("sw", gv))

            with contextlib.ExitStack() as wctx:
                wbuf = [wctx.enter_context(nc.sbuf_tensor(f"wbuf{i}", [128, 2 * 8192], BF16))
                        for i in range(2)]

                def wload(eng_unused, ee, buf):
                    if ee == 0:
                        w1src, w2src = sW1d, sW2d
                    else:
                        w1src = eW1d[ee - 1]
                        w2src = eW2d[ee - 1]
                    v1 = w1src.rearrange("(c p) n -> c p n", p=128).transpose([1, 0, 2])
                    v2 = w2src.rearrange("(c p) n -> c p n", p=128).transpose([1, 0, 2])
                    d1 = wbuf[buf][:, 0:8192].rearrange("p (c n) -> p c n", c=DC)
                    d2 = wbuf[buf][:, 8192:16384].rearrange("p (c n) -> p c n", c=DC)
                    return v1, d1, v2, d2

                ev_wload = [0] * NEXP
                ev_ee_mm_end = [0] * NEXP

                for ee in range(2):
                    v1, d1, v2, d2 = wload(None, ee, ee % 2)
                    w = [("ssp", ev_spill_h[NTILE - 1]), ("sw", ev_spill_a[NTILE - 1])] if ee == 0 else []
                    p.emit("gpsimd", (lambda a=v1, b=d1: lambda eng: eng.dma_start(out=b, in_=a))(),
                           waits=w, inc=("sw", 16), tag=f"w1l{ee}")
                    ev_wload[ee] = p.emit(
                        "gpsimd", (lambda a=v2, b=d2: lambda eng: eng.dma_start(out=b, in_=a))(),
                        inc=("sw", 16), tag=f"w2l{ee}")
                eb2l = p.emit("gpsimd", lambda eng: eng.dma_start(out=eb2t[:], in_=eb2d),
                              waits=[("sdv", ln2_all), ("ssp", x1_spilled)], inc=("sw", 16), tag="eb2l")

                ev_accj = [0] * NJ
                hfree = [0, 0]
                yfree = [0, 0]
                cwbfree = [0, 0]
                rtfree = [0, 0]

                for ee in range(NEXP):
                    wpar = ee % 2
                    hm_done = {}

                    def emit_cwb(j, ee=None):
                        cb = 4 + (j % 2)
                        w = [("sdv", max(cwstate["cwout"], cwbfree[j % 2])), ("sx", ohl)]
                        cwb = p.emit("tensor", (lambda ee=ee, j=j, cb=cb: lambda eng: eng.matmul(
                            pb[cb][:], oneht[:, (ee - 1) * 128:ee * 128],
                            cwfb[0:15, j * NSUB:(j + 1) * NSUB], start=True, stop=True))(),
                            waits=w, inc=("smm", 1), tag=f"cwb{ee}_{j}")
                        return cwb

                    def emit_w1(j, cwb, ee=None):
                        hpar = j % 2
                        cb = 4 + (j % 2)
                        hmul_events = []
                        for hc in range(HCN):
                            bk = hc % 2
                            w = []
                            if ee == 0 and j == 0 and hc == 0:
                                w.append(("sdv", ln2_all))
                                w.append(("ssp", x1_spilled))
                            w.append(("sw", ev_wload[ee]))
                            if hfree[bk]:
                                w.append(("sac", hfree[bk]))
                            for dc in range(DC):
                                def mkw1(ee=ee, j=j, hc=hc, dc=dc, bk=bk, wpar=wpar):
                                    def fn(eng):
                                        return eng.matmul(
                                            pb[bk][:],
                                            wbuf[wpar][:, dc * 1024 + hc * 128:dc * 1024 + (hc + 1) * 128],
                                            ln2b[:, dc * TOK + j * NSUB:dc * TOK + (j + 1) * NSUB],
                                            start=(dc == 0), stop=(dc == DC - 1))
                                    return fn
                                inc = ("smm", 1) if dc == DC - 1 else None
                                p.emit("tensor", mkw1(), waits=w if dc == 0 else [], inc=inc,
                                       tag=f"W1_{ee}_{j}_{hc}_{dc}")
                                w = []
                            w1g = p.val("smm")
                            if ee == 0:
                                hfree[bk] = p.emit("scalar", (lambda hc=hc, bk=bk, hpar=hpar: lambda eng:
                                                   eng.activation(
                                                       h_sb[:, hpar * 4096 + hc * 512:hpar * 4096 + (hc + 1) * 512],
                                                       pb[bk][:], AF.Relu, bias=sbvt[:, hc:hc + 1], scale=1.0))(),
                                                   waits=[("smm", w1g)], inc=("sac", 1), tag=f"relu{ee}_{j}_{hc}")
                                hmul_events.append(("sac", hfree[bk]))
                            else:
                                rslot = hc % 2
                                rw = [("smm", w1g)]
                                if rtfree[rslot]:
                                    rw.append(("sdv", rtfree[rslot]))
                                rel = p.emit("scalar", (lambda ee=ee, hc=hc, bk=bk, rslot=rslot: lambda eng:
                                             eng.activation(
                                                 rtmp[:, rslot * 512:(rslot + 1) * 512], pb[bk][:], AF.Relu,
                                                 bias=eb1t[:, (ee - 1) * 8 + hc:(ee - 1) * 8 + hc + 1],
                                                 scale=1.0))(),
                                             waits=rw, inc=("sac", 1), tag=f"relu{ee}_{j}_{hc}")
                                hfree[bk] = rel
                                hm_ = p.emit("vector", (lambda hc=hc, hpar=hpar, cb=cb, rslot=rslot: lambda eng:
                                             eng.tensor_tensor(
                                                 h_sb[:, hpar * 4096 + hc * 512:hpar * 4096 + (hc + 1) * 512],
                                                 rtmp[:, rslot * 512:(rslot + 1) * 512], pb[cb][:], OP.mult))(),
                                             waits=[("sac", rel), ("smm", cwb)], inc=("sdv", 1),
                                             tag=f"hmul{ee}_{j}_{hc}")
                                rtfree[rslot] = hm_
                                hmul_events.append(("sdv", hm_))
                        if ee >= 1:
                            cwbfree[j % 2] = hmul_events[-1][1]
                        hm_done[j] = hmul_events[-1]

                    def emit_w2(j, ee=None):
                        hpar = j % 2
                        for dc in range(DC):
                            bk = 2 + dc % 2
                            w = []
                            if dc == 0:
                                w.append(hm_done[j])
                            if yfree[dc % 2]:
                                w.append(("sdv", yfree[dc % 2]))
                            for hc in range(HCN):
                                def mkw2(ee=ee, j=j, hc=hc, dc=dc, bk=bk, wpar=wpar, hpar=hpar):
                                    def fn(eng):
                                        return eng.matmul(
                                            pb[bk][:],
                                            wbuf[wpar][:, 8192 + hc * 1024 + dc * 128:8192 + hc * 1024 + (dc + 1) * 128],
                                            h_sb[:, hpar * 4096 + hc * 512:hpar * 4096 + (hc + 1) * 512],
                                            start=(hc == 0), stop=(hc == HCN - 1 and ee != 1))
                                    return fn
                                inc = ("smm", 1) if (hc == HCN - 1 and ee != 1) else None
                                p.emit("tensor", mkw2(), waits=w if hc == 0 else [],
                                       inc=inc, tag=f"W2_{ee}_{j}_{dc}_{hc}")
                                w = []
                            if ee == 1:
                                def mkeb(j=j, dc=dc, bk=bk):
                                    def fn(eng):
                                        return eng.matmul(
                                            pb[bk][:], eb2t[:, dc * 128:(dc + 1) * 128],
                                            cwfb[0:15, j * NSUB:(j + 1) * NSUB],
                                            start=False, stop=True)
                                    return fn
                                p.emit("tensor", mkeb(), waits=[("sw", eb2l)],
                                       inc=("smm", 1), tag=f"eb2_{j}_{dc}")
                            w2g = p.val("smm")
                            if ee == 0:
                                yfree[dc % 2] = p.emit("vector", (lambda j=j, dc=dc, bk=bk: lambda eng:
                                                       eng.tensor_copy(
                                                           acc[:, dc * TOK + j * NSUB:dc * TOK + (j + 1) * NSUB],
                                                           pb[bk][:]))(),
                                                       waits=[("smm", w2g)], inc=("sdv", 1),
                                                       tag=f"acc{ee}_{j}_{dc}")
                            else:
                                yfree[dc % 2] = p.emit("vector", (lambda j=j, dc=dc, bk=bk: lambda eng:
                                                       eng.tensor_tensor(
                                                           acc[:, dc * TOK + j * NSUB:dc * TOK + (j + 1) * NSUB],
                                                           acc[:, dc * TOK + j * NSUB:dc * TOK + (j + 1) * NSUB],
                                                           pb[bk][:], OP.add))(),
                                                       waits=[("smm", w2g)], inc=("sdv", 1),
                                                       tag=f"acc{ee}_{j}_{dc}")

                    cwbs = {}
                    if ee >= 1:
                        cwbs[0] = emit_cwb(0, ee=ee)
                    emit_w1(0, cwbs.get(0), ee=ee)
                    if ee == 0:
                        emit_cw(NMT - 1)
                    if ee >= 1:
                        cwbs[1] = emit_cwb(1, ee=ee)
                    emit_w1(1, cwbs.get(1), ee=ee)
                    emit_w2(0, ee=ee)
                    if ee == NEXP - 1:
                        ev_accj[0] = p.val("sdv")
                    if ee >= 1:
                        cwbs[2] = emit_cwb(2, ee=ee)
                    emit_w1(2, cwbs.get(2), ee=ee)
                    emit_w2(1, ee=ee)
                    if ee == NEXP - 1:
                        ev_accj[1] = p.val("sdv")
                    if ee >= 1:
                        cwbs[3] = emit_cwb(3, ee=ee)
                    emit_w1(3, cwbs.get(3), ee=ee)
                    emit_w2(2, ee=ee)
                    if ee == NEXP - 1:
                        ev_accj[2] = p.val("sdv")
                    emit_w2(3, ee=ee)
                    if ee == NEXP - 1:
                        ev_accj[3] = p.val("sdv")

                    ev_ee_mm_end[ee] = p.val("smm")
                    if ee + 2 < NEXP:
                        v1, d1, v2, d2 = wload(None, ee + 2, wpar)
                        p.emit("gpsimd", (lambda a=v1, b=d1: lambda eng:
                               eng.dma_start(out=b, in_=a))(),
                               waits=[("smm", ev_ee_mm_end[ee])], inc=("sw", 16), tag=f"w1l{ee + 2}")
                        ev_wload[ee + 2] = p.emit(
                            "gpsimd", (lambda a=v2, b=d2: lambda eng: eng.dma_start(out=b, in_=a))(),
                            inc=("sw", 16), tag=f"w2l{ee + 2}")

                acc_done = p.val("sdv")
                if DEBUG:
                    p.emit("gpsimd", lambda eng: eng.dma_start(
                        out=cpview(dbg["dbg_acc"]).transpose([1, 0, 2]),
                        in_=acc[:].rearrange("p (c n) -> p c n", c=DC)),
                        waits=[("sdv", acc_done)], inc=("sw", 16), tag="dbgacc")

            with contextlib.ExitStack() as ectx:
                xr = ectx.enter_context(nc.sbuf_tensor("xr", [128, 2 * 4096], F32))
                ev_out = [0] * NJ
                ev_fin = [0] * NJ
                for f in range(NJ):
                    par = f % 2
                    w = [("ssp", x1_spilled), ("smm", ev_ee_mm_end[NEXP - 2])]
                    if f >= 2:
                        w.append(("ssp", ev_out[f - 2]))
                    xld = p.emit("sync", (lambda f=f, par=par: lambda eng: eng.dma_start(
                        out=xr[:, par * 4096:(par + 1) * 4096].rearrange("p (c n) -> p c n", c=DC),
                        in_=x1v[:, :, f * NSUB:(f + 1) * NSUB].transpose([1, 0, 2])))(),
                        waits=w, inc=("sx", 16), tag=f"xrl{f}")
                    w = [("sx", xld), ("sdv", ev_accj[f])]
                    for dc in range(DC):
                        def mkf(f=f, dc=dc, par=par):
                            def fn(eng):
                                sl = xr[:, par * 4096 + dc * NSUB:par * 4096 + (dc + 1) * NSUB]
                                return eng.scalar_tensor_tensor(
                                    sl,
                                    acc[:, dc * TOK + f * NSUB:dc * TOK + (f + 1) * NSUB],
                                    sbvt[:, 8 + dc:9 + dc], sl,
                                    OP.add, OP.add)
                            return fn
                        inc = ("sdv", 1) if dc == DC - 1 else None
                        p.emit("vector", mkf(), waits=w if dc == 0 else [], inc=inc, tag=f"fin{f}_{dc}")
                    ev_fin[f] = p.val("sdv")
                    ev_out[f] = p.emit("sync", (lambda f=f, par=par: lambda eng: eng.dma_start(
                        out=outv[:, :, f * NSUB:(f + 1) * NSUB].transpose([1, 0, 2]),
                        in_=xr[:, par * 4096:(par + 1) * 4096].rearrange("p (c n) -> p c n", c=DC)))(),
                        waits=[("sdv", ev_fin[f])], inc=("ssp", 16), tag=f"outs{f}")

                p.emit("sync", lambda eng: None, waits=[("ssp", p.val("ssp")), ("sw", p.val("sw"))],
                       tag="final_wait")

        p.check_deadlock()
        with nc.Block() as block:
            p.build(block)
    return nc


def kernel(L0_attn_x_t, L0_attn_w_i, L0_attn_w_f, L0_attn_w_h,
           L0_ffn_router_W, L0_ffn_router_b, L0_ffn_sW1, L0_ffn_sb1,
           L0_ffn_sW2, L0_ffn_sb2, L0_ffn_eW1, L0_ffn_eb1, L0_ffn_eW2, L0_ffn_eb2):
    x = np.asarray(L0_attn_x_t, dtype=np.float32)

    def fm(v):
        return np.ascontiguousarray(np.asarray(v, np.float32).reshape(8, 128).T)

    wif = np.concatenate([fm(L0_attn_w_f), fm(L0_attn_w_i), fm(L0_attn_w_h)], axis=1)
    sbv = np.concatenate([fm(L0_ffn_sb1), fm(L0_ffn_sb2)], axis=1)
    eb1p = np.ascontiguousarray(
        np.asarray(L0_ffn_eb1, np.float32).reshape(15, 8, 128).transpose(2, 0, 1).reshape(128, 120))
    eb2p = np.ascontiguousarray(np.asarray(L0_ffn_eb2, np.float32))
    rwp = np.ascontiguousarray(
        np.asarray(L0_ffn_router_W, np.float32).reshape(8, 128, 15).transpose(1, 0, 2).reshape(128, 120))
    rbp = np.ascontiguousarray(np.asarray(L0_ffn_router_b, np.float32).reshape(15, 1))
    sW1 = np.ascontiguousarray(np.asarray(L0_ffn_sW1, np.float32))
    sW2 = np.ascontiguousarray(np.asarray(L0_ffn_sW2, np.float32))
    eW1 = np.ascontiguousarray(np.asarray(L0_ffn_eW1, np.float32))
    eW2 = np.ascontiguousarray(np.asarray(L0_ffn_eW2, np.float32))

    nc = build_nc()
    in_maps = []
    for c in range(8):
        b, pp = c // 4, c % 4
        xcn = np.ascontiguousarray(x[b, pp * TOK:(pp + 1) * TOK, :].T)
        cm = np.zeros((128, 8), np.float32)
        for j in range(8):
            if j // 4 == b and j % 4 < pp:
                cm[:, j] = 1.0
        in_maps.append({
            "xc": xcn, "cmask": cm, "cmask1m": np.ascontiguousarray(1.0 - cm),
            "cm64": np.ascontiguousarray(np.repeat(cm, 8, axis=1)),
            "wif": wif, "sbv": sbv, "eb1p": eb1p, "eb2p": eb2p, "rwp": rwp, "rbp": rbp,
            "sW1": sW1, "sW2": sW2, "eW1": eW1, "eW2": eW2,
        })

    res = run_bass_kernel_spmd(nc, in_maps, core_ids=list(range(8)))
    out = np.empty((B, S, D), np.float32)
    for c in range(8):
        b, pp = c // 4, c % 4
        out[b, pp * TOK:(pp + 1) * TOK, :] = res.results[c]["out_fm"].T
    if DEBUG:
        kernel.last_results = res
    return out


# version 12
# speedup vs baseline: 1.0455x; 1.0335x over previous
import contextlib

import numpy as np

import concourse.bass as bass
import concourse.mybir as mybir
from concourse.bass_utils import run_bass_kernel_spmd

F32 = mybir.dt.float32
BF16 = mybir.dt.bfloat16
OP = mybir.AluOpType
AF = mybir.ActivationFunctionType
AX = mybir.AxisListType.X

B, S, D = 2, 8192, 1024
E, H = 15, 1024
DC = D // 128
HCN = H // 128
TOK = 2048
TS = 256
NTILE = TOK // TS
MSUB = 512
NMT = TOK // MSUB
NSUB = 512
NJ = TOK // NSUB
EPS = 1e-5
NEXP = 16

DEBUG = False


class Prog:

    ENGINES = ("sync", "tensor", "vector", "scalar", "gpsimd")

    def __init__(self, nc):
        self.nc = nc
        self.ops = {e: [] for e in self.ENGINES}
        self.sems = {}
        self.pending_gate = None
        self.gated = set()

    def set_gate(self, gate):
        self.pending_gate = gate
        self.gated = set()

    def add_sem(self, ctx, name):
        h = ctx.enter_context(self.nc.semaphore(name))
        self.sems[name] = [h, 0]

    def emit(self, eng, fn, waits=(), inc=None, tag=""):
        wl = [(s, v) for (s, v) in waits]
        if self.pending_gate is not None and eng not in self.gated:
            wl.append(self.pending_gate)
            self.gated.add(eng)
            if len(self.gated) == len(self.ENGINES):
                self.pending_gate = None
        self.ops[eng].append((fn, wl, inc, tag))
        if inc is not None:
            self.sems[inc[0]][1] += inc[1]
            return self.sems[inc[0]][1]
        return None

    def val(self, name):
        return self.sems[name][1]

    @staticmethod
    def _rv(v):
        return v() if callable(v) else v

    def check_deadlock(self):
        counts = {name: 0 for name in self.sems}
        ptrs = {e: 0 for e in self.ENGINES}
        progress = True
        while progress:
            progress = False
            for e in self.ENGINES:
                while ptrs[e] < len(self.ops[e]):
                    fn, waits, inc, tag = self.ops[e][ptrs[e]]
                    if all(counts[s] >= self._rv(v) for (s, v) in waits):
                        if inc is not None:
                            counts[inc[0]] += inc[1]
                        ptrs[e] += 1
                        progress = True
                    else:
                        break
        stuck = {e: ptrs[e] for e in self.ENGINES if ptrs[e] < len(self.ops[e])}
        if stuck:
            msgs = []
            for e, i in stuck.items():
                fn, waits, inc, tag = self.ops[e][i]
                blocked = [(s, self._rv(v), counts[s]) for (s, v) in waits
                           if counts[s] < self._rv(v)]
                msgs.append(f"{e}[{i}] tag={tag} blocked on {blocked}")
            raise RuntimeError("DEADLOCK in schedule:\n" + "\n".join(msgs))

    def build(self, block):
        def mk(e):
            def body(engobj):
                for fn, waits, inc, tag in self.ops[e]:
                    for (s, v) in waits:
                        engobj.wait_ge(self.sems[s][0], Prog._rv(v))
                    instr = fn(engobj)
                    if inc is not None:
                        assert instr is not None, f"inc without instruction: {e} {tag}"
                        instr.then_inc(self.sems[inc[0]][0], inc[1])
            return body

        block.sync(mk("sync"))
        block.tensor(mk("tensor"))
        block.vector(mk("vector"))
        block.scalar(mk("scalar"))
        block.gpsimd(mk("gpsimd"))


def build_nc():
    nc = bass.Bass("TRN2", target_bir_lowering=False, debug=False)

    xc = nc.dram_tensor("xc", [D, TOK], F32, kind="ExternalInput").ap()
    cmaskd = nc.dram_tensor("cmask", [128, 8], F32, kind="ExternalInput").ap()
    cm64d = nc.dram_tensor("cm64", [128, 64], F32, kind="ExternalInput").ap()
    cmask1d = nc.dram_tensor("cmask1m", [128, 8], F32, kind="ExternalInput").ap()
    wifd = nc.dram_tensor("wif", [128, 24], F32, kind="ExternalInput").ap()
    sbvd = nc.dram_tensor("sbv", [128, 16], F32, kind="ExternalInput").ap()
    eb1d = nc.dram_tensor("eb1p", [128, 120], F32, kind="ExternalInput").ap()
    eb2d = nc.dram_tensor("eb2p", [15, 1024], F32, kind="ExternalInput").ap()
    rwd = nc.dram_tensor("rwp", [128, 120], F32, kind="ExternalInput").ap()
    rbd = nc.dram_tensor("rbp", [15, 1], F32, kind="ExternalInput").ap()
    sW1d = nc.dram_tensor("sW1", [1024, 1024], F32, kind="ExternalInput").ap()
    sW2d = nc.dram_tensor("sW2", [1024, 1024], F32, kind="ExternalInput").ap()
    eW1d = nc.dram_tensor("eW1", [15, 1024, 1024], F32, kind="ExternalInput").ap()
    eW2d = nc.dram_tensor("eW2", [15, 1024, 1024], F32, kind="ExternalInput").ap()

    out_fm = nc.dram_tensor("out_fm", [D, TOK], F32, kind="ExternalOutput").ap()

    x1sp = nc.dram_tensor("x1sp", [D, TOK], F32, kind="Internal").ap()
    hsp = nc.dram_tensor("hsp", [D, TOK], F32, kind="Internal").ap()
    asp = nc.dram_tensor("asp", [D, TOK], BF16, kind="Internal").ap()
    agin = nc.dram_tensor("agin", [2, 1024], F32, kind="Internal").ap()
    agout = nc.dram_tensor("agout", [16, 1024], F32, kind="Internal").ap()

    identd = nc.inline_tensor(np.eye(128, dtype=np.float32), "identc").ap()
    onesk1d = nc.inline_tensor(np.ones((1, 128), dtype=np.float32), "onesk1c").ap()
    onesm1d = nc.inline_tensor(np.ones((128, 1), dtype=np.float32), "onesm1c").ap()
    oneh_np = np.zeros((15, 15 * 128), dtype=np.float32)
    for e in range(15):
        oneh_np[e, e * 128:(e + 1) * 128] = 1.0
    import ml_dtypes
    onehd = nc.inline_tensor(oneh_np.astype(ml_dtypes.bfloat16), "onehc").ap()

    dbg = {}
    if DEBUG:
        for name, shape in [
            ("dbg_ln1", [D, TOK]), ("dbg_f", [D, TOK]), ("dbg_v", [D, TOK]),
            ("dbg_h", [D, TOK]), ("dbg_ln2", [D, TOK]), ("dbg_cw", [15, TOK]),
            ("dbg_acc", [D, TOK]), ("dbg_csum", [128, 128]), ("dbg_cvec", [128, 8]),
            ("dbg_car", [128, 16]),
        ]:
            dbg[name] = nc.dram_tensor(name, shape, F32, kind="ExternalOutput").ap()

    def cpview(ap):
        return ap.rearrange("(c p) n -> c p n", p=128)

    xcv, hspv, aspv, x1v, outv = map(cpview, (xc, hsp, asp, x1sp, out_fm))

    p = Prog(nc)

    with contextlib.ExitStack() as ctx:
        for s in ("sx", "ssp", "sw", "scc", "smm", "sdv", "sac", "sgp"):
            p.add_sem(ctx, s)

        sb = nc.alloc_sbuf_tensor
        identt = sb("identt", [128, 128], F32)
        onesk1t = sb("onesk1t", [1, 128], F32)
        onesm1t = sb("onesm1t", [128, 1], F32)
        wift = sb("wift", [128, 24], F32)
        sbvt = sb("sbvt", [128, 16], F32)
        eb1t = sb("eb1t", [128, 120], F32)
        rwt = sb("rwt", [128, 120], F32)
        rwbt = sb("rwbt", [128, 120], BF16)
        rbt = sb("rbt", [15, 1], F32)
        cmt = sb("cmt", [128, 8], F32)
        cm1t = sb("cm1t", [128, 8], F32)
        cm64t = sb("cm64t", [128, 64], F32)
        carA = sb("carA", [128, 8], F32)
        carH = sb("carH", [128, 8], F32)
        cvec = sb("cvec", [128, 8], F32)
        csumA = sb("csumA", [128, 64], F32)
        csumB = sb("csumB", [128, 64], F32)
        lgb = sb("lgb", [128, 352], F32)
        cwfb = sb("cwfb", [15, TOK], BF16)
        lgsb = sb("lgsb", [15, 512], F32)
        ln2b = sb("ln2b", [128, DC * TOK], BF16)

        pb = [ctx.enter_context(nc.psum_tensor(f"pb{i}", [128, 512], F32))
              for i in range(8)]

        def d(fn):
            return fn

        consts = [
            (identt, identd), (onesk1t, onesk1d), (onesm1t, onesm1d),
            (wift, wifd), (sbvt, sbvd), (eb1t, eb1d), (rwt, rwd), (rbt, rbd),
            (cmt, cmaskd), (cm1t, cmask1d), (cm64t, cm64d),
        ]
        for tt, dd_ in consts:
            p.emit("sync", (lambda t=tt, dd=dd_: lambda eng: eng.dma_start(out=t[:], in_=dd))(),
                   inc=("sx", 16), tag="const")
        consts_loaded = p.val("sx")

        rwcast = p.emit("vector", lambda eng: eng.tensor_copy(rwbt[:], rwt[:]),
                        waits=[("sx", consts_loaded)], inc=("sdv", 1), tag="rwcast")
        p.emit("vector", lambda eng: eng.memset(carA[:], 1.0), tag="carA0")
        p.emit("vector", lambda eng: eng.memset(carH[:], 0.0), inc=("sdv", 1), tag="carH0")
        carinit = p.val("sdv")

        with contextlib.ExitStack() as sctx:
            srow = sctx.enter_context(nc.sbuf_tensor("srow", [1, 2048], F32))
            xb = sctx.enter_context(nc.sbuf_tensor("xb", [128, 3 * 2048], F32))
            ln1 = sctx.enter_context(nc.sbuf_tensor("ln1", [128, 2 * 2048], F32))
            t1 = sctx.enter_context(nc.sbuf_tensor("t1", [128, 2 * 2048], F32))
            t2 = sctx.enter_context(nc.sbuf_tensor("t2", [128, 2 * 2048], F32))
            t3 = sctx.enter_context(nc.sbuf_tensor("t3", [128, 2 * 2048], F32))
            t4 = sctx.enter_context(nc.sbuf_tensor("t4", [128, 2048], F32))
            hb = sctx.enter_context(nc.sbuf_tensor("hb", [128, 2048], F32))
            ab = sctx.enter_context(nc.sbuf_tensor("ab", [128, 2 * 2048], F32))
            xsq = sctx.enter_context(nc.sbuf_tensor("xsq", [128, 2048], F32))
            wtl = sctx.enter_context(nc.sbuf_tensor("wtl", [128, 3 * 2048], F32))

            p.emit("vector", lambda eng: eng.memset(pb[4][:], 0.0), tag="zt0")
            for wsel in range(3):
                for dc in range(DC):
                    def mkw(wsel=wsel, dc=dc):
                        def fn(eng):
                            return eng.tensor_scalar(
                                wtl[:, wsel * 2048 + dc * TS:(wsel * 2048 + (dc + 1) * TS)],
                                pb[4][:, 0:TS], 0.0, wift[:, wsel * 8 + dc:wsel * 8 + dc + 1],
                                OP.mult, OP.add)
                        return fn
                    lastw = p.emit("vector", mkw(),
                                   waits=[("sx", consts_loaded)], inc=("sdv", 1), tag=f"wtl{wsel}_{dc}")
            wtl_done = p.val("sdv")

            ev_lnapply = [0] * NTILE
            ev_spill_h = [0] * NTILE
            ev_spill_a = [0] * NTILE
            ev_scan = [0] * NTILE
            ev_xload = [0] * NTILE
            ev_sq_used = [0] * NTILE
            ev_rows_dve = [0] * NTILE

            ev_bcast = [0] * NTILE
            def pre_sq_stats(k):
                xsl = xb[:, (k % 3) * 2048:((k % 3) + 1) * 2048]
                xqsl = xsq[:, 0:2048]
                waits = [("sx", lambda k=k: ev_xload[k])]
                if k >= 1:
                    waits.append(("smm", ev_sq_used[k - 1]))
                p.emit("scalar", (lambda x_=xsl, q_=xqsl: lambda eng: eng.activation(
                    q_, x_, AF.Square, bias=0.0, scale=1.0))(),
                    waits=waits, inc=("sac", 1), tag=f"sq{k}")
                sq_done = p.val("sac")

                waits = [("sac", sq_done)]
                if k >= 1:
                    waits.append(("sdv", ev_rows_dve[k - 1]))
                for dc in range(DC):
                    def mks(dc=dc, x_=xsl):
                        def fn(eng):
                            return eng.matmul(pb[0][0:1, 0:TS], onesm1t[:],
                                              x_[:, dc * TS:(dc + 1) * TS],
                                              start=(dc == 0), stop=(dc == DC - 1))
                        return fn
                    p.emit("tensor", mks(), waits=waits if dc == 0 else [], tag=f"summ{k}_{dc}")
                for dc in range(DC):
                    def mks2(dc=dc, q_=xqsl):
                        def fn(eng):
                            return eng.matmul(pb[1][0:1, 0:TS], onesm1t[:],
                                              q_[:, dc * TS:(dc + 1) * TS],
                                              start=(dc == 0), stop=(dc == DC - 1))
                        return fn
                    inc = ("smm", 1) if dc == DC - 1 else None
                    p.emit("tensor", mks2(), inc=inc, tag=f"sumsq{k}_{dc}")
                ev_sq_used[k] = p.val("smm")

            def pre_rows_bcast(k):
                par = k % 2
                mb = pb[2] if par == 0 else pb[5]
                rb = pb[3] if par == 0 else pb[6]
                base = par * 1024
                mrow = srow[0:1, base:base + TS]
                arow = srow[0:1, base + 256:base + 256 + TS]
                trow = srow[0:1, base + 512:base + 512 + TS]
                rrow = srow[0:1, base + 768:base + 768 + TS]

                waits = [("smm", ev_sq_used[k])]
                if k >= 2:
                    waits.append(("smm", ev_bcast[k - 2]))
                p.emit("vector", (lambda mr=mrow: lambda eng: eng.tensor_single_scalar(
                    mr, pb[0][0:1, 0:TS], 1.0 / D, OP.mult))(),
                    waits=waits, tag=f"rowm{k}")
                p.emit("vector", (lambda ar=arow: lambda eng: eng.tensor_scalar(
                    ar, pb[1][0:1, 0:TS], 1.0 / D, EPS, OP.mult, OP.add))(), tag=f"rowa{k}")
                p.emit("vector", (lambda mr=mrow, tr=trow: lambda eng: eng.tensor_tensor(
                    tr, mr, mr, OP.mult))(), tag=f"rowt{k}")
                ev_rows_dve[k] = p.emit("vector", (lambda ar=arow, tr=trow: lambda eng: eng.tensor_tensor(
                    ar, ar, tr, OP.subtract))(), inc=("sdv", 1), tag=f"rowb{k}")
                p.emit("scalar", (lambda ar=arow, tr=trow: lambda eng: eng.activation(
                    tr, ar, AF.Ln, bias=0.0, scale=1.0))(),
                    waits=[("sdv", ev_rows_dve[k])], tag=f"rowln{k}")
                rowsr = p.emit("scalar", (lambda rr=rrow, tr=trow: lambda eng: eng.activation(
                    rr, tr, AF.Exp, bias=0.0, scale=-0.5))(), inc=("sac", 1), tag=f"rowr{k}")

                waits = [("sac", rowsr)]
                if k >= 2:
                    waits.append(("sdv", ev_lnapply[k - 2]))
                p.emit("tensor", (lambda mr=mrow, mb=mb: lambda eng: eng.matmul(
                    mb[:, 0:TS], onesk1t[:], mr, start=True, stop=True))(),
                    waits=waits, tag=f"bcm{k}")
                ev_bcast[k] = p.emit("tensor", (lambda rr=rrow, rb=rb: lambda eng: eng.matmul(
                    rb[:, 0:TS], onesk1t[:], rr, start=True, stop=True))(),
                    inc=("smm", 1), tag=f"bcr{k}")

            def pre_load(k):
                xsl = xb[:, (k % 3) * 2048:((k % 3) + 1) * 2048]
                waits = []
                if k >= 3:
                    waits.append(("ssp", ev_spill_h[k - 3]))
                src = xcv[:, :, k * TS:(k + 1) * TS].transpose([1, 0, 2])
                dst = xsl.rearrange("p (c n) -> p c n", c=DC)
                ev_xload[k] = p.emit(
                    "sync", (lambda s_=src, d_=dst: lambda eng: eng.dma_start(out=d_, in_=s_))(),
                    waits=waits, inc=("sx", 16), tag=f"xload{k}")

            def mainpass(k):
                par = k % 2
                xsl = xb[:, (k % 3) * 2048:((k % 3) + 1) * 2048]
                l1sl = ln1[:, par * 2048:(par + 1) * 2048]
                t1sl = t1[:, par * 2048:(par + 1) * 2048]
                fsl = t2[:, par * 2048:(par + 1) * 2048]
                t3sl = t3[:, par * 2048:(par + 1) * 2048]
                vsl = t4[:, 0:2048]
                hsl = hb[:, 0:2048]
                asl = ab[:, par * 2048:(par + 1) * 2048]
                mb = pb[2] if par == 0 else pb[5]
                rb = pb[3] if par == 0 else pb[6]
                H = 1024

                def hsub(ap_, h):
                    return ap_[:, h * H:(h + 1) * H]

                for dc in range(DC):
                    def mka(dc=dc, x_=xsl, l_=l1sl, mb=mb):
                        def fn(eng):
                            return eng.tensor_tensor(l_[:, dc * TS:(dc + 1) * TS],
                                                     x_[:, dc * TS:(dc + 1) * TS],
                                                     mb[:, 0:TS], OP.subtract)
                        return fn
                    p.emit("vector", mka(), waits=[("smm", ev_bcast[k])] if dc == 0 else [],
                           tag=f"lna{k}_{dc}")
                lnh = [0, 0]
                for dc in range(DC):
                    def mkb(dc=dc, l_=l1sl, rb=rb):
                        def fn(eng):
                            sl = l_[:, dc * TS:(dc + 1) * TS]
                            return eng.tensor_tensor(sl, sl, rb[:, 0:TS], OP.mult)
                        return fn
                    inc = ("sdv", 1) if dc in (3, 7) else None
                    p.emit("vector", mkb(), inc=inc, tag=f"lnb{k}_{dc}")
                    if dc == 3:
                        lnh[0] = p.val("sdv")
                ev_lnapply[k] = lnh[1] = p.val("sdv")

                if DEBUG:
                    dmaf = (lambda k=k, l_=l1sl: lambda eng: eng.dma_start(
                        out=cpview(dbg["dbg_ln1"])[:, :, k * TS:(k + 1) * TS].transpose([1, 0, 2]),
                        in_=l_.rearrange("p (c n) -> p c n", c=DC)))()
                    gv = p.emit("sync", dmaf, waits=[("sdv", ev_lnapply[k])], inc=("ssp", 16),
                                tag=f"dbgln1_{k}")
                    p.set_gate(("ssp", gv))

                gfg = [0, 0]
                for h in range(2):
                    gw = [("sdv", max(lnh[h], wtl_done))]
                    p.emit("gpsimd", (lambda l_=l1sl, o_=t1sl, h=h: lambda eng: eng.tensor_tensor(
                        hsub(o_, h), hsub(l_, h), wtl[:, h * H:(h + 1) * H], OP.mult))(),
                        waits=gw, tag=f"fg{k}_{h}")
                    gfg[h] = p.emit("gpsimd", (lambda l_=l1sl, o_=t3sl, h=h: lambda eng: eng.tensor_tensor(
                        hsub(o_, h), hsub(l_, h), wtl[:, 2048 + h * H:2048 + (h + 1) * H], OP.mult))(),
                        inc=("sgp", 1), tag=f"ig{k}_{h}")

                spd = [0, 0]
                for h in range(2):
                    p.emit("scalar", (lambda o_=t1sl, h=h: lambda eng: eng.activation(
                        hsub(o_, h), hsub(o_, h), AF.Exp, bias=0.0, scale=-1.0))(),
                        waits=[("sgp", gfg[h])], tag=f"e1{k}_{h}")
                    p.emit("scalar", (lambda o_=t3sl, h=h: lambda eng: eng.activation(
                        hsub(o_, h), hsub(o_, h), AF.Exp, bias=0.0, scale=-1.0))(), tag=f"e2{k}_{h}")
                    p.emit("scalar", (lambda o_=t1sl, h=h: lambda eng: eng.activation(
                        hsub(o_, h), hsub(o_, h), AF.Ln, bias=1.0, scale=1.0))(), tag=f"l1{k}_{h}")
                    spd[h] = p.emit("scalar", (lambda o_=t3sl, h=h: lambda eng: eng.activation(
                        hsub(o_, h), hsub(o_, h), AF.Ln, bias=1.0, scale=1.0))(),
                        inc=("sac", 1), tag=f"l2{k}_{h}")

                gdiff = [0, 0]
                hgd = [0, 0]
                for h in range(2):
                    gdiff[h] = p.emit("gpsimd", (lambda a_=t1sl, b_=t3sl, h=h: lambda eng:
                                      eng.tensor_tensor(hsub(a_, h), hsub(a_, h), hsub(b_, h),
                                                        OP.subtract))(),
                                      waits=[("sac", spd[h])], inc=("sgp", 1), tag=f"diff{k}_{h}")
                for h in range(2):
                    hgd[h] = p.emit("vector", (lambda l_=l1sl, h=h: lambda eng: eng.tensor_tensor(
                        hsub(l_, h), hsub(l_, h), wtl[:, 4096 + h * H:4096 + (h + 1) * H], OP.mult))(),
                        waits=[("sgp", gfg[h])], inc=("sdv", 1), tag=f"hg{k}_{h}")

                sgd = [0, 0]
                for h in range(2):
                    p.emit("scalar", (lambda f_=fsl, a_=t1sl, h=h: lambda eng: eng.activation(
                        hsub(f_, h), hsub(a_, h), AF.Sigmoid, bias=0.0, scale=-1.0))(),
                        waits=[("sdv", hgd[h]), ("sgp", gdiff[h])], tag=f"sf{k}_{h}")
                    sgd[h] = p.emit("scalar", (lambda l_=l1sl, b_=t3sl, h=h: lambda eng: eng.activation(
                        hsub(b_, h), hsub(l_, h), AF.Sigmoid, bias=0.0, scale=1.0))(),
                        inc=("sac", 1), tag=f"sh{k}_{h}")

                iv = [0, 0]
                vv = [0, 0]
                for h in range(2):
                    p.emit("vector", (lambda l_=l1sl, h=h: lambda eng: eng.tensor_single_scalar(
                        hsub(l_, h), hsub(l_, h), 0.5, OP.add))(),
                        waits=[("sac", sgd[h])], tag=f"h05{k}_{h}")
                    p.emit("vector", (lambda l_=l1sl, b_=t3sl, h=h: lambda eng: eng.tensor_tensor(
                        hsub(b_, h), hsub(l_, h), hsub(b_, h), OP.max))(), tag=f"g{k}_{h}")
                    iv[h] = p.emit("vector", (lambda f_=fsl, a_=t1sl, h=h: lambda eng: eng.tensor_scalar(
                        hsub(a_, h), hsub(f_, h), -1.0, 1.0, OP.mult, OP.add))(),
                        inc=("sdv", 1), tag=f"i{k}_{h}")
                for h in range(2):
                    vw = [("sdv", iv[h])]
                    if k >= 1:
                        vw.append(("sdv", ev_scan[k - 1]))
                    vv[h] = p.emit("gpsimd", (lambda v_=vsl, a_=t1sl, b_=t3sl, h=h: lambda eng:
                                   eng.tensor_tensor(hsub(v_, h), hsub(a_, h), hsub(b_, h),
                                                     OP.mult))(),
                                   waits=vw, inc=("sgp", 1), tag=f"v{k}_{h}")

                for h in range(2):
                    waits = [("sdv", carinit), ("sgp", vv[h])]
                    if k >= 2 and h == 0 and k - 2 < 2:
                        waits += [("sw", ev_spill_a[k - 2])]
                    first = True
                    for dc in (range(4 * h, 4 * h + 4) if k < 2 else []):
                        def mksa(dc=dc, f_=fsl, a_=asl):
                            def fn(eng):
                                return eng.tensor_tensor_scan(
                                    a_[:, dc * TS:(dc + 1) * TS], f_[:, dc * TS:(dc + 1) * TS],
                                    pb[4][:, 0:TS], carA[:, dc:dc + 1], OP.mult, OP.add)
                            return fn
                        p.emit("vector", mksa(), waits=waits if first else [], tag=f"sa{k}_{dc}")
                        first = False
                    for dc in range(4 * h, 4 * h + 4):
                        def mksh(dc=dc, f_=fsl, v_=vsl, h_=hsl):
                            def fn(eng):
                                return eng.tensor_tensor_scan(
                                    h_[:, dc * TS:(dc + 1) * TS], f_[:, dc * TS:(dc + 1) * TS],
                                    v_[:, dc * TS:(dc + 1) * TS], carH[:, dc:dc + 1], OP.mult, OP.add)
                            return fn
                        p.emit("vector", mksh(), waits=waits if first else [], tag=f"shs{k}_{dc}")
                        first = False
                if k < 2:
                    p.emit("vector", lambda eng, a_=asl: eng.tensor_copy(
                        carA[:], a_.rearrange("p (c n) -> p c n", c=DC)[:, :, TS - 1]), tag=f"ca{k}")
                p.emit("vector", lambda eng, h_=hsl: eng.tensor_copy(
                    carH[:], h_.rearrange("p (c n) -> p c n", c=DC)[:, :, TS - 1]), tag=f"ch{k}")
                ev_scan[k] = p.emit("vector", (lambda x_=xsl, h_=hsl: lambda eng: eng.tensor_tensor(
                    x_, x_, h_, OP.add))(), inc=("sdv", 1), tag=f"x1p{k}")

                hdst = hspv[:, :, k * TS:(k + 1) * TS].transpose([1, 0, 2])
                ev_spill_h[k] = p.emit(
                    "sync", (lambda d_=hdst, s_=xsl: lambda eng: eng.dma_start(
                        out=d_, in_=s_.rearrange("p (c n) -> p c n", c=DC)))(),
                    waits=[("sdv", ev_scan[k])], inc=("ssp", 16), tag=f"hspill{k}")
                if k < 2:
                    adst = aspv[:, :, k * TS:(k + 1) * TS].transpose([1, 0, 2])
                    ev_spill_a[k] = p.emit(
                        "gpsimd", (lambda d_=adst, s_=asl: lambda eng: eng.dma_start(
                            out=d_, in_=s_.rearrange("p (c n) -> p c n", c=DC)))(),
                        waits=[("sdv", ev_scan[k])], inc=("sw", 16), tag=f"aspill{k}")
                else:
                    ev_spill_a[k] = ev_spill_a[1]

            for k in range(NTILE + 2):
                if k < NTILE:
                    pre_load(k)
                    pre_sq_stats(k)
                if k >= 2:
                    mainpass(k - 2)
                if k < NTILE:
                    pre_rows_bcast(k)

            agin_vA = agin[0:1, :].rearrange("o (d q) -> (o q) d", q=128)
            agin_vB = agin[1:2, :].rearrange("o (d q) -> (o q) d", q=128)
            sA = p.emit("sync", lambda eng: eng.dma_start(out=agin_vA, in_=carA[:],
                                                          allow_slow_non_contiguous=True),
                        waits=[("sdv", ev_scan[NTILE - 1])], inc=("ssp", 16), tag="sumA")
            sB = p.emit("sync", lambda eng: eng.dma_start(out=agin_vB, in_=carH[:],
                                                          allow_slow_non_contiguous=True),
                        inc=("ssp", 16), tag="sumB")
            cc = p.emit("gpsimd", lambda eng: eng.collective_compute(
                "AllGather", OP.bypass, ins=[agin], outs=[agout],
                replica_groups=[[0, 1, 2, 3, 4, 5, 6, 7]]),
                waits=[("ssp", sB)], inc=("scc", 1), tag="allgather")
            agv = agout.rearrange("(j a) (d q) -> j a d q", a=2, q=128)
            gB = 0
            for j in range(8):
                for ab, dst in ((0, csumA), (1, csumB)):
                    gB = p.emit("scalar", (lambda j=j, ab=ab, dst=dst: lambda eng: eng.dma_start(
                        out=dst[:, j * 8:(j + 1) * 8],
                        in_=agv[j, ab, :, :].transpose([1, 0]),
                        allow_slow_non_contiguous=True))(),
                        waits=[("scc", cc)] if (j == 0 and ab == 0) else [],
                        inc=("sx", 16), tag=f"gath{ab}_{j}")

            if DEBUG:
                p.emit("sync", lambda eng: eng.dma_start(out=dbg["dbg_car"][:, 0:8], in_=carA[:]),
                       waits=[("sdv", ev_scan[NTILE - 1])], inc=("ssp", 16), tag="dbgcarA")
                p.emit("sync", lambda eng: eng.dma_start(out=dbg["dbg_car"][:, 8:16], in_=carH[:]),
                       inc=("ssp", 16), tag="dbgcarB")
                p.emit("sync", lambda eng: eng.dma_start(out=dbg["dbg_csum"][:, 0:64], in_=csumA[:]),
                       waits=[("sx", gB)], inc=("ssp", 16), tag="dbgcsA")
                gv = p.emit("sync", lambda eng: eng.dma_start(out=dbg["dbg_csum"][:, 64:128], in_=csumB[:]),
                            inc=("ssp", 16), tag="dbgcsB")
                p.set_gate(("ssp", gv))
            p.emit("vector", lambda eng: eng.memset(cvec[:], 0.0), waits=[("sx", gB)], tag="cv0")
            p.emit("vector", lambda eng: eng.tensor_single_scalar(
                lgb[:, 96:160], csumA[:], -1.0, OP.add), tag="cAm1")
            p.emit("vector", lambda eng: eng.tensor_tensor(
                lgb[:, 96:160], lgb[:, 96:160], cm64t[:], OP.mult), tag="cAm2")
            p.emit("vector", lambda eng: eng.tensor_tensor(
                lgb[:, 160:224], csumB[:], cm64t[:], OP.mult), tag="cBm")
            p.emit("vector", lambda eng: eng.drain(), tag="drn")
            for j in range(8):
                p.emit("vector", (lambda j=j: lambda eng: eng.tensor_tensor(
                    lgb[:, 88:96], cvec[:], lgb[:, 96 + j * 8:104 + j * 8], OP.mult))(),
                    tag=f"cm{j}")
                p.emit("vector", lambda eng: eng.drain(), tag="drn")
                p.emit("vector", lambda eng: eng.tensor_tensor(
                    cvec[:], cvec[:], lgb[:, 88:96], OP.add), tag=f"cu{j}")
                p.emit("vector", lambda eng: eng.drain(), tag="drn")
                p.emit("vector", (lambda j=j: lambda eng: eng.tensor_tensor(
                    cvec[:], cvec[:], lgb[:, 160 + j * 8:168 + j * 8], OP.add))(),
                    tag=f"cp{j}")
                p.emit("vector", lambda eng: eng.drain(), tag="drn")
            cdone = p.emit("vector", lambda eng: eng.tensor_copy(lgb[:, 0:8], cvec[:]),
                           inc=("sdv", 1), tag="cdone")
            if DEBUG:
                gv = p.emit("sync", lambda eng: eng.dma_start(out=dbg["dbg_cvec"], in_=cvec[:]),
                            waits=[("sdv", cdone)], inc=("ssp", 16), tag="dbgcvec")
                p.set_gate(("ssp", gv))
            spills_all = p.val("ssp")
            aspills_all = p.val("sw")

        with contextlib.ExitStack() as mctx:
            srow = mctx.enter_context(nc.sbuf_tensor("srow2", [1, 2048], F32))
            xm = mctx.enter_context(nc.sbuf_tensor("xm", [128, 2 * 4096], F32))
            am = mctx.enter_context(nc.sbuf_tensor("am", [128, 2 * 4096], BF16))
            xsqm = mctx.enter_context(nc.sbuf_tensor("xsqm", [128, 4096], BF16))
            x1bf = mctx.enter_context(nc.sbuf_tensor("x1bf", [128, 4096], BF16))
            onesmb = mctx.enter_context(nc.sbuf_tensor("onesmb", [128, 1], BF16))

            ev_ln2 = [0] * NMT
            ev_sqm_used = [0] * NMT
            ev_rows2 = [0] * NMT
            ev_x1sp = [0] * NMT
            cwstate = {"lgevac": 0, "cwout": 0}

            def emit_cw(j4):
                for dc in range(DC):
                    def mkl(dc=dc, j4=j4):
                        def fn(eng):
                            return eng.matmul(pb[6][0:15, :],
                                              rwbt[:, dc * 15:(dc + 1) * 15],
                                              ln2b[:, dc * TOK + j4 * NSUB:dc * TOK + (j4 + 1) * NSUB],
                                              start=(dc == 0), stop=(dc == DC - 1))
                        return fn
                    w = []
                    if dc == 0:
                        w = [("sdv", max(ev_ln2[j4], rwcast, cwstate["cwout"]))]
                        if j4 > 0:
                            w.append(("sac", cwstate["lgevac"]))
                    inc = ("smm", 1) if dc == DC - 1 else None
                    p.emit("tensor", mkl(), waits=w, inc=inc, tag=f"lg{j4}_{dc}")
                lgmm = p.val("smm")
                lgevac = p.emit("vector", lambda eng: eng.tensor_scalar_add(
                    lgsb[:], pb[6][0:15, :], rbt[:, 0:1]),
                    waits=[("smm", lgmm)], inc=("sac", 1), tag=f"lgev{j4}")
                cwstate["lgevac"] = lgevac

                w = [("sac", lgevac)]
                if j4 > 0:
                    w.append(("sdv", cwstate["cwout"]))
                tf = 0
                for g2 in range(4):
                    tf = p.emit("tensor", (lambda g2=g2: lambda eng: eng.transpose(
                        pb[7][:, g2 * 15:(g2 + 1) * 15], lgsb[:, g2 * 128:(g2 + 1) * 128],
                        identt[0:15, 0:15]))(),
                        waits=w if g2 == 0 else [], inc=("smm", 1) if g2 == 3 else None,
                        tag=f"trf{j4}_{g2}")
                def rounds1(g2):
                    L = g2 * 88
                    yield lambda eng: eng.tensor_copy(lgb[:, L:L + 15], pb[7][:, g2 * 15:(g2 + 1) * 15])
                    yield lambda eng: eng.reduce_max(lgb[:, L + 15:L + 16], lgb[:, L:L + 15],
                                                     axis=AX, negate=True)
                    yield lambda eng: eng.tensor_single_scalar(lgb[:, L + 16:L + 17],
                                                               lgb[:, L + 15:L + 16], -1.0, OP.mult)
                    yield lambda eng: eng.tensor_scalar(lgb[:, L + 21:L + 36], lgb[:, L:L + 15],
                                                        lgb[:, L + 16:L + 17], -1e30,
                                                        OP.is_equal, OP.mult)
                    yield lambda eng: eng.tensor_tensor(lgb[:, L + 36:L + 51], lgb[:, L:L + 15],
                                                        lgb[:, L + 21:L + 36], OP.add)
                    yield lambda eng: eng.reduce_max(lgb[:, L + 17:L + 18], lgb[:, L + 36:L + 51],
                                                     axis=AX, negate=True)
                    yield lambda eng: eng.tensor_single_scalar(lgb[:, L + 18:L + 19],
                                                               lgb[:, L + 17:L + 18], -1.0, OP.mult)
                    yield lambda eng: eng.tensor_scalar(lgb[:, L + 21:L + 36], lgb[:, L + 36:L + 51],
                                                        lgb[:, L + 18:L + 19], -1e30,
                                                        OP.is_equal, OP.mult)
                    yield lambda eng: eng.tensor_tensor(lgb[:, L + 36:L + 51], lgb[:, L + 36:L + 51],
                                                        lgb[:, L + 21:L + 36], OP.add)
                    yield lambda eng: eng.reduce_max(lgb[:, L + 17:L + 18], lgb[:, L + 36:L + 51],
                                                     axis=AX, negate=True)
                    yield lambda eng: eng.tensor_single_scalar(lgb[:, L + 18:L + 19],
                                                               lgb[:, L + 17:L + 18], -1.0, OP.mult)
                    yield lambda eng: eng.tensor_single_scalar(lgb[:, L + 21:L + 36], lgb[:, L:L + 15],
                                                               lgb[:, L + 18:L + 19], OP.is_ge)

                gens = [rounds1(g2) for g2 in range(4)]
                for r in range(12):
                    for g2 in range(4):
                        fn = next(gens[g2])
                        inc = ("sdv", 1) if (r == 11 and g2 == 3) else None
                        p.emit("vector", fn,
                               waits=[("smm", tf)] if (r == 0 and g2 == 0) else [],
                               inc=inc, tag=f"cwr1_{j4}_{r}_{g2}")
                r1d = p.val("sdv")
                exd = 0
                for g2 in range(4):
                    L = g2 * 88
                    exd = p.emit("scalar", (lambda L=L: lambda eng: eng.activation(
                        lgb[:, L + 51:L + 66], lgb[:, L:L + 15], AF.Exp,
                        bias=lgb[:, L + 15:L + 16], scale=1.0))(),
                        waits=[("sdv", r1d)] if g2 == 0 else [],
                        inc=("sac", 1) if g2 == 3 else None, tag=f"cwex{j4}_{g2}")

                def rounds2(g2):
                    L = g2 * 88
                    yield lambda eng: eng.reduce_sum(lgb[:, L + 19:L + 20], lgb[:, L + 51:L + 66], axis=AX)
                    yield lambda eng: eng.reciprocal(lgb[:, L + 20:L + 21], lgb[:, L + 19:L + 20])
                    yield lambda eng: eng.tensor_single_scalar(lgb[:, L + 51:L + 66],
                                                               lgb[:, L + 51:L + 66],
                                                               lgb[:, L + 20:L + 21], OP.mult)
                    yield lambda eng: eng.tensor_tensor(lgb[:, L + 66:L + 81], lgb[:, L + 51:L + 66],
                                                        lgb[:, L + 21:L + 36], OP.mult)

                gens = [rounds2(g2) for g2 in range(4)]
                for r in range(4):
                    for g2 in range(4):
                        fn = next(gens[g2])
                        inc = ("sdv", 1) if (r == 3 and g2 == 3) else None
                        p.emit("vector", fn, waits=[("sac", exd)] if (r == 0 and g2 == 0) else [],
                               inc=inc, tag=f"cwr2_{j4}_{r}_{g2}")
                r2d = p.val("sdv")
                tb = 0
                for g2 in range(4):
                    L = g2 * 88
                    tb = p.emit("tensor", (lambda L=L, g2=g2: lambda eng: eng.transpose(
                        pb[6][0:15, g2 * 128:(g2 + 1) * 128], lgb[:, L + 66:L + 81], identt[:]))(),
                        waits=[("sdv", r2d)] if g2 == 0 else [],
                        inc=("smm", 1) if g2 == 3 else None, tag=f"trb{j4}_{g2}")
                cwstate["cwout"] = p.emit("vector", (lambda j4=j4: lambda eng: eng.tensor_copy(
                    cwfb[0:15, j4 * 512:(j4 + 1) * 512], pb[6][0:15, 0:512]))(),
                    waits=[("smm", tb)], inc=("sdv", 1), tag=f"cwout{j4}")

            for m in range(NMT):
                par = m % 2
                xsl = xm[:, par * 4096:(par + 1) * 4096]
                asl = am[:, par * 4096:(par + 1) * 4096]
                MS = MSUB

                waits = [("ssp", spills_all), ("sw", aspills_all)]
                if m >= 2:
                    waits.append(("sdv", ev_ln2[m - 2]))
                    if m - 2 == 0:
                        waits.append(("ssp", ev_x1sp[0]))
                xld = p.emit("sync", (lambda m=m, d_=xsl: lambda eng: eng.dma_start(
                    out=d_.rearrange("p (c n) -> p c n", c=DC),
                    in_=hspv[:, :, m * MSUB:(m + 1) * MSUB].transpose([1, 0, 2])))(),
                    waits=waits, inc=("sx", 16), tag=f"mx{m}")
                if m == 0:
                    ald = p.emit("sync", (lambda m=m, d_=asl: lambda eng: eng.dma_start(
                        out=d_.rearrange("p (c n) -> p c n", c=DC),
                        in_=aspv[:, :, m * MSUB:(m + 1) * MSUB].transpose([1, 0, 2])))(),
                        inc=("sx", 16), tag=f"ma{m}")
                    for dc in range(DC):
                        def mkf(dc=dc, x_=xsl, a_=asl):
                            def fn(eng):
                                sl = slice(dc * MS, (dc + 1) * MS)
                                return eng.scalar_tensor_tensor(x_[:, sl], a_[:, sl],
                                                                cvec[:, dc:dc + 1],
                                                                x_[:, sl], OP.mult, OP.add)
                            return fn
                        w = [("sx", ald), ("sdv", cdone)] if dc == 0 else []
                        inc = ("sdv", 1) if dc == DC - 1 else None
                        p.emit("vector", mkf(), waits=w, inc=inc, tag=f"fix{m}_{dc}")
                    x1rdy = ("sdv", p.val("sdv"))
                else:
                    x1rdy = ("sx", xld)
                x1d = p.val("sdv")
                if DEBUG:
                    gv = p.emit("sync", (lambda m=m, x_=xsl: lambda eng: eng.dma_start(
                        out=cpview(dbg["dbg_h"])[:, :, m * MSUB:(m + 1) * MSUB].transpose([1, 0, 2]),
                        in_=x_.rearrange("p (c n) -> p c n", c=DC)))(),
                        waits=[x1rdy], inc=("ssp", 16), tag=f"dbgh{m}")
                    p.set_gate(("ssp", gv))

                if m == 0:
                    ev_x1sp[m] = p.emit("sync", (lambda m=m, s_=xsl: lambda eng: eng.dma_start(
                        out=x1v[:, :, m * MSUB:(m + 1) * MSUB].transpose([1, 0, 2]),
                        in_=s_.rearrange("p (c n) -> p c n", c=DC)))(),
                        waits=[x1rdy], inc=("ssp", 16), tag=f"x1sp{m}")

                waits = [x1rdy]
                if m >= 1:
                    waits.append(("smm", ev_sqm_used[m - 1]))
                    waits.append(("sdv", ev_ln2[m - 1]))
                if m == 0:
                    p.emit("vector", lambda eng: eng.tensor_copy(onesmb[:], onesm1t[:]),
                           tag="onesmb")
                bw = [x1rdy]
                if m >= 1:
                    bw.append(("smm", ev_sqm_used[m - 1]))
                mxbf = p.emit("vector", (lambda x_=xsl: lambda eng: eng.tensor_copy(
                    x1bf[:], x_))(), waits=bw, inc=("sdv", 1), tag=f"mxbf{m}")
                sqm = p.emit("scalar", (lambda x_=xsl: lambda eng: eng.activation(
                    xsqm[:], x_, AF.Square, bias=0.0, scale=1.0))(),
                    waits=waits, inc=("sac", 1), tag=f"msq{m}")
                waits = [("sac", sqm), ("sdv", mxbf)]
                if m >= 1:
                    waits.append(("sdv", ev_rows2[m - 1]))
                for dc in range(DC):
                    def mks(dc=dc):
                        def fn(eng):
                            return eng.matmul(pb[0][0:1, 0:MS], onesmb[:],
                                              x1bf[:, dc * MS:(dc + 1) * MS],
                                              start=(dc == 0), stop=(dc == DC - 1))
                        return fn
                    p.emit("tensor", mks(), waits=waits if dc == 0 else [], tag=f"msum{m}_{dc}")
                for dc in range(DC):
                    def mks2(dc=dc):
                        def fn(eng):
                            return eng.matmul(pb[1][0:1, 0:MS], onesmb[:],
                                              xsqm[:, dc * MS:(dc + 1) * MS],
                                              start=(dc == 0), stop=(dc == DC - 1))
                        return fn
                    inc = ("smm", 1) if dc == DC - 1 else None
                    p.emit("tensor", mks2(), inc=inc, tag=f"msumsq{m}_{dc}")
                ev_sqm_used[m] = p.val("smm")

                mrow = srow[0:1, 0:MS]
                arow = srow[0:1, 512:512 + MS]
                trow = srow[0:1, 1024:1024 + MS]
                rrow = srow[0:1, 1536:1536 + MS]
                p.emit("vector", (lambda mr=mrow: lambda eng: eng.tensor_single_scalar(
                    mr, pb[0][0:1, 0:MS], 1.0 / D, OP.mult))(),
                    waits=[("smm", ev_sqm_used[m])], tag=f"mrowm{m}")
                p.emit("vector", (lambda ar=arow: lambda eng: eng.tensor_scalar(
                    ar, pb[1][0:1, 0:MS], 1.0 / D, EPS, OP.mult, OP.add))(), tag=f"mrowa{m}")
                p.emit("vector", (lambda mr=mrow, tr=trow: lambda eng: eng.tensor_tensor(
                    tr, mr, mr, OP.mult))(), tag=f"mrowt{m}")
                ev_rows2[m] = p.emit("vector", (lambda ar=arow, tr=trow: lambda eng: eng.tensor_tensor(
                    ar, ar, tr, OP.subtract))(), inc=("sdv", 1), tag=f"mrowb{m}")
                p.emit("scalar", (lambda ar=arow, tr=trow: lambda eng: eng.activation(
                    tr, ar, AF.Ln, bias=0.0, scale=1.0))(),
                    waits=[("sdv", ev_rows2[m])], tag=f"mrowln{m}")
                rr2 = p.emit("scalar", (lambda rr=rrow, tr=trow: lambda eng: eng.activation(
                    rr, tr, AF.Exp, bias=0.0, scale=-0.5))(), inc=("sac", 1), tag=f"mrowr{m}")

                waits = [("sac", rr2)]
                if m >= 1:
                    waits.append(("sdv", ev_ln2[m - 1]))
                p.emit("tensor", (lambda mr=mrow: lambda eng: eng.matmul(
                    pb[2][:, 0:MS], onesk1t[:], mr, start=True, stop=True))(),
                    waits=waits, tag=f"mbcm{m}")
                bcs = p.emit("tensor", (lambda rr=rrow: lambda eng: eng.matmul(
                    pb[3][:, 0:MS], onesk1t[:], rr, start=True, stop=True))(),
                    inc=("smm", 1), tag=f"mbcr{m}")

                for dc in range(DC):
                    def mka(dc=dc, x_=xsl):
                        def fn(eng):
                            sl = slice(dc * MS, (dc + 1) * MS)
                            return eng.tensor_tensor(xsqm[:, sl], x_[:, sl], pb[2][:, 0:MS], OP.subtract)
                        return fn
                    p.emit("vector", mka(), waits=[("smm", bcs)] if dc == 0 else [], tag=f"l2a{m}_{dc}")
                for dc in range(DC):
                    def mkb(dc=dc, m=m):
                        def fn(eng):
                            return eng.tensor_tensor(
                                ln2b[:, dc * TOK + m * MSUB:dc * TOK + (m + 1) * MSUB],
                                xsqm[:, dc * MS:(dc + 1) * MS], pb[3][:, 0:MS], OP.mult)
                        return fn
                    inc = ("sdv", 1) if dc == DC - 1 else None
                    p.emit("vector", mkb(), inc=inc, tag=f"l2b{m}_{dc}")
                ev_ln2[m] = p.val("sdv")
                if m < NMT - 1:
                    emit_cw(m)


            ln2_all = p.val("sdv")
            x1_spilled = p.val("ssp")

        with contextlib.ExitStack() as fctx:
            oneht = fctx.enter_context(nc.sbuf_tensor("oneht", [15, 1920], BF16))
            eb2t = fctx.enter_context(nc.sbuf_tensor("eb2t", [15, 1024], BF16))
            acc = fctx.enter_context(nc.sbuf_tensor("acc", [128, DC * TOK], F32))
            h_sb = fctx.enter_context(nc.sbuf_tensor("h_sb", [128, 2 * 4096], BF16))
            rtmp = fctx.enter_context(nc.sbuf_tensor("rtmp", [128, 2 * 512], F32))

            ohl = p.emit("sync", lambda eng: eng.dma_start(out=oneht[:], in_=onehd),
                         waits=[("sdv", ln2_all), ("ssp", x1_spilled)], inc=("sx", 16), tag="ohl")
            eb2l = None


            if DEBUG:
                gv = p.emit("gpsimd", lambda eng: eng.dma_start(out=dbg["dbg_cw"], in_=cwfb[:]),
                            waits=[("sdv", cwstate["cwout"])], inc=("sw", 16), tag="dbgcw")
                p.set_gate(("sw", gv))
                lv = (lambda eng: eng.dma_start(
                    out=cpview(dbg["dbg_ln2"]).transpose([1, 0, 2]),
                    in_=ln2b[:].rearrange("p (c n) -> p c n", c=DC)))
                gv = p.emit("gpsimd", lv, waits=[("sdv", ln2_all)], inc=("sw", 16), tag="dbgln2")
                p.set_gate(("sw", gv))

            with contextlib.ExitStack() as wctx:
                wbuf = [wctx.enter_context(nc.sbuf_tensor(f"wbuf{i}", [128, 2 * 8192], BF16))
                        for i in range(2)]

                def wload(eng_unused, ee, buf):
                    if ee == 0:
                        w1src, w2src = sW1d, sW2d
                    else:
                        w1src = eW1d[ee - 1]
                        w2src = eW2d[ee - 1]
                    v1 = w1src.rearrange("(c p) n -> c p n", p=128).transpose([1, 0, 2])
                    v2 = w2src.rearrange("(c p) n -> c p n", p=128).transpose([1, 0, 2])
                    d1 = wbuf[buf][:, 0:8192].rearrange("p (c n) -> p c n", c=DC)
                    d2 = wbuf[buf][:, 8192:16384].rearrange("p (c n) -> p c n", c=DC)
                    return v1, d1, v2, d2

                ev_wload = [0] * NEXP
                ev_ee_mm_end = [0] * NEXP

                for ee in range(2):
                    v1, d1, v2, d2 = wload(None, ee, ee % 2)
                    w = [("ssp", ev_spill_h[NTILE - 1]), ("sw", ev_spill_a[NTILE - 1])] if ee == 0 else []
                    p.emit("gpsimd", (lambda a=v1, b=d1: lambda eng: eng.dma_start(out=b, in_=a))(),
                           waits=w, inc=("sw", 16), tag=f"w1l{ee}")
                    ev_wload[ee] = p.emit(
                        "gpsimd", (lambda a=v2, b=d2: lambda eng: eng.dma_start(out=b, in_=a))(),
                        inc=("sw", 16), tag=f"w2l{ee}")
                eb2l = p.emit("gpsimd", lambda eng: eng.dma_start(out=eb2t[:], in_=eb2d),
                              waits=[("sdv", ln2_all), ("ssp", x1_spilled)], inc=("sw", 16), tag="eb2l")

                ev_accj = [0] * NJ
                hfree = [0, 0]
                yfree = [0, 0]
                cwbfree = [0, 0]
                rtfree = [0, 0]

                for ee in range(NEXP):
                    wpar = ee % 2
                    hm_done = {}

                    def emit_cwb(j, ee=None):
                        cb = 4 + (j % 2)
                        w = [("sdv", max(cwstate["cwout"], cwbfree[j % 2])), ("sx", ohl)]
                        cwb = p.emit("tensor", (lambda ee=ee, j=j, cb=cb: lambda eng: eng.matmul(
                            pb[cb][:], oneht[:, (ee - 1) * 128:ee * 128],
                            cwfb[0:15, j * NSUB:(j + 1) * NSUB], start=True, stop=True))(),
                            waits=w, inc=("smm", 1), tag=f"cwb{ee}_{j}")
                        return cwb

                    def emit_w1(j, cwb, ee=None):
                        hpar = j % 2
                        cb = 4 + (j % 2)
                        hmul_events = []
                        for hc in range(HCN):
                            bk = hc % 2
                            w = []
                            if ee == 0 and j == 0 and hc == 0:
                                w.append(("sdv", ln2_all))
                                w.append(("ssp", x1_spilled))
                            w.append(("sw", ev_wload[ee]))
                            if hfree[bk]:
                                w.append(("sac", hfree[bk]))
                            for dc in range(DC):
                                def mkw1(ee=ee, j=j, hc=hc, dc=dc, bk=bk, wpar=wpar):
                                    def fn(eng):
                                        return eng.matmul(
                                            pb[bk][:],
                                            wbuf[wpar][:, dc * 1024 + hc * 128:dc * 1024 + (hc + 1) * 128],
                                            ln2b[:, dc * TOK + j * NSUB:dc * TOK + (j + 1) * NSUB],
                                            start=(dc == 0), stop=(dc == DC - 1))
                                    return fn
                                inc = ("smm", 1) if dc == DC - 1 else None
                                p.emit("tensor", mkw1(), waits=w if dc == 0 else [], inc=inc,
                                       tag=f"W1_{ee}_{j}_{hc}_{dc}")
                                w = []
                            w1g = p.val("smm")
                            if ee == 0:
                                hfree[bk] = p.emit("scalar", (lambda hc=hc, bk=bk, hpar=hpar: lambda eng:
                                                   eng.activation(
                                                       h_sb[:, hpar * 4096 + hc * 512:hpar * 4096 + (hc + 1) * 512],
                                                       pb[bk][:], AF.Relu, bias=sbvt[:, hc:hc + 1], scale=1.0))(),
                                                   waits=[("smm", w1g)], inc=("sac", 1), tag=f"relu{ee}_{j}_{hc}")
                                hmul_events.append(("sac", hfree[bk]))
                            else:
                                rslot = hc % 2
                                rw = [("smm", w1g)]
                                if rtfree[rslot]:
                                    rw.append(("sdv", rtfree[rslot]))
                                rel = p.emit("scalar", (lambda ee=ee, hc=hc, bk=bk, rslot=rslot: lambda eng:
                                             eng.activation(
                                                 rtmp[:, rslot * 512:(rslot + 1) * 512], pb[bk][:], AF.Relu,
                                                 bias=eb1t[:, (ee - 1) * 8 + hc:(ee - 1) * 8 + hc + 1],
                                                 scale=1.0))(),
                                             waits=rw, inc=("sac", 1), tag=f"relu{ee}_{j}_{hc}")
                                hfree[bk] = rel
                                hm_ = p.emit("vector", (lambda hc=hc, hpar=hpar, cb=cb, rslot=rslot: lambda eng:
                                             eng.tensor_tensor(
                                                 h_sb[:, hpar * 4096 + hc * 512:hpar * 4096 + (hc + 1) * 512],
                                                 rtmp[:, rslot * 512:(rslot + 1) * 512], pb[cb][:], OP.mult))(),
                                             waits=[("sac", rel), ("smm", cwb)], inc=("sdv", 1),
                                             tag=f"hmul{ee}_{j}_{hc}")
                                rtfree[rslot] = hm_
                                hmul_events.append(("sdv", hm_))
                        if ee >= 1:
                            cwbfree[j % 2] = hmul_events[-1][1]
                        hm_done[j] = hmul_events[-1]

                    def emit_w2(j, ee=None):
                        hpar = j % 2
                        for dc in range(DC):
                            bk = 2 + dc % 2
                            w = []
                            if dc == 0:
                                w.append(hm_done[j])
                            if yfree[dc % 2]:
                                w.append(("sdv", yfree[dc % 2]))
                            for hc in range(HCN):
                                def mkw2(ee=ee, j=j, hc=hc, dc=dc, bk=bk, wpar=wpar, hpar=hpar):
                                    def fn(eng):
                                        return eng.matmul(
                                            pb[bk][:],
                                            wbuf[wpar][:, 8192 + hc * 1024 + dc * 128:8192 + hc * 1024 + (dc + 1) * 128],
                                            h_sb[:, hpar * 4096 + hc * 512:hpar * 4096 + (hc + 1) * 512],
                                            start=(hc == 0), stop=(hc == HCN - 1 and ee != 1))
                                    return fn
                                inc = ("smm", 1) if (hc == HCN - 1 and ee != 1) else None
                                p.emit("tensor", mkw2(), waits=w if hc == 0 else [],
                                       inc=inc, tag=f"W2_{ee}_{j}_{dc}_{hc}")
                                w = []
                            if ee == 1:
                                def mkeb(j=j, dc=dc, bk=bk):
                                    def fn(eng):
                                        return eng.matmul(
                                            pb[bk][:], eb2t[:, dc * 128:(dc + 1) * 128],
                                            cwfb[0:15, j * NSUB:(j + 1) * NSUB],
                                            start=False, stop=True)
                                    return fn
                                p.emit("tensor", mkeb(), waits=[("sw", eb2l)],
                                       inc=("smm", 1), tag=f"eb2_{j}_{dc}")
                            w2g = p.val("smm")
                            if ee == 0:
                                yfree[dc % 2] = p.emit("vector", (lambda j=j, dc=dc, bk=bk: lambda eng:
                                                       eng.tensor_copy(
                                                           acc[:, dc * TOK + j * NSUB:dc * TOK + (j + 1) * NSUB],
                                                           pb[bk][:]))(),
                                                       waits=[("smm", w2g)], inc=("sdv", 1),
                                                       tag=f"acc{ee}_{j}_{dc}")
                            else:
                                yfree[dc % 2] = p.emit("vector", (lambda j=j, dc=dc, bk=bk: lambda eng:
                                                       eng.tensor_tensor(
                                                           acc[:, dc * TOK + j * NSUB:dc * TOK + (j + 1) * NSUB],
                                                           acc[:, dc * TOK + j * NSUB:dc * TOK + (j + 1) * NSUB],
                                                           pb[bk][:], OP.add))(),
                                                       waits=[("smm", w2g)], inc=("sdv", 1),
                                                       tag=f"acc{ee}_{j}_{dc}")

                    cwbs = {}
                    if ee >= 1:
                        cwbs[0] = emit_cwb(0, ee=ee)
                    emit_w1(0, cwbs.get(0), ee=ee)
                    if ee == 0:
                        emit_cw(NMT - 1)
                    if ee >= 1:
                        cwbs[1] = emit_cwb(1, ee=ee)
                    emit_w1(1, cwbs.get(1), ee=ee)
                    emit_w2(0, ee=ee)
                    if ee == NEXP - 1:
                        ev_accj[0] = p.val("sdv")
                    if ee >= 1:
                        cwbs[2] = emit_cwb(2, ee=ee)
                    emit_w1(2, cwbs.get(2), ee=ee)
                    emit_w2(1, ee=ee)
                    if ee == NEXP - 1:
                        ev_accj[1] = p.val("sdv")
                    if ee >= 1:
                        cwbs[3] = emit_cwb(3, ee=ee)
                    emit_w1(3, cwbs.get(3), ee=ee)
                    emit_w2(2, ee=ee)
                    if ee == NEXP - 1:
                        ev_accj[2] = p.val("sdv")
                    emit_w2(3, ee=ee)
                    if ee == NEXP - 1:
                        ev_accj[3] = p.val("sdv")

                    ev_ee_mm_end[ee] = p.val("smm")
                    if ee + 2 < NEXP:
                        v1, d1, v2, d2 = wload(None, ee + 2, wpar)
                        p.emit("gpsimd", (lambda a=v1, b=d1: lambda eng:
                               eng.dma_start(out=b, in_=a))(),
                               waits=[("smm", ev_ee_mm_end[ee])], inc=("sw", 16), tag=f"w1l{ee + 2}")
                        ev_wload[ee + 2] = p.emit(
                            "gpsimd", (lambda a=v2, b=d2: lambda eng: eng.dma_start(out=b, in_=a))(),
                            inc=("sw", 16), tag=f"w2l{ee + 2}")

                acc_done = p.val("sdv")
                if DEBUG:
                    p.emit("gpsimd", lambda eng: eng.dma_start(
                        out=cpview(dbg["dbg_acc"]).transpose([1, 0, 2]),
                        in_=acc[:].rearrange("p (c n) -> p c n", c=DC)),
                        waits=[("sdv", acc_done)], inc=("sw", 16), tag="dbgacc")

            with contextlib.ExitStack() as ectx:
                xr = ectx.enter_context(nc.sbuf_tensor("xr", [128, 2 * 4096], F32))
                ev_out = [0] * NJ
                ev_fin = [0] * NJ
                ev_finh = [0] * NJ

                def emit_xrl(f):
                    par = f % 2
                    w = [("ssp", x1_spilled), ("smm", ev_ee_mm_end[NEXP - 2])]
                    if f >= 2:
                        w.append(("ssp", ev_out[f - 2]))
                    return p.emit("sync", (lambda f=f, par=par: lambda eng: eng.dma_start(
                        out=xr[:, par * 4096:(par + 1) * 4096].rearrange("p (c n) -> p c n", c=DC),
                        in_=(x1v if f == 0 else hspv)[:, :, f * NSUB:(f + 1) * NSUB]
                        .transpose([1, 0, 2])))(),
                        waits=w, inc=("sx", 16), tag=f"xrl{f}")

                xlds = [emit_xrl(0), emit_xrl(1)]
                for f in range(NJ):
                    par = f % 2
                    w = [("sx", xlds[f]), ("sdv", ev_accj[f])]
                    for dc in range(DC):
                        def mkf(f=f, dc=dc, par=par):
                            def fn(eng):
                                sl = xr[:, par * 4096 + dc * NSUB:par * 4096 + (dc + 1) * NSUB]
                                return eng.scalar_tensor_tensor(
                                    sl,
                                    acc[:, dc * TOK + f * NSUB:dc * TOK + (f + 1) * NSUB],
                                    sbvt[:, 8 + dc:9 + dc], sl,
                                    OP.add, OP.add)
                            return fn
                        inc = ("sdv", 1) if dc in (DC // 2 - 1, DC - 1) else None
                        p.emit("vector", mkf(), waits=w if dc == 0 else [], inc=inc, tag=f"fin{f}_{dc}")
                        if dc == DC // 2 - 1:
                            ev_finh[f] = p.val("sdv")
                    ev_fin[f] = p.val("sdv")
                    p.emit("sync", (lambda f=f, par=par: lambda eng: eng.dma_start(
                        out=outv[0:DC // 2, :, f * NSUB:(f + 1) * NSUB].transpose([1, 0, 2]),
                        in_=xr[:, par * 4096:par * 4096 + 4 * NSUB]
                        .rearrange("p (c n) -> p c n", c=DC // 2)))(),
                        waits=[("sdv", ev_finh[f])], inc=("ssp", 16), tag=f"outsl{f}")
                    p.emit("gpsimd", (lambda f=f, par=par: lambda eng: eng.dma_start(
                        out=outv[DC // 2:DC, :, f * NSUB:(f + 1) * NSUB].transpose([1, 0, 2]),
                        in_=xr[:, par * 4096 + 4 * NSUB:par * 4096 + 8 * NSUB]
                        .rearrange("p (c n) -> p c n", c=DC // 2)))(),
                        waits=[("sdv", ev_fin[f])], inc=("ssp", 16), tag=f"outsh{f}")
                    ev_out[f] = p.val("ssp")
                    if f + 2 < NJ:
                        xlds.append(emit_xrl(f + 2))

                p.emit("sync", lambda eng: None, waits=[("ssp", p.val("ssp")), ("sw", p.val("sw"))],
                       tag="final_wait")

        p.check_deadlock()
        with nc.Block() as block:
            p.build(block)
    return nc


def kernel(L0_attn_x_t, L0_attn_w_i, L0_attn_w_f, L0_attn_w_h,
           L0_ffn_router_W, L0_ffn_router_b, L0_ffn_sW1, L0_ffn_sb1,
           L0_ffn_sW2, L0_ffn_sb2, L0_ffn_eW1, L0_ffn_eb1, L0_ffn_eW2, L0_ffn_eb2):
    x = np.asarray(L0_attn_x_t, dtype=np.float32)

    def fm(v):
        return np.ascontiguousarray(np.asarray(v, np.float32).reshape(8, 128).T)

    wif = np.concatenate([fm(L0_attn_w_f), fm(L0_attn_w_i), fm(L0_attn_w_h)], axis=1)
    sbv = np.concatenate([fm(L0_ffn_sb1), fm(L0_ffn_sb2)], axis=1)
    eb1p = np.ascontiguousarray(
        np.asarray(L0_ffn_eb1, np.float32).reshape(15, 8, 128).transpose(2, 0, 1).reshape(128, 120))
    eb2p = np.ascontiguousarray(np.asarray(L0_ffn_eb2, np.float32))
    rwp = np.ascontiguousarray(
        np.asarray(L0_ffn_router_W, np.float32).reshape(8, 128, 15).transpose(1, 0, 2).reshape(128, 120))
    rbp = np.ascontiguousarray(np.asarray(L0_ffn_router_b, np.float32).reshape(15, 1))
    sW1 = np.ascontiguousarray(np.asarray(L0_ffn_sW1, np.float32))
    sW2 = np.ascontiguousarray(np.asarray(L0_ffn_sW2, np.float32))
    eW1 = np.ascontiguousarray(np.asarray(L0_ffn_eW1, np.float32))
    eW2 = np.ascontiguousarray(np.asarray(L0_ffn_eW2, np.float32))

    nc = build_nc()
    in_maps = []
    for c in range(8):
        b, pp = c // 4, c % 4
        xcn = np.ascontiguousarray(x[b, pp * TOK:(pp + 1) * TOK, :].T)
        cm = np.zeros((128, 8), np.float32)
        for j in range(8):
            if j // 4 == b and j % 4 < pp:
                cm[:, j] = 1.0
        in_maps.append({
            "xc": xcn, "cmask": cm, "cmask1m": np.ascontiguousarray(1.0 - cm),
            "cm64": np.ascontiguousarray(np.repeat(cm, 8, axis=1)),
            "wif": wif, "sbv": sbv, "eb1p": eb1p, "eb2p": eb2p, "rwp": rwp, "rbp": rbp,
            "sW1": sW1, "sW2": sW2, "eW1": eW1, "eW2": eW2,
        })

    res = run_bass_kernel_spmd(nc, in_maps, core_ids=list(range(8)))
    out = np.empty((B, S, D), np.float32)
    for c in range(8):
        b, pp = c // 4, c % 4
        out[b, pp * TOK:(pp + 1) * TOK, :] = res.results[c]["out_fm"].T
    if DEBUG:
        kernel.last_results = res
    return out
